# version 02
# baseline (speedup 1.0000x reference)
import numpy as np
import contextlib
import concourse.bass as bass
import concourse.mybir as mybir
from concourse.alu_op_type import AluOpType as OP
from concourse.bass_utils import run_bass_kernel_spmd

F32 = mybir.dt.float32
BF16 = mybir.dt.bfloat16
U8 = mybir.dt.uint8
I32 = mybir.dt.int32
AF = mybir.ActivationFunctionType
AX = mybir.AxisListType

D = 1024
DC = 8
H = 16
DH = 64
B = 4
T = 4096
TL = 2048
NCORE = 8


PROG_SEM = {"vector": "V", "scalar": "S", "tensor": "Tq", "gpsimd": "G", "sync": "SYq"}
DMA_OPS = {"dma_start", "dma_start_transpose", "indirect_dma_start"}
REG_OPS = {"reg_load", "reg_mov", "reg_add", "reg_sub", "reg_mul", "reg_save", "reg_alu"}


class Eng:

    def __init__(self, prog, name):
        self._p = prog
        self._n = name
        self._ps = PROG_SEM[name]

    def wait(self, sem, val):
        if val <= 0:
            return
        h = self._p.sem(sem)
        self._p.steps[self._n].append(lambda q: q.wait_ge(h, val))

    def wait_now(self, sem):
        self.wait(sem, self._p.now(sem))

    def fence(self):
        self.wait(self._ps, self._p.now(self._ps))

    def __getattr__(self, attr):
        if attr.startswith("_"):
            raise AttributeError(attr)
        p = self._p

        def call(*args, inc=None, k=1, **kw):
            if attr in DMA_OPS:
                assert inc is not None, f"DMA op {attr} needs inc="
                semh = p.sem(inc)
                p.inc(inc, 16)

                def f(q):
                    getattr(q, attr)(*args, **kw).then_inc(semh, 16)
            elif attr in REG_OPS:
                def f(q):
                    getattr(q, attr)(*args, **kw)
            else:
                assert inc is None
                semh = p.sem(self._ps)
                p.inc(self._ps, 1)

                def f(q):
                    getattr(q, attr)(*args, **kw).then_inc(semh, 1)
            p.steps[self._n].append(f)
        return call


class Prog:
    def __init__(self):
        self.nc = bass.Bass("TRN2", target_bir_lowering=False)
        self.es = contextlib.ExitStack()
        self.steps = {e: [] for e in ("sync", "vector", "scalar", "tensor", "gpsimd")}
        self.cnt = {}
        self.sems = {}
        self._cms = {}
        self.T = Eng(self, "tensor")
        self.V = Eng(self, "vector")
        self.S = Eng(self, "scalar")
        self.G = Eng(self, "gpsimd")
        self.SY = Eng(self, "sync")

    def sem(self, name):
        if name not in self.sems:
            self.sems[name] = self.es.enter_context(self.nc.semaphore(name))
            self.cnt[name] = 0
        return self.sems[name]

    def inc(self, name, kk=1):
        self.sem(name)
        self.cnt[name] += kk
        return self.cnt[name]

    def now(self, name):
        self.sem(name)
        return self.cnt.get(name, 0)

    tag = ""
    phase_mode = False

    def sb(self, name, shape, dt=F32, freeable=False):
        name = name + self.tag
        cm = self.nc.sbuf_tensor(name, shape, dt)
        if freeable or self.phase_mode:
            h = cm.__enter__()
            self._cms[name] = cm
            return h
        return self.es.enter_context(cm)

    def free(self, name):
        name2 = name + self.tag
        if name2 not in self._cms:
            name2 = name
        self._cms.pop(name2).__exit__(None, None, None)

    def free_ps(self, name):
        self.free(name)

    def ps(self, name, shape, dt=F32, freeable=False):
        name = name + self.tag
        cm = self.nc.psum_tensor(name, shape, dt)
        if freeable or self.phase_mode:
            h = cm.__enter__()
            self._cms[name] = cm
            return h
        return self.es.enter_context(cm)

    import contextlib as _ctx

    @_ctx.contextmanager
    def phase(self, tag):
        before = list(self._cms)
        old_tag, old_mode = self.tag, self.phase_mode
        self.tag, self.phase_mode = tag, True
        try:
            yield
        finally:
            self.tag, self.phase_mode = old_tag, old_mode
            emit_barrier(self, all_sems=True)
            for name in reversed([n for n in self._cms if n not in before]):
                self._cms.pop(name).__exit__(None, None, None)

    def din(self, name, shape, dt=F32):
        return self.nc.dram_tensor(name, shape, dt, kind="ExternalInput").ap()

    def dout(self, name, shape, dt=F32):
        return self.nc.dram_tensor(name, shape, dt, kind="ExternalOutput").ap()

    def finalize(self):
        for name in reversed(list(self._cms)):
            self.free(name)
        with self.es:
            with self.nc.Block() as blk:
                for eng, fns in self.steps.items():
                    if fns:
                        getattr(blk, eng)(lambda q, fns=fns: [f(q) for f in fns])
        return self.nc


def ceil_div(a, b):
    return (a + b - 1) // b


def emit_barrier(P, dma_sems=(), all_sems=False):
    pairs = [(P.V, "V"), (P.S, "S"), (P.T, "Tq"), (P.G, "G"), (P.SY, "SYq")]
    counts = {s: P.now(s) for _, s in pairs}
    if all_sems:
        for s in list(P.cnt):
            counts[s] = P.now(s)
    for ds in dma_sems:
        counts[ds] = P.now(ds)
    for em, s in pairs:
        for s2, c in counts.items():
            if s2 != s:
                em.wait(s2, c)


class MMPool:

    def __init__(self, P, n=4, name="mmp", shape=(128, 512)):
        self.P = P
        self.tiles = [P.ps(f"{name}{i}", list(shape), F32) for i in range(n)]
        self.n = n
        self.group = 0
        self.marks = []

    def begin(self):
        i = self.group
        t = self.tiles[i % self.n]
        if i >= self.n:
            sem, cntv = self.marks[i - self.n]
            self.P.T.wait(sem, cntv)
        self.group += 1
        return t

    def mark(self, sem):
        self.marks.append((sem, self.P.now(sem)))



LN_SCRATCH_NAMES = ["sq16", "ones16", "ones1f", "c1", "c0", "epsv", "c1b", "c0b", "tmp16"]


def free_ln_scratch(P, tag=""):
    P.free_ps("lns_pbc1" + tag); P.free_ps("lns_pbc0" + tag)
    P.free_ps("lns_pred1" + tag); P.free_ps("lns_pred0" + tag)
    names = ["lns_sq0", "lns_ones16", "lns_ones1f", "lns_c1", "lns_c0", "lns_eps",
             "lns_c1b", "lns_c0b", "lns_tmp16"]
    for n in reversed(names):
        P.free(n + tag)


def make_ln_scratch(P, nW_max, tag="", freeable=False):
    return {
        "nW_max": nW_max,
        "sq16": P.sb("lns_sq0" + tag, [128, 512], BF16, freeable=freeable),
        "ones16": P.sb("lns_ones16" + tag, [128, 1], BF16, freeable=freeable),
        "ones1f": P.sb("lns_ones1f" + tag, [1, 128], F32, freeable=freeable),
        "c1": P.sb("lns_c1" + tag, [1, nW_max], F32, freeable=freeable),
        "c0": P.sb("lns_c0" + tag, [1, nW_max], F32, freeable=freeable),
        "epsv": P.sb("lns_eps" + tag, [1, 1], F32, freeable=freeable),
        "c1b": P.sb("lns_c1b" + tag, [128, 512], BF16, freeable=freeable),
        "c0b": P.sb("lns_c0b" + tag, [128, 512], BF16, freeable=freeable),
        "tmp16": P.sb("lns_tmp16" + tag, [128, 512], BF16, freeable=freeable),
        "p_red": [P.ps("lns_pred0" + tag, [1, 512], F32, freeable=freeable),
                  P.ps("lns_pred1" + tag, [1, 512], F32, freeable=freeable)],
        "p_bc": [P.ps("lns_pbc0" + tag, [128, 512], F32, freeable=freeable),
                 P.ps("lns_pbc1" + tag, [128, 512], F32, freeable=freeable)],
    }


def emit_ln_init(P, sc):
    P.V.memset(sc["ones16"][:, :], 1.0)
    P.V.memset(sc["epsv"][:, :], 1e-5)
    P.V.memset(sc["ones1f"][:, :], 1.0)


def emit_ln_reduce_chunk(P, sc, x16ch, tc, w, wait=None):
    V, S, Tn = P.V, P.S, P.T
    if wait is not None:
        S.wait(*wait)
        Tn.wait(*wait)
    Tn.wait("V", P.now("V"))
    for kc in range(DC):
        S.wait("Tq", P.now("Tq"))
        S.activation(sc["sq16"][:, 0:w], x16ch[:, kc, 0:w], AF.Square, bias=0.0, scale=1.0)
        Tn.wait("S", P.now("S"))
        Tn.matmul(sc["p_red"][0][0:1, 0:w], sc["ones16"][:, :], x16ch[:, kc, 0:w],
                  start=(kc == 0), stop=(kc == DC - 1))
        Tn.matmul(sc["p_red"][1][0:1, 0:w], sc["ones16"][:, :], sc["sq16"][:, 0:w],
                  start=(kc == 0), stop=(kc == DC - 1))
    V.wait("Tq", P.now("Tq"))
    V.tensor_copy(sc["c1"][0:1, tc * 512:tc * 512 + w], sc["p_red"][0][0:1, 0:w])
    V.tensor_copy(sc["c0"][0:1, tc * 512:tc * 512 + w], sc["p_red"][1][0:1, 0:w])


def emit_ln_stats(P, sc, nW):
    V, S = P.V, P.S
    c1, c0 = sc["c1"], sc["c0"]
    V.tensor_scalar(c1[0:1, 0:nW], c1[0:1, 0:nW], 1.0 / D, None, op0=OP.mult)
    V.tensor_scalar(c0[0:1, 0:nW], c0[0:1, 0:nW], 1.0 / D, None, op0=OP.mult)
    for tc in range(ceil_div(nW, 512)):
        w = min(512, nW - tc * 512)
        sl = (slice(0, 1), slice(tc * 512, tc * 512 + w))
        V.tensor_tensor(sc["p_red"][0][0:1, 0:w], c1[sl], c1[sl], op=OP.mult)
        V.tensor_tensor(c0[sl], c0[sl], sc["p_red"][0][0:1, 0:w], op=OP.subtract)
    S.wait_now("V")
    S.activation(c0[0:1, 0:nW], c0[0:1, 0:nW], AF.Sqrt, bias=sc["epsv"][0:1, 0:1], scale=1.0)
    V.wait_now("S")
    V.reciprocal(c0[0:1, 0:nW], c0[0:1, 0:nW])
    V.tensor_tensor(c1[0:1, 0:nW], c1[0:1, 0:nW], c0[0:1, 0:nW], op=OP.mult)
    V.tensor_scalar(c1[0:1, 0:nW], c1[0:1, 0:nW], -1.0, None, op0=OP.mult)


def emit_ln_bcast_chunk(P, sc, tc, w):
    V, Tn = P.V, P.T
    Tn.wait("V", P.now("V"))
    Tn.matmul(sc["p_bc"][0][:, 0:w], sc["ones1f"][0:1, :], sc["c0"][0:1, tc * 512:tc * 512 + w],
              start=True, stop=True)
    Tn.matmul(sc["p_bc"][1][:, 0:w], sc["ones1f"][0:1, :], sc["c1"][0:1, tc * 512:tc * 512 + w],
              start=True, stop=True)
    V.wait("Tq", P.now("Tq"))
    V.tensor_copy(sc["c1b"][:, 0:w], sc["p_bc"][0][:, 0:w])
    V.tensor_copy(sc["c0b"][:, 0:w], sc["p_bc"][1][:, 0:w])


def emit_ln_apply_chunk(P, sc, x16ch, out16ch, w):
    V = P.V
    for kc in range(DC):
        V.tensor_tensor(sc["tmp16"][:, 0:w], x16ch[:, kc, 0:w], sc["c1b"][:, 0:w], op=OP.mult)
        V.tensor_tensor(out16ch[:, kc, 0:w], sc["tmp16"][:, 0:w], sc["c0b"][:, 0:w], op=OP.add)



def emit_l0a(P, A):
    xT = A["xT"]
    w_uv = A["w_uv"]
    b_uv = A["b_uv"]
    conv_w3 = A["conv_w3"]
    conv_b4 = A["conv_b4"]
    logit_a4 = A["logit_a4"]
    ssm_b4 = A["ssm_b4"]
    ssm_c4 = A["ssm_c4"]
    ssm_d4 = A["ssm_d4"]
    g_out = A["g_out"]

    V, S, Tn, G, SY = P.V, P.S, P.T, P.G, P.SY

    xln16 = P.sb("xln16", [128, DC, T], BF16)
    wuv16 = P.sb("wuv16", [128, DC, 1024], BF16)
    vb_uv = P.sb("vb_uv", [128, 8])
    vconv_w = P.sb("vconv_w", [128, 4, 3])
    vconv_b = P.sb("vconv_b", [128, 4])
    vlogit = P.sb("vlogit", [128, 4])
    vsb = P.sb("vsb", [128, 4])
    vsc_ = P.sb("vsc", [128, 4])
    vsd = P.sb("vsd", [128, 4])
    va = P.sb("va", [128, 4])
    vlna = P.sb("vlna", [128, 4])
    va128 = P.sb("va128", [128, 4])
    s128 = P.sb("s128", [128, 1])
    ones128 = P.sb("ones128", [128, 128])
    iotaf = P.sb("iotaf", [128, 128])
    iotai = P.sb("iotai", [128, 128], I32)
    apw = P.sb("apw", [128, 128])
    Acg = P.sb("Acg", [128, 128])
    Acg128 = P.sb("Acg128", [128, 32])
    hend = P.sb("hend", [128, 32])
    scanC = P.sb("scanC", [128, 32])
    sc = make_ln_scratch(P, T, freeable=True)
    xbuf = [P.sb("xbuf0", [128, DC, 512], F32, freeable=True),
            P.sb("xbuf1", [128, DC, 512], F32, freeable=True)]
    x16ch = [P.sb("x16ch0", [128, DC, 512], BF16, freeable=True),
             P.sb("x16ch1", [128, DC, 512], BF16, freeable=True)]

    for kc in range(DC):
        G.dma_start(wuv16[:, kc, :], w_uv[kc * 128:(kc + 1) * 128, :], inc="ld", k=16)
    for dst, src_ in [(vb_uv, b_uv), (vconv_w, conv_w3), (vconv_b, conv_b4), (vlogit, logit_a4),
                      (vsb, ssm_b4), (vsc_, ssm_c4), (vsd, ssm_d4)]:
        SY.dma_start(dst[:, :] if dst is not vconv_w else dst[:, :, :], src_, inc="ld2", k=16)
    nld2 = P.now("ld2")

    emit_ln_init(P, sc)

    cast_done = []

    def stream_pass(apply):
        for tc in range(8):
            par = tc % 2
            if len(cast_done) >= 2:
                SY.wait("V", cast_done[-2])
            SY.dma_start(xbuf[par][:, :, :],
                         xT[:, tc * 512:(tc + 1) * 512].rearrange("(c p) w -> p c w", p=128),
                         inc=f"ldx{par}", k=16)
            V.wait(f"ldx{par}", P.now(f"ldx{par}"))
            for kc in range(DC):
                V.tensor_copy(x16ch[par][:, kc, :], xbuf[par][:, kc, :])
            cast_done.append(P.now("V"))
            if apply:
                emit_ln_bcast_chunk(P, sc, tc, 512)
                emit_ln_apply_chunk(P, sc, x16ch[par], xln16[:, :, tc * 512:(tc + 1) * 512], 512)
            else:
                emit_ln_reduce_chunk(P, sc, x16ch[par], tc, 512, wait=("V", P.now("V")))

    stream_pass(False)
    emit_ln_stats(P, sc, T)
    stream_pass(True)
    emit_barrier(P)
    P.free("x16ch1"); P.free("x16ch0"); P.free("xbuf1"); P.free("xbuf0")
    free_ln_scratch(P)

    u16 = P.sb("u16", [128, T], BF16)
    vA = P.sb("vA", [128, T + 2])
    vB = P.sb("vB", [128, T])
    vC = P.sb("vC", [128, T])
    Hc = P.sb("Hc", [128, T])
    g16 = P.sb("g16", [128, T], BF16)
    ys16 = P.sb("ys16", [128, T], BF16)
    pool = MMPool(P, n=4)

    V.memset(ones128[:, :], 1.0)
    V.memset(s128[:, :], 128.0)
    G.wait("ld2", nld2)
    G.iota(iotai[:, :], [[1, 128]], base=1, channel_multiplier=0)
    G.tensor_copy(iotaf[:, :], iotai[:, :])
    S.wait("ld2", nld2)
    S.wait_now("V")
    S.activation(va[:, :], vlogit[:, :], AF.Sigmoid, bias=0.0, scale=1.0)
    S.activation(vlna[:, :], va[:, :], AF.Ln, bias=0.0, scale=1.0)
    S.activation(va128[:, :], vlna[:, :], AF.Exp, bias=0.0, scale=s128[:, 0:1])
    emit_barrier(P)

    Tn.wait("ld", 16 * DC)
    v_after_gate = 0
    for cg in range(4):
        first_evac = True
        for which, oc in (("u", cg), ("v", cg + 4)):
            for tc in range(8):
                p = pool.begin()
                for kc in range(DC):
                    Tn.matmul(p[:, :], wuv16[:, kc, oc * 128:(oc + 1) * 128],
                              xln16[:, kc, tc * 512:(tc + 1) * 512],
                              start=(kc == 0), stop=(kc == DC - 1))
                S.wait("Tq", P.now("Tq"))
                if first_evac and cg > 0:
                    S.wait("V", v_after_gate)
                first_evac = False
                if which == "u":
                    S.activation(u16[:, tc * 512:(tc + 1) * 512], p[:, :], AF.Identity,
                                 bias=vb_uv[:, oc:oc + 1], scale=1.0)
                else:
                    S.activation(vA[:, 1 + tc * 512:1 + (tc + 1) * 512], p[:, :], AF.Silu,
                                 bias=vb_uv[:, oc:oc + 1], scale=1.0)
                pool.mark("S")
        V.memset(vA[:, 0:1], 0.0)
        V.memset(vA[:, T + 1:T + 2], 0.0)
        V.wait("S", P.now("S"))
        V.tensor_scalar(vB[:, :], vA[:, 0:T], vconv_w[:, cg, 0:1], vconv_b[:, cg:cg + 1],
                        op0=OP.mult, op1=OP.add)
        V.scalar_tensor_tensor(vC[:, :], vA[:, 1:T + 1], vconv_w[:, cg, 1:2], vB[:, :],
                               op0=OP.mult, op1=OP.add)
        V.scalar_tensor_tensor(vB[:, :], vA[:, 2:T + 2], vconv_w[:, cg, 2:3], vC[:, :],
                               op0=OP.mult, op1=OP.add)
        V.tensor_scalar(vC[:, :], vB[:, :], vsb[:, cg:cg + 1], None, op0=OP.mult)
        V.tensor_scalar(Acg[:, :], ones128[:, :], va[:, cg:cg + 1], None, op0=OP.mult)
        for tch in range(32):
            V.tensor_tensor_scan(Hc[:, tch * 128:(tch + 1) * 128], Acg[:, :],
                                 vC[:, tch * 128:(tch + 1) * 128], 0.0,
                                 op0=OP.mult, op1=OP.add)
            V.fence()
        V.tensor_copy(hend[:, :], Hc[:, 127:T:128])
        V.tensor_scalar(Acg128[:, :], ones128[:, 0:32], va128[:, cg:cg + 1], None, op0=OP.mult)
        V.fence()
        V.tensor_tensor_scan(scanC[:, :], Acg128[:, :], hend[:, :], 0.0,
                             op0=OP.mult, op1=OP.add)
        V.fence()
        V.tensor_scalar(apw[:, :], iotaf[:, :], vlna[:, cg:cg + 1], None, op0=OP.mult)
        S.wait("V", P.now("V"))
        S.activation(apw[:, :], apw[:, :], AF.Exp, bias=0.0, scale=1.0)
        V.wait("S", P.now("S"))
        for tch in range(1, 32):
            V.scalar_tensor_tensor(Hc[:, tch * 128:(tch + 1) * 128], apw[:, :],
                                   scanC[:, tch - 1:tch], Hc[:, tch * 128:(tch + 1) * 128],
                                   op0=OP.mult, op1=OP.add)
        V.tensor_scalar(vA[:, 0:T], vB[:, :], vsd[:, cg:cg + 1], None, op0=OP.mult)
        V.scalar_tensor_tensor(vC[:, :], Hc[:, :], vsc_[:, cg:cg + 1], vA[:, 0:T],
                               op0=OP.mult, op1=OP.add)
        S.wait("V", P.now("V"))
        S.activation(ys16[:, :], vC[:, :], AF.Silu, bias=0.0, scale=1.0)
        V.wait("S", P.now("S"))
        V.wait("st", 16 * cg)
        V.tensor_tensor(g16[:, :], u16[:, :], ys16[:, :], op=OP.mult)
        v_after_gate = P.now("V")
        SY.wait("V", P.now("V"))
        SY.dma_start(g_out[cg * 128:(cg + 1) * 128, :], g16[:, :], inc="st", k=16)
    SY.wait("st", P.now("st"))



def build_l0a():
    P = Prog()
    A = {}
    A["xT"] = P.din("xT", [D, T])
    A["w_uv"] = P.din("w_uv", [D, 1024])
    A["b_uv"] = P.din("b_uv", [128, 8])
    A["conv_w3"] = P.din("conv_w3", [128, 4, 3])
    A["conv_b4"] = P.din("conv_b4", [128, 4])
    A["logit_a4"] = P.din("logit_a4", [128, 4])
    A["ssm_b4"] = P.din("ssm_b4", [128, 4])
    A["ssm_c4"] = P.din("ssm_c4", [128, 4])
    A["ssm_d4"] = P.din("ssm_d4", [128, 4])
    A["g_out"] = P.dout("g_out", [512, T], BF16)
    emit_l0a(P, A)
    return P

def prep_l0a_inputs(ins, xT_host):
    W = ins["L0_attn_in_proj_w"]
    bias = ins["L0_attn_in_proj_b"]
    cw = ins["L0_attn_conv_w"]

    def vecT(v, ncol):
        return np.ascontiguousarray(v.reshape(ncol, 128).T)

    maps = []
    for c in range(NCORE):
        b, j = c // 2, c % 2
        cols_u = slice(j * 512, (j + 1) * 512)
        cols_v = slice(1024 + j * 512, 1024 + (j + 1) * 512)
        w_uv = np.ascontiguousarray(np.concatenate([W[:, cols_u], W[:, cols_v]], axis=1))
        b_uv = vecT(np.concatenate([bias[cols_u], bias[cols_v]]), 8)
        ch = slice(j * 512, (j + 1) * 512)
        conv_w3 = np.ascontiguousarray(cw[ch, 0, :].reshape(4, 128, 3).transpose(1, 0, 2))
        maps.append(dict(
            xT=np.ascontiguousarray(xT_host[b]),
            w_uv=w_uv, b_uv=b_uv, conv_w3=conv_w3,
            conv_b4=vecT(ins["L0_attn_conv_b"][ch], 4),
            logit_a4=vecT(ins["L0_attn_logit_a"][ch], 4),
            ssm_b4=vecT(ins["L0_attn_ssm_b"][ch], 4),
            ssm_c4=vecT(ins["L0_attn_ssm_c"][ch], 4),
            ssm_d4=vecT(ins["L0_attn_ssm_d"][ch], 4),
        ))
    return maps



def emit_l0b1(P, A):
    xT_sl = A["xT_sl"]
    g16T = A["g16T"]
    opw = A["opw"]
    opb8 = A["opb8"]
    lnw8 = A["lnw8"]
    lnb8 = A["lnb8"]
    x1T = A["x1T"]
    x1lnT = A["x1lnT"]

    V, S, Tn, G, SY = P.V, P.S, P.T, P.G, P.SY
    NT = TL // 512

    h16 = P.sb("h16", [128, DC, TL], BF16)
    xln16 = P.sb("xln16", [128, DC, TL], BF16)
    sc = make_ln_scratch(P, TL)
    w2sb = P.sb("w2sb", [128, DC, D], BF16)
    vopb = P.sb("vopb", [128, 8])
    vlnw = P.sb("vlnw", [128, 8])
    vlnb = P.sb("vlnb", [128, 8])
    x16ch = [P.sb("x16ch0", [128, DC, 512], BF16), P.sb("x16ch1", [128, DC, 512], BF16)]
    tmpfD = P.sb("tmpfD", [128, 512], F32)
    pool = MMPool(P, n=4)
    gbuf = [P.sb("gbuf0", [128, DC, 512], BF16, freeable=True),
            P.sb("gbuf1", [128, DC, 512], BF16, freeable=True)]

    for kc in range(DC):
        G.dma_start(w2sb[:, kc, :], opw[kc * 128:(kc + 1) * 128, :], inc="ld", k=16)
    for dst, src_ in [(vopb, opb8), (vlnw, lnw8), (vlnb, lnb8)]:
        SY.dma_start(dst[:, :], src_, inc="ld2", k=16)
    emit_ln_init(P, sc)

    use_done = {"x16ch0": None, "x16ch1": None}

    def stream16(dram, tc, consumers=None):
        par = tc % 2
        name = f"x16ch{par}"
        if use_done[name] is not None:
            G.wait(*use_done[name])
        G.dma_start(x16ch[par][:, :, :],
                    dram[:, tc * 512:(tc + 1) * 512].rearrange("(c p) w -> p c w", p=128),
                    inc=f"ld16{par}", k=16)
        return par, (f"ld16{par}", P.now(f"ld16{par}"))

    for tc in range(NT):
        par, wt = stream16(xT_sl, tc)
        emit_ln_reduce_chunk(P, sc, x16ch[par], tc, 512, wait=wt)
        use_done[f"x16ch{par}"] = ("Tq", P.now("Tq"))
    emit_ln_stats(P, sc, TL)
    for tc in range(NT):
        par, wt = stream16(xT_sl, tc)
        emit_ln_bcast_chunk(P, sc, tc, 512)
        V.wait(*wt)
        emit_ln_apply_chunk(P, sc, x16ch[par], xln16[:, :, tc * 512:(tc + 1) * 512], 512)
        use_done[f"x16ch{par}"] = ("V", P.now("V"))
    emit_barrier(P, dma_sems=("ld", "ld2"))

    hw16 = sc["tmp16"]
    gdone = [0, 0]
    for tc in range(NT):
        par = tc % 2
        if gdone[par]:
            SY.wait("Tq", gdone[par])
        SY.dma_start(gbuf[par][:, :, :],
                     g16T[:, tc * 512:(tc + 1) * 512].rearrange("(c p) w -> p c w", p=128),
                     inc=f"ldg{par}", k=16)
        Tn.wait(f"ldg{par}", P.now(f"ldg{par}"))
        for oc in range(DC):
            p = pool.begin()
            for kc in range(DC):
                Tn.matmul(p[:, :], w2sb[:, kc, oc * 128:(oc + 1) * 128],
                          gbuf[par][:, kc, :],
                          start=(kc == 0), stop=(kc == DC - 1))
            S.wait("Tq", P.now("Tq"))
            S.wait("V", P.now("V"))
            S.activation(hw16[:, :], p[:, :], AF.Identity, bias=vopb[:, oc:oc + 1], scale=1.0)
            V.wait("S", P.now("S"))
            V.tensor_tensor(h16[:, oc, tc * 512:(tc + 1) * 512], hw16[:, :],
                            xln16[:, oc, tc * 512:(tc + 1) * 512], op=OP.add)
            pool.mark("V")
        gdone[par] = P.now("Tq")
    emit_barrier(P)
    P.free("gbuf1"); P.free("gbuf0")
    xbuf = [P.sb("xbuf0", [128, DC, 512], F32, freeable=True),
            P.sb("xbuf1", [128, DC, 512], F32, freeable=True)]
    x1c = [P.sb("x1c0", [128, DC, 512], F32, freeable=True),
           P.sb("x1c1", [128, DC, 512], F32, freeable=True)]

    for tc in range(NT):
        emit_ln_reduce_chunk(P, sc, h16[:, :, tc * 512:(tc + 1) * 512], tc, 512,
                             wait=("V", P.now("V")))
    emit_ln_stats(P, sc, TL)
    xd_done = [0, 0]
    st_done = [0, 0]
    for tc in range(NT):
        par = tc % 2
        if xd_done[par]:
            SY.wait("V", xd_done[par])
        SY.dma_start(xbuf[par][:, :, :],
                     xT_sl[:, tc * 512:(tc + 1) * 512].rearrange("(c p) w -> p c w", p=128),
                     inc=f"ldD{par}", k=16)
        emit_ln_bcast_chunk(P, sc, tc, 512)
        V.wait(f"ldD{par}", P.now(f"ldD{par}"))
        if st_done[par]:
            V.wait(f"stx{par}", st_done[par])
        for kc in range(DC):
            V.tensor_tensor(sc["tmp16"][:, :], h16[:, kc, tc * 512:(tc + 1) * 512],
                            sc["c1b"][:, :], op=OP.mult)
            V.tensor_tensor(sc["tmp16"][:, :], sc["tmp16"][:, :], sc["c0b"][:, :], op=OP.add)
            S.wait("V", P.now("V"))
            S.activation(tmpfD[:, :], sc["tmp16"][:, :], AF.Identity,
                         bias=vlnb[:, kc:kc + 1], scale=vlnw[:, kc:kc + 1])
            V.wait("S", P.now("S"))
            V.tensor_tensor(x1c[par][:, kc, :], tmpfD[:, :], xbuf[par][:, kc, :], op=OP.add)
        xd_done[par] = P.now("V")
        SY.wait("V", P.now("V"))
        SY.dma_start(x1T[:, tc * 512:(tc + 1) * 512].rearrange("(c p) w -> p c w", p=128),
                     x1c[par][:, :, :], inc=f"stx{par}", k=16)
        st_done[par] = P.now(f"stx{par}")
    emit_barrier(P, dma_sems=("stx0", "stx1"))
    P.free("x1c1"); P.free("x1c0"); P.free("xbuf1"); P.free("xbuf0")
    xlnc = [P.sb("xlnc0", [128, DC, 512], BF16, freeable=True),
            P.sb("xlnc1", [128, DC, 512], BF16, freeable=True)]

    use_done["x16ch0"] = ("V", P.now("V")); use_done["x16ch1"] = ("V", P.now("V"))
    for tc in range(NT):
        par, wt = stream16(x1T, tc)
        emit_ln_reduce_chunk(P, sc, x16ch[par], tc, 512, wait=wt)
        use_done[f"x16ch{par}"] = ("Tq", P.now("Tq"))
    emit_ln_stats(P, sc, TL)
    stl_done = [0, 0]
    for tc in range(NT):
        par, wt = stream16(x1T, tc)
        emit_ln_bcast_chunk(P, sc, tc, 512)
        V.wait(*wt)
        if stl_done[par]:
            V.wait(f"stl{par}", stl_done[par])
        emit_ln_apply_chunk(P, sc, x16ch[par], xlnc[par], 512)
        use_done[f"x16ch{par}"] = ("V", P.now("V"))
        SY.wait("V", P.now("V"))
        SY.dma_start(x1lnT[:, tc * 512:(tc + 1) * 512].rearrange("(c p) w -> p c w", p=128),
                     xlnc[par][:, :, :], inc=f"stl{par}", k=16)
        stl_done[par] = P.now(f"stl{par}")
    SY.wait("stl0", P.now("stl0"))
    SY.wait("stl1", P.now("stl1"))



def build_l0b1():
    P = Prog()
    A = {}
    A["xT_sl"] = P.din("xT_sl", [D, TL])
    A["g16T"] = P.din("g16T", [D, TL], BF16)
    A["opw"] = P.din("opw", [D, D])
    A["opb8"] = P.din("opb8", [128, 8])
    A["lnw8"] = P.din("lnw8", [128, 8])
    A["lnb8"] = P.din("lnb8", [128, 8])
    A["x1T"] = P.dout("x1T", [D, TL])
    A["x1lnT"] = P.dout("x1lnT", [D, TL], BF16)
    emit_l0b1(P, A)
    return P

def prep_l0b1_inputs(ins, xT_host, g_all):
    def vecT(v, ncol):
        return np.ascontiguousarray(v.reshape(ncol, 128).T)
    maps = []
    for c in range(NCORE):
        b, half = c // 2, c % 2
        t0 = half * TL
        m = dict(
            xT_sl=np.ascontiguousarray(xT_host[b][:, t0:t0 + TL]),
            opw=ins["L0_attn_out_proj_w"],
            opb8=vecT(ins["L0_attn_out_proj_b"], 8),
            lnw8=vecT(ins["L0_attn_ln_w"], 8),
            lnb8=vecT(ins["L0_attn_ln_b"], 8),
        )
        if g_all[b] is not None:
            m["g16T"] = np.ascontiguousarray(g_all[b][:, t0:t0 + TL])
        maps.append(m)
    return maps




def emit_softmax_top2(P, lgT, nch, E, iotaE, scT, mkT, dispT, mselT, stat, spacer, need_msel=True):
    V, S = P.V, P.S
    m1, sums, rr, m1s, m2s, es = (stat["m1"], stat["sums"], stat["rr"],
                                  stat["m1s"], stat["m2s"], stat["es"])

    def space(n=2):
        for _ in range(n):
            V.tensor_copy(spacer[:, :], spacer[:, :])
        V.fence()

    for c in range(nch):
        V.tensor_reduce(m1[:, c:c + 1], lgT[:, c, :], axis=AX.X, op=OP.max, negate=True)
    space()
    S.wait("V", P.now("V"))
    for c in range(nch):
        S.activation(scT[:, c, :], lgT[:, c, :], AF.Exp, bias=m1[:, c:c + 1], scale=1.0)
    V.wait("S", P.now("S"))
    for c in range(nch):
        V.tensor_reduce(sums[:, c:c + 1], scT[:, c, :], axis=AX.X, op=OP.add)
    for c in range(nch):
        V.reciprocal(rr[:, c:c + 1], sums[:, c:c + 1])
    space()
    S.wait("V", P.now("V"))
    for c in range(nch):
        S.activation(scT[:, c, :], scT[:, c, :], AF.Identity, bias=0.0, scale=rr[:, c:c + 1])
    V.wait("S", P.now("S"))
    for c in range(nch):
        V.tensor_reduce(m1s[:, c:c + 1], scT[:, c, :], axis=AX.X, op=OP.max)
    space()
    for c in range(nch):
        V.tensor_scalar(mkT[:, c, :], scT[:, c, :], m1s[:, c:c + 1], None, op0=OP.is_lt)
    for c in range(nch):
        V.tensor_tensor(mkT[:, c, :], scT[:, c, :], mkT[:, c, :], op=OP.mult)
    for c in range(nch):
        V.tensor_reduce(m2s[:, c:c + 1], mkT[:, c, :], axis=AX.X, op=OP.max)
    space()
    for c in range(nch):
        V.tensor_scalar(mkT[:, c, :], scT[:, c, :], m2s[:, c:c + 1], None, op0=OP.is_ge)
    for c in range(nch):
        V.tensor_tensor(dispT[:, c, :], scT[:, c, :], mkT[:, c, :], op=OP.mult)
    if need_msel:
        for c in range(nch):
            V.tensor_scalar(mkT[:, c, :], dispT[:, c, :], 0.0, None, op0=OP.is_gt)
        for c in range(nch):
            V.tensor_tensor(mkT[:, c, :], iotaE[:, 0:E], mkT[:, c, :], op=OP.mult)
        for c in range(nch):
            V.tensor_reduce(es[:, c:c + 1], mkT[:, c, :], axis=AX.X, op=OP.max)
        space()
        for c in range(nch):
            V.tensor_scalar(mselT[:, c, :], iotaE[:, 0:E], es[:, c:c + 1], None, op0=OP.is_equal)



def emit_l0b2(P, A):
    TLh = TL // 2
    x1T = A["x1T"]
    x1lnT = A["x1lnT"]
    gate_wT = A["gate_wT"]
    gate_b8 = A["gate_b8"]
    w1t = A["w1t"]
    b1T = A["b1T"]
    w2t = A["w2t"]
    b2T = A["b2T"]
    oh8 = A["oh8"]
    eye128 = A["eye128"]
    x2T = A["x2T"]

    V, S, Tn, G, SY = P.V, P.S, P.T, P.G, P.SY
    NT = TL // 512

    acc16 = P.sb("acc16", [128, DC, TL], BF16)
    xlnc = [P.sb("xlnc0", [128, DC, 512], BF16), P.sb("xlnc1", [128, DC, 512], BF16)]
    wdispb = P.sb("wdispb", [128, TLh], BF16)
    mselbs = P.sb("mselbs", [128, TLh])
    disp8 = P.sb("disp8", [8, TL], BF16)
    msel8 = P.sb("msel8", [8, TL], BF16)
    gw16 = P.sb("gw16", [128, DC, 8], BF16)
    eye = P.sb("eye", [128, 128])
    oh16 = P.sb("oh16", [8, 8, 128], BF16)
    vgb = P.sb("vgb", [8, 1])
    b1sb = P.sb("b1sb", [128, 32])
    b2sb = P.sb("b2sb", [128, 8])
    tmpy = P.sb("tmpy", [128, 512])
    tmpy2 = P.sb("tmpy2", [128, 512])
    iota8 = P.sb("iota8", [128, 8])
    iota8i = P.sb("iota8i", [128, 8], I32)
    sc8 = P.sb("sc8", [128, 8])
    m8 = P.sb("m8", [128, 8])
    r1 = P.sb("r1", [128, 1]); r2 = P.sb("r2", [128, 1]); r3 = P.sb("r3", [128, 1])
    pool = MMPool(P, n=4)

    G.dma_start(gw16[:, :, :], gate_wT.rearrange("(c p) e -> p c e", p=128), inc="ld", k=16)
    G.dma_start(oh16[:, :, :], oh8, inc="ld", k=16)
    for dst, src_ in [(vgb, gate_b8), (eye, eye128)]:
        SY.dma_start(dst[:, :], src_, inc="ld2", k=16)
    for kc in range(DC):
        V.memset(acc16[:, kc, :], 0.0)
    G.iota(iota8i[:, :], [[1, 8]], base=0, channel_multiplier=0)
    G.tensor_copy(iota8[:, :], iota8i[:, :])

    lgT = P.sb("lgT", [128, 16, 8], freeable=True)
    dispT = P.sb("dispT", [128, 16, 8], freeable=True)
    scT_dbg = P.sb("scT_dbg", [128, 16, 8], freeable=True)
    m1_dbg_big = P.sb("m1_dbg_big", [128, 16, 8], freeable=True)
    mselT = P.sb("mselT", [128, 16, 8], freeable=True)
    lg = P.sb("lg", [8, TL], F32, freeable=True)
    pg = P.ps("pg", [8, 512], F32, freeable=True)
    ptT = P.ps("ptT", [128, 128], F32, freeable=True)
    free_pg_later = True
    Tn.wait("ld", P.now("ld"))
    S.wait("ld2", P.now("ld2"))
    xl_done = [None, None]
    for tc in range(NT):
        par = tc % 2
        if xl_done[par] is not None:
            SY.wait(*xl_done[par])
        SY.dma_start(xlnc[par][:, :, :],
                     x1lnT[:, tc * 512:(tc + 1) * 512].rearrange("(c p) w -> p c w", p=128),
                     inc=f"lnx{par}", k=16)
        Tn.wait(f"lnx{par}", P.now(f"lnx{par}"))
        Tn.wait("S", P.now("S"))
        for kc in range(DC):
            Tn.matmul(pg[:, :], gw16[:, kc, :], xlnc[par][:, kc, :],
                      start=(kc == 0), stop=(kc == DC - 1))
        xl_done[par] = ("Tq", P.now("Tq"))
        S.wait("Tq", P.now("Tq"))
        S.activation(lg[0:8, tc * 512:(tc + 1) * 512], pg[0:8, :], AF.Identity,
                     bias=vgb[0:8, 0:1], scale=1.0)
    for half in range(2):
        Tn.wait("S", P.now("S"))
        Tn.wait("V", P.now("V"))
        for cx in range(8):
            c = half * 8 + cx
            Tn.transpose(ptT[:, cx * 8:(cx + 1) * 8], lg[0:8, c * 128:(c + 1) * 128],
                         eye[0:8, 0:8])
        V.wait("Tq", P.now("Tq"))
        for cx in range(8):
            V.tensor_copy(lgT[:, half * 8 + cx, :], ptT[:, cx * 8:(cx + 1) * 8])
    G.wait("G", P.now("G"))
    V.wait("G", P.now("G"))
    stat = {k: P.sb("gstat_" + k, [128, 16], freeable=True)
            for k in ("m1", "sums", "rr", "m1s", "m2s", "es")}
    V.memset(tmpy[:, :], 0.0)
    emit_softmax_top2(P, lgT, 16, 8, iota8, scT_dbg, m1_dbg_big, dispT, mselT, stat, tmpy)
    for k in reversed(list(stat)):
        P.free("gstat_" + k)
    emit_barrier(P)
    P.free("ptT"); P.free("pg")
    pb8 = P.ps("pb8", [8, 2048], F32, freeable=True)
    for what, dst in ((dispT, disp8), (mselT, msel8)):
        Tn.wait("V", P.now("V"))
        for c in range(16):
            Tn.transpose(pb8[0:8, c * 128:(c + 1) * 128], what[:, c, :], eye[:, :])
        V.wait("Tq", P.now("Tq"))
        V.tensor_copy(dst[0:8, :], pb8[0:8, :])
    emit_barrier(P)
    P.free("pb8")
    P.free("lg")
    P.free("mselT")
    P.free("m1_dbg_big"); P.free("scT_dbg")
    P.free("dispT"); P.free("lgT")
    ypool = MMPool(P, n=4, name="ypool")
    h4k = P.sb("h4k", [128, 32, TLh], BF16, freeable=True)
    wbuf = P.sb("wbuf", [128, 16384], BF16, freeable=True)
    xw16 = P.sb("xw16", [128, DC, TLh], BF16, freeable=True)

    w1done = [None, None]
    w2done = None
    bdone = None
    for e in range(8):
        if bdone is not None:
            SY.wait("S", bdone)
        SY.dma_start(b1sb[:, :], b1T[e, :, :], inc="ldb", k=16)
        SY.dma_start(b2sb[:, :], b2T[e, :, :], inc="ldb", k=16)
        for st in range(2):
            t0 = st * TLh
            for q in range(2):
                p = pool.begin()
                Tn.wait("V", P.now("V"))
                Tn.matmul(p[:, :], oh16[:, e, :], disp8[0:8, t0 + q * 512:t0 + (q + 1) * 512],
                          start=True, stop=True)
                V.wait("Tq", P.now("Tq"))
                V.tensor_copy(wdispb[:, q * 512:(q + 1) * 512], p[:, :])
                pool.mark("V")
                p = pool.begin()
                Tn.matmul(p[:, :], oh16[:, e, :], msel8[0:8, t0 + q * 512:t0 + (q + 1) * 512],
                          start=True, stop=True)
                V.wait("Tq", P.now("Tq"))
                V.tensor_copy(mselbs[:, q * 512:(q + 1) * 512], p[:, :])
                pool.mark("V")
            V.wait("Tq", P.now("Tq"))
            for q in range(2):
                par = q % 2
                if xl_done[par] is not None:
                    SY.wait(*xl_done[par])
                SY.dma_start(xlnc[par][:, :, :],
                             x1lnT[:, t0 + q * 512:t0 + (q + 1) * 512].rearrange("(c p) w -> p c w", p=128),
                             inc=f"lnx{par}", k=16)
                V.wait(f"lnx{par}", P.now(f"lnx{par}"))
                for kc in range(DC):
                    V.tensor_tensor(xw16[:, kc, q * 512:(q + 1) * 512],
                                    xlnc[par][:, kc, :], wdispb[:, q * 512:(q + 1) * 512],
                                    op=OP.mult)
                xl_done[par] = ("V", P.now("V"))
            G.wait("Tq", P.now("Tq"))
            for hq in range(4):
                par = hq % 2
                if w1done[par] is not None:
                    G.wait(*w1done[par])
                w1v = wbuf[:, par * 8192:(par + 1) * 8192].rearrange("p (k w) -> p k w", k=DC)
                for kc in range(DC):
                    G.dma_start(w1v[:, kc, :],
                                w1t[e, kc * 128:(kc + 1) * 128, hq * 1024:(hq + 1) * 1024],
                                inc=f"ldw{par}", k=16)
                Tn.wait(f"ldw{par}", P.now(f"ldw{par}"))
                Tn.wait("V", P.now("V"))
                for hoc in range(8):
                    for q in range(2):
                        p = pool.begin()
                        for kc in range(DC):
                            Tn.matmul(p[:, :], w1v[:, kc, hoc * 128:(hoc + 1) * 128],
                                      xw16[:, kc, q * 512:(q + 1) * 512],
                                      start=(kc == 0), stop=(kc == DC - 1))
                        S.wait("Tq", P.now("Tq"))
                        S.wait("ldb", P.now("ldb"))
                        S.activation(h4k[:, hq * 8 + hoc, q * 512:(q + 1) * 512], p[:, :],
                                     AF.Relu, bias=b1sb[:, hq * 8 + hoc:hq * 8 + hoc + 1],
                                     scale=1.0)
                        pool.mark("S")
                w1done[par] = ("Tq", P.now("Tq"))
            S_h_done = P.now("S")
            for oh_ in range(2):
                G.wait("Tq", P.now("Tq"))
                w2v = wbuf.rearrange("p (k w) -> p k w", k=32)
                for kc2 in range(32):
                    G.dma_start(w2v[:, kc2, :],
                                w2t[e, kc2 * 128:(kc2 + 1) * 128, oh_ * 512:(oh_ + 1) * 512],
                                inc="ldw2", k=16)
                Tn.wait("ldw2", P.now("ldw2"))
                Tn.wait("S", S_h_done)
                for tg in range(2):
                    ps4 = [ypool.begin() for _ in range(4)]
                    for o4 in range(4):
                        for kc2 in range(32):
                            Tn.matmul(ps4[o4][:, :], w2v[:, kc2, o4 * 128:(o4 + 1) * 128],
                                      h4k[:, kc2, tg * 512:(tg + 1) * 512],
                                      start=(kc2 == 0), stop=(kc2 == 31))
                    for o4 in range(4):
                        oc2 = oh_ * 4 + o4
                        S.wait("Tq", P.now("Tq"))
                        S.wait("V", P.now("V"))
                        S.activation(tmpy[:, :], ps4[o4][:, :], AF.Identity,
                                     bias=b2sb[:, oc2:oc2 + 1], scale=1.0)
                        V.wait("S", P.now("S"))
                        V.tensor_tensor(tmpy2[:, :], tmpy[:, :],
                                        mselbs[:, tg * 512:(tg + 1) * 512], op=OP.mult)
                        V.tensor_tensor(acc16[:, oc2, t0 + tg * 512:t0 + (tg + 1) * 512],
                                        tmpy2[:, :],
                                        acc16[:, oc2, t0 + tg * 512:t0 + (tg + 1) * 512],
                                        op=OP.add)
                        ypool.mark("V")
                w2done = ("Tq", P.now("Tq"))
        bdone = P.now("S")
    emit_barrier(P)
    P.free("xw16"); P.free("wbuf"); P.free("h4k")
    x1b = [P.sb("x1b0", [128, DC, 512], F32, freeable=True),
           P.sb("x1b1", [128, DC, 512], F32, freeable=True)]
    x2b = [P.sb("x2b0", [128, DC, 512], F32, freeable=True),
           P.sb("x2b1", [128, DC, 512], F32, freeable=True)]
    xs_done = [None, None]
    ss_done = [None, None]
    for tc in range(NT):
        par = tc % 2
        if xs_done[par] is not None:
            SY.wait(*xs_done[par])
        SY.dma_start(x1b[par][:, :, :],
                     x1T[:, tc * 512:(tc + 1) * 512].rearrange("(c p) w -> p c w", p=128),
                     inc=f"lx{par}", k=16)
        V.wait(f"lx{par}", P.now(f"lx{par}"))
        if ss_done[par] is not None:
            V.wait(*ss_done[par])
        for kc in range(DC):
            V.tensor_tensor(x2b[par][:, kc, :], x1b[par][:, kc, :],
                            acc16[:, kc, tc * 512:(tc + 1) * 512], op=OP.add)
        xs_done[par] = ("V", P.now("V"))
        SY.wait("V", P.now("V"))
        SY.dma_start(x2T[:, tc * 512:(tc + 1) * 512].rearrange("(c p) w -> p c w", p=128),
                     x2b[par][:, :, :], inc=f"st{par}", k=16)
        ss_done[par] = (f"st{par}", P.now(f"st{par}"))
    SY.wait("st0", P.now("st0"))
    SY.wait("st1", P.now("st1"))



def build_l0b2():
    P = Prog()
    A = {}
    A["x1T"] = P.din("x1T", [D, TL])
    A["x1lnT"] = P.din("x1lnT", [D, TL], BF16)
    A["gate_wT"] = P.din("gate_wT", [D, 8])
    A["gate_b8"] = P.din("gate_b8", [8, 1])
    A["w1t"] = P.din("w1t", [8, D, 4096])
    A["b1T"] = P.din("b1T", [8, 128, 32])
    A["w2t"] = P.din("w2t", [8, 4096, D])
    A["b2T"] = P.din("b2T", [8, 128, 8])
    A["oh8"] = P.din("oh8", [8, 8, 128])
    A["eye128"] = P.din("eye128", [128, 128])
    A["x2T"] = P.dout("x2T", [D, TL])
    emit_l0b2(P, A)
    return P

def prep_l0b2_inputs(ins, x1T_all, x1lnT_all):
    maps = []
    oh8 = np.zeros((8, 8, 128), np.float32)
    for e in range(8):
        oh8[e, e, :] = 1.0
    eye128 = np.eye(128, dtype=np.float32)
    gate_wT = np.ascontiguousarray(ins["L0_ffn_gate_w"].T)
    gate_b8 = np.ascontiguousarray(ins["L0_ffn_gate_b"].reshape(8, 1))
    w1t = np.ascontiguousarray(ins["L0_ffn_w1"].transpose(0, 2, 1))
    w2t = np.ascontiguousarray(ins["L0_ffn_w2"].transpose(0, 2, 1))
    b1T = np.ascontiguousarray(ins["L0_ffn_b1"].reshape(8, 32, 128).transpose(0, 2, 1))
    b2T = np.ascontiguousarray(ins["L0_ffn_b2"].reshape(8, 8, 128).transpose(0, 2, 1))
    for c in range(NCORE):
        m = dict(
            gate_wT=gate_wT, gate_b8=gate_b8, w1t=w1t, b1T=b1T, w2t=w2t, b2T=b2T,
            oh8=oh8, eye128=eye128,
        )
        if x1T_all[c] is not None:
            m["x1T"] = x1T_all[c]; m["x1lnT"] = x1lnT_all[c]
        maps.append(m)
    return maps



def emit_l1a(P, A):
    x2T = A["x2T"]
    wqT = A["wqT"]
    wkT = A["wkT"]
    wvT = A["wvT"]
    bq8 = A["bq8"]
    bk8 = A["bk8"]
    bv8 = A["bv8"]
    wprojd = A["wprojd"]
    v16out = A["v16out"]
    zq_out = A["zq_out"]
    zk_out = A["zk_out"]

    V, S, Tn, G, SY = P.V, P.S, P.T, P.G, P.SY
    NT = TL // 512

    xln16 = P.sb("xln16", [128, DC, TL], BF16)
    v16 = P.sb("v16", [128, DC, TL], BF16)
    wq16 = P.sb("wq16", [128, DC, D], BF16)
    wk16 = P.sb("wk16", [128, DC, D], BF16)
    wv16 = P.sb("wv16", [128, DC, D], BF16)
    wp16 = P.sb("wp16", [128, 16, H], BF16)
    vbq = P.sb("vbq", [128, 8]); vbk = P.sb("vbk", [128, 8]); vbv = P.sb("vbv", [128, 8])
    sc = make_ln_scratch(P, TL)
    x16ch = [P.sb("x16ch0", [128, DC, 512], BF16), P.sb("x16ch1", [128, DC, 512], BF16)]
    qc16 = [P.sb("qc16_0", [128, TL], BF16), P.sb("qc16_1", [128, TL], BF16)]
    zq_sb = P.sb("zq_sb", [16, H, 128])
    zk_sb = P.sb("zk_sb", [16, H, 128])
    pool = MMPool(P, n=3)
    pz0 = P.ps("pz0", [16, 128], F32)
    pz = [pz0, pz0]

    for name, w, dst in (("q", wqT, wq16), ("k", wkT, wk16), ("v", wvT, wv16)):
        for kc in range(DC):
            G.dma_start(dst[:, kc, :], w[kc * 128:(kc + 1) * 128, :], inc="ld", k=16)
    G.dma_start(wp16[:, :, :], wprojd.rearrange("j p h -> p j h"), inc="ld", k=16)
    for dst, src_ in ((vbq, bq8), (vbk, bk8), (vbv, bv8)):
        SY.dma_start(dst[:, :], src_, inc="ld2", k=16)

    emit_ln_init(P, sc)
    use_done = {"x16ch0": None, "x16ch1": None}

    def stream16(dram, tc):
        par = tc % 2
        name = f"x16ch{par}"
        if use_done[name] is not None:
            G.wait(*use_done[name])
        G.dma_start(x16ch[par][:, :, :],
                    dram[:, tc * 512:(tc + 1) * 512].rearrange("(c p) w -> p c w", p=128),
                    inc=f"ld16{par}", k=16)
        return par, (f"ld16{par}", P.now(f"ld16{par}"))

    for tc in range(NT):
        par, wt = stream16(x2T, tc)
        emit_ln_reduce_chunk(P, sc, x16ch[par], tc, 512, wait=wt)
        use_done[f"x16ch{par}"] = ("Tq", P.now("Tq"))
    emit_ln_stats(P, sc, TL)
    for tc in range(NT):
        par, wt = stream16(x2T, tc)
        emit_ln_bcast_chunk(P, sc, tc, 512)
        V.wait(*wt)
        emit_ln_apply_chunk(P, sc, x16ch[par], xln16[:, :, tc * 512:(tc + 1) * 512], 512)
        use_done[f"x16ch{par}"] = ("V", P.now("V"))
    emit_barrier(P, dma_sems=("ld", "ld2"))

    zmark = 0
    qdone = [None, None]
    for which, wsb, bias, zsb in (("q", wq16, vbq, zq_sb), ("k", wk16, vbk, zk_sb), ("v", wv16, vbv, None)):
        for oc in range(DC):
            par = oc % 2
            for tc in range(NT):
                p = pool.begin()
                for kc in range(DC):
                    Tn.matmul(p[:, :], wsb[:, kc, oc * 128:(oc + 1) * 128],
                              xln16[:, kc, tc * 512:(tc + 1) * 512],
                              start=(kc == 0), stop=(kc == DC - 1))
                S.wait("Tq", P.now("Tq"))
                if zsb is None:
                    S.activation(v16[:, oc, tc * 512:(tc + 1) * 512], p[:, :], AF.Identity,
                                 bias=bias[:, oc:oc + 1], scale=1.0)
                else:
                    if qdone[par] is not None:
                        S.wait(*qdone[par])
                    S.activation(qc16[par][:, tc * 512:(tc + 1) * 512], p[:, :], AF.Identity,
                                 bias=bias[:, oc:oc + 1], scale=1.0)
                pool.mark("S")
            if zsb is not None:
                Tn.wait("S", P.now("S"))
                for gg in range(2):
                    g = 2 * oc + gg
                    pzt = pz[gg]
                    Tn.wait("V", P.now("V"))
                    for j in range(16):
                        Tn.matmul(pzt[:, :], wp16[gg * 64:gg * 64 + 64, j, :],
                                  qc16[par][gg * 64:gg * 64 + 64, j::16],
                                  start=(j == 0), stop=(j == 15))
                    V.wait("Tq", P.now("Tq"))
                    V.tensor_copy(zsb[:, g, :], pzt[0:16, :])
                zmark = P.now("Tq")
                qdone[par] = ("V", P.now("V"))
    SY.wait("S", P.now("S"))
    SY.wait("V", P.now("V"))
    SY.dma_start(v16out.rearrange("(c p) w -> p c w", p=128), v16[:, :, :], inc="st", k=16)
    SY.dma_start(zq_out.rearrange("h (g r) -> h g r", r=128), zq_sb[0:16, :, :], inc="st", k=16)
    SY.dma_start(zk_out.rearrange("h (g r) -> h g r", r=128), zk_sb[0:16, :, :], inc="st", k=16)
    SY.wait("st", P.now("st"))



def build_l1a():
    P = Prog()
    A = {}
    A["x2T"] = P.din("x2T", [D, TL])
    A["wqT"] = P.din("wqT", [D, D])
    A["wkT"] = P.din("wkT", [D, D])
    A["wvT"] = P.din("wvT", [D, D])
    A["bq8"] = P.din("bq8", [128, 8])
    A["bk8"] = P.din("bk8", [128, 8])
    A["bv8"] = P.din("bv8", [128, 8])
    A["wprojd"] = P.din("wprojd", [16, 128, H])
    A["v16out"] = P.dout("v16out", [D, TL], BF16)
    A["zq_out"] = P.dout("zq_out", [H, TL])
    A["zk_out"] = P.dout("zk_out", [H, TL])
    emit_l1a(P, A)
    return P

def prep_l1a_inputs(ins, x2T_all):
    def vecT(v, ncol):
        return np.ascontiguousarray(v.reshape(ncol, 128).T)
    wqT = np.ascontiguousarray(ins["L1_attn_Wq"].T)
    wkT = np.ascontiguousarray(ins["L1_attn_Wk"].T)
    wvT = np.ascontiguousarray(ins["L1_attn_Wv"].T)
    wpT = ins["L1_attn_Wproj"].T.reshape(16, 64, 16)
    wprojd = np.ascontiguousarray(np.concatenate([wpT, wpT], axis=1))
    maps = []
    for c in range(NCORE):
        m = dict(
            wqT=wqT, wkT=wkT, wvT=wvT,
            bq8=vecT(ins["L1_attn_bq"], 8), bk8=vecT(ins["L1_attn_bk"], 8),
            bv8=vecT(ins["L1_attn_bv"], 8), wprojd=wprojd,
        )
        if x2T_all[c] is not None:
            m["x2T"] = x2T_all[c]
        maps.append(m)
    return maps



def emit_bump(P, A):
    HL = 8
    v_slice = A["v_slice"]
    zq_loc = A["zq_loc"]
    zk_loc = A["zk_loc"]
    log_bw8 = A["log_bw8"]
    oh8x = A["oh8x"]
    eye16 = A["eye16"]
    ctx_out = A["ctx_out"]

    V, S, Tn, G, SY = P.V, P.S, P.T, P.G, P.SY

    zq = P.sb("zq", [HL, T])
    zk = P.sb("zk", [HL, T])
    lb = P.sb("lb", [1, HL])
    ohf = P.sb("ohf", [8, 8, 128])
    eye = P.sb("eye", [16, 16])
    row1 = P.sb("row1", [1, HL]); row2 = P.sb("row2", [1, HL])
    invb = P.sb("invb", [1, HL])
    invb128 = P.sb("invb128", [128, HL])
    epsb = P.sb("epsb", [1, 1])
    ones1f = P.sb("ones1f", [1, 128])
    ones16b = P.sb("ones16b", [128, 1], BF16)
    nzkT = P.sb("nzkT", [128, 32, HL])
    V2 = P.sb("V2", [128, 32, 128], BF16)
    zqb16 = P.sb("zqb16", [128, T], BF16)
    A16 = [P.sb("A16_0", [128, T], BF16), P.sb("A16_1", [128, T], BF16)]
    B16 = [P.sb("B16_0", [128, T], BF16), P.sb("B16_1", [128, T], BF16)]
    vsum = P.sb("vsum", [64, 1])
    ctxsb = P.sb("ctxsb", [64, T])

    SY.dma_start(zq[:, :], zq_loc, inc="ld", k=16)
    SY.dma_start(zk[:, :], zk_loc, inc="ld", k=16)
    SY.dma_start(lb[:, :], log_bw8, inc="ld", k=16)
    SY.dma_start(ohf[:, :, :], oh8x, inc="ld", k=16)
    SY.dma_start(eye[:, :], eye16, inc="ld", k=16)
    nld = P.now("ld")

    V.memset(epsb[:, :], 1e-4)
    V.memset(ones1f[:, :], 1.0)
    V.memset(ones16b[:, :], 1.0)
    S.wait("ld", nld)
    S.wait("V", P.now("V"))
    S.activation(row1[0:1, :], lb[0:1, :], AF.Abs, bias=0.0, scale=1.0)
    S.activation(row1[0:1, :], row1[0:1, :], AF.Exp, bias=0.0, scale=-1.0)
    S.activation(row1[0:1, :], row1[0:1, :], AF.Ln, bias=1.0, scale=1.0)
    S.activation(row2[0:1, :], lb[0:1, :], AF.Relu, bias=0.0, scale=1.0)
    V.wait("S", P.now("S"))
    V.tensor_tensor(invb[0:1, :], row1[0:1, :], row2[0:1, :], op=OP.add)
    V.tensor_scalar(invb[0:1, :], invb[0:1, :], epsb[0:1, 0:1], None, op0=OP.add)
    V.reciprocal(invb[0:1, :], invb[0:1, :])
    V.fence()
    pbc0 = P.ps("pbc0", [128, 512], F32, freeable=True)
    Tn.wait("V", P.now("V"))
    Tn.matmul(pbc0[:, 0:HL], ones1f[0:1, :], invb[0:1, :], start=True, stop=True)
    V.wait("Tq", P.now("Tq"))
    V.tensor_copy(invb128[:, :], pbc0[:, 0:HL])
    V.fence()
    P.free("pbc0")

    pzt = P.ps("pzt", [128, 16], F32, freeable=True)
    Tn.wait("ld", nld)
    for jc in range(32):
        Tn.wait("V", P.now("V"))
        Tn.transpose(pzt[:, 0:HL], zk[0:HL, jc * 128:(jc + 1) * 128], eye[0:HL, 0:HL])
        V.wait("Tq", P.now("Tq"))
        V.tensor_scalar(nzkT[:, jc, :], pzt[:, 0:HL], -1.0, None, op0=OP.mult)
    emit_barrier(P)
    P.free("pzt")

    vdone = None
    for hh in range(8):
        hp, hq = hh // 2, hh % 2
        if hq == 0:
            if vdone is not None:
                SY.wait(*vdone)
            for jc in range(32):
                SY.dma_start(V2[:, jc, :], v_slice[hp * 128:(hp + 1) * 128, jc * 128:(jc + 1) * 128],
                             transpose=True, inc="ldv", k=16)
        pbc = P.ps(f"pbch{hh}", [128, 512], F32, freeable=True)
        Tn.wait("V", P.now("V"))
        Tn.wait("S", P.now("S"))
        for qn in range(8):
            Tn.matmul(pbc[:, :], ohf[:, hh, :], zq[0:HL, qn * 512:(qn + 1) * 512],
                      start=True, stop=True)
            V.wait("Tq", P.now("Tq"))
            V.tensor_copy(zqb16[:, qn * 512:(qn + 1) * 512], pbc[:, :])
            Tn.wait("V", P.now("V"))
        P.free(f"pbch{hh}")
        pv = P.ps(f"pv{hh}", [64, 1], F32, freeable=True)
        Tn.wait("ldv", P.now("ldv"))
        Tn.wait("V", P.now("V"))
        for jc in range(32):
            Tn.matmul(pv[:, :], V2[:, jc, hq * 64:hq * 64 + 64], ones16b[:, :],
                      start=(jc == 0), stop=(jc == 31))
        V.wait("Tq", P.now("Tq"))
        V.tensor_copy(vsum[:, :], pv[0:64, :])
        V.fence()
        P.free(f"pv{hh}")
        pctx = P.ps(f"pctx{hh}", [64, T], F32, freeable=True)
        bdone = [None, None]
        for jc in range(32):
            par = jc % 2
            S.wait("V", P.now("V"))
            if bdone[par] is not None:
                S.wait(*bdone[par])
            S.activation(A16[par][:, :], zqb16[:, :], AF.Abs,
                         bias=nzkT[:, jc, hh:hh + 1], scale=1.0)
            V.wait("S", P.now("S"))
            if bdone[par] is not None:
                V.wait(*bdone[par])
            V.tensor_scalar(B16[par][:, :], A16[par][:, :], invb128[:, hh:hh + 1], 1.0,
                            op0=OP.mult, op1=OP.min)
            Tn.wait("V", P.now("V"))
            for qn in range(8):
                Tn.matmul(pctx[:, qn * 512:(qn + 1) * 512], V2[:, jc, hq * 64:hq * 64 + 64],
                          B16[par][:, qn * 512:(qn + 1) * 512],
                          start=(jc == 0), stop=(jc == 31))
            bdone[par] = ("Tq", P.now("Tq"))
        V.wait("Tq", P.now("Tq"))
        for qn in range(8):
            V.tensor_scalar(ctxsb[:, qn * 512:(qn + 1) * 512], pctx[0:64, qn * 512:(qn + 1) * 512],
                            vsum[0:64, 0:1], -1.0 / T, op0=OP.subtract, op1=OP.mult)
        P.free(f"pctx{hh}")
        SY.wait("V", P.now("V"))
        SY.dma_start(ctx_out[hh * 64:(hh + 1) * 64, :], ctxsb[0:64, :], inc="st", k=16)
        if hq == 1:
            vdone = ("Tq", P.now("Tq"))
    SY.wait("st", P.now("st"))



def build_bump():
    P = Prog()
    A = {}
    A["v_slice"] = P.din("v_slice", [512, T], BF16)
    A["zq_loc"] = P.din("zq_loc", [HL, T])
    A["zk_loc"] = P.din("zk_loc", [HL, T])
    A["log_bw8"] = P.din("log_bw8", [1, HL])
    A["oh8x"] = P.din("oh8x", [8, 8, 128])
    A["eye16"] = P.din("eye16", [16, 16])
    A["ctx_out"] = P.dout("ctx_out", [512, T])
    emit_bump(P, A)
    return P

def prep_bump_inputs(ins, v_all, zq_all, zk_all):
    oh8x = np.zeros((8, 8, 128), np.float32)
    for k in range(8):
        oh8x[k, k, :] = 1.0
    eye16 = np.eye(16, dtype=np.float32)
    maps = []
    for c in range(NCORE):
        b, half = c // 2, c % 2
        hs = slice(half * 8, half * 8 + 8)
        maps.append(dict(
            v_slice=np.ascontiguousarray(v_all[b][half * 512:(half + 1) * 512, :]),
            zq_loc=np.ascontiguousarray(zq_all[b][hs, :]),
            zk_loc=np.ascontiguousarray(zk_all[b][hs, :]),
            log_bw8=np.ascontiguousarray(ins["L1_attn_log_bandwidth"][hs].reshape(1, 8)),
            oh8x=oh8x, eye16=eye16,
        ))
    return maps



def emit_l1b(P, A):
    x2T = A["x2T"]
    ctxT = A["ctxT"]
    centT = A["centT"]
    eb16 = A["eb16"]
    wrt = A["wrt"]
    br16m = A["br16m"]
    wst = A["wst"]
    bsT2 = A["bsT2"]
    oh16x = A["oh16x"]
    eye128 = A["eye128"]
    outT = A["outT"]

    V, S, Tn, G, SY = P.V, P.S, P.T, P.G, P.SY
    NT = TL // 512

    x3 = P.sb("x3", [128, DC, TL])
    x3ln16 = P.sb("x3ln16", [128, DC, TL], BF16)
    wsum16 = P.sb("wsum16", [128, DC, D], BF16)
    cent16 = P.sb("cent16", [128, DC, H], BF16)
    veb = P.sb("veb", [H, 1])
    bs16 = P.sb("bs16", [2, D], BF16)
    halfones = P.sb("halfones", [2, 512], BF16)
    br16 = P.sb("br16", [H, D], BF16)
    eye = P.sb("eye", [128, 128])
    oh16 = P.sb("oh16", [16, 16, 128], BF16)
    g16b = P.sb("g16b", [H, TL], BF16)
    tmpf = P.sb("tmpf", [128, 512])
    tmpf2 = P.sb("tmpf2", [128, 512])
    spacer = P.sb("spacer", [128, 512])
    sc = make_ln_scratch(P, TL, freeable=True)

    G.dma_start(cent16[:, :, :], centT.rearrange("(c p) h -> p c h", p=128), inc="ld", k=16)
    G.dma_start(oh16[:, :, :], oh16x, inc="ld", k=16)
    G.dma_start(br16[:, :], br16m, inc="ld", k=16)
    G.dma_start(bs16[:, :], bsT2, inc="ld", k=16)
    SY.dma_start(veb[:, :], eb16, inc="ld2", k=16)
    SY.dma_start(eye[:, :], eye128, inc="ld2", k=16)

    ab = [P.sb("ab0", [128, DC, 512], F32, freeable=True)]
    cb = [P.sb("cb0", [128, DC, 512], F32, freeable=True)]
    for tc in range(NT):
        par = 0
        SY.dma_start(ab[par][:, :, :],
                     x2T[:, tc * 512:(tc + 1) * 512].rearrange("(c p) w -> p c w", p=128),
                     inc=f"lda{par}", k=16)
        SY.dma_start(cb[par][:, :, :],
                     ctxT[:, tc * 512:(tc + 1) * 512].rearrange("(c p) w -> p c w", p=128),
                     inc=f"ldc{par}", k=16)
        V.wait(f"lda{par}", P.now(f"lda{par}"))
        V.wait(f"ldc{par}", P.now(f"ldc{par}"))
        for kc in range(DC):
            V.tensor_tensor(x3[:, kc, tc * 512:(tc + 1) * 512], ab[par][:, kc, :],
                            cb[par][:, kc, :], op=OP.add)
        SY.wait("V", P.now("V"))
    emit_barrier(P)
    P.free("cb0"); P.free("ab0")

    emit_ln_init(P, sc)
    x16cE = P.sb("x16cE", [128, DC, TL], BF16, freeable=True)
    for kc in range(DC):
        V.tensor_copy(x16cE[:, kc, :], x3[:, kc, :])
    for tc in range(NT):
        emit_ln_reduce_chunk(P, sc, x16cE[:, :, tc * 512:(tc + 1) * 512], tc, 512,
                             wait=("V", P.now("V")))
    emit_ln_stats(P, sc, TL)
    for tc in range(NT):
        emit_ln_bcast_chunk(P, sc, tc, 512)
        emit_ln_apply_chunk(P, sc, x16cE[:, :, tc * 512:(tc + 1) * 512],
                            x3ln16[:, :, tc * 512:(tc + 1) * 512], 512)
    emit_barrier(P, dma_sems=("ld", "ld2"))
    P.free("x16cE")
    free_ln_scratch(P)

    lgT = P.sb("lgT", [128, 16, H], freeable=True)
    scT = P.sb("scT", [128, 16, H], freeable=True)
    mkT = P.sb("mkT", [128, 16, H], freeable=True)
    dispT = P.sb("dispT", [128, 16, H], freeable=True)
    lgs = P.sb("lgs", [H, TL], F32, freeable=True)
    stat = {k: P.sb("gstat_" + k, [128, 16], freeable=True)
            for k in ("m1", "sums", "rr", "m1s", "m2s", "es")}
    pg = P.ps("pg", [16, 512], F32, freeable=True)
    ptT = P.ps("ptT", [128, 256], F32, freeable=True)
    V.memset(spacer[:, :], 0.0)
    Tn.wait("ld", P.now("ld"))
    S.wait("ld2", P.now("ld2"))
    for tc in range(NT):
        Tn.wait("S", P.now("S"))
        for kc in range(DC):
            Tn.matmul(pg[:, :], cent16[:, kc, :], x3ln16[:, kc, tc * 512:(tc + 1) * 512],
                      start=(kc == 0), stop=(kc == DC - 1))
        S.wait("Tq", P.now("Tq"))
        S.activation(lgs[0:16, tc * 512:(tc + 1) * 512], pg[0:16, :], AF.Identity,
                     bias=veb[0:16, 0:1], scale=1.0)
    for half in range(2):
        Tn.wait("S", P.now("S"))
        Tn.wait("V", P.now("V"))
        for cx in range(8):
            c = half * 8 + cx
            Tn.transpose(ptT[:, cx * 16:(cx + 1) * 16], lgs[0:16, c * 128:(c + 1) * 128],
                         eye[0:16, 0:16])
        V.wait("Tq", P.now("Tq"))
        for cx in range(8):
            V.tensor_copy(lgT[:, half * 8 + cx, :], ptT[:, cx * 16:(cx + 1) * 16])
    emit_softmax_top2(P, lgT, 16, H, None, scT, mkT, dispT, None, stat, spacer, need_msel=False)
    pb16 = P.ps("pb16", [16, 2048], F32, freeable=True)
    Tn.wait("V", P.now("V"))
    for c in range(16):
        Tn.transpose(pb16[0:16, c * 128:(c + 1) * 128], dispT[:, c, :], eye[:, :])
    V.wait("Tq", P.now("Tq"))
    V.tensor_copy(g16b[0:16, :], pb16[0:16, :])
    emit_barrier(P)
    P.free("pb16"); P.free("ptT"); P.free("pg")
    for k in reversed(list(stat)):
        P.free("gstat_" + k)
    P.free("lgs"); P.free("dispT"); P.free("mkT"); P.free("scT"); P.free("lgT")

    wsb = [P.sb("wsb0", [128, D], F32, freeable=True),
           P.sb("wsb1", [128, D], F32, freeable=True)]
    V.memset(halfones[:, :], 0.5)
    for kc in range(DC):
        SY.wait("V", P.now("V"))
        SY.dma_start(wsb[0][:, :], wst[0, kc * 128:(kc + 1) * 128, :], inc="ldS", k=16)
        SY.dma_start(wsb[1][:, :], wst[1, kc * 128:(kc + 1) * 128, :], inc="ldS", k=16)
        V.wait("ldS", P.now("ldS"))
        V.tensor_tensor(wsb[0][:, :], wsb[0][:, :], wsb[1][:, :], op=OP.add)
        V.tensor_scalar(wsum16[:, kc, :], wsb[0][:, :], 0.5, None, op0=OP.mult)
    emit_barrier(P)
    P.free("wsb1"); P.free("wsb0")

    gb_all = P.sb("gb_all", [128, 16, 512], BF16)
    Gt0 = P.sb("Gt0", [128, DC, 512], BF16)
    Gt = [Gt0, Gt0]
    wrb = [P.sb("wrb0", [128, DC, D], BF16), P.sb("wrb1", [128, DC, D], BF16)]
    outb = P.sb("outb", [128, DC, 512], F32)
    pacc = [P.ps(f"pacc{i}", [128, 512], F32, freeable=True) for i in range(8)]
    pgb = pacc[0]
    wr_done = [None, None]
    gt_done = [None, None]
    st_done = None
    for tc in range(NT):
        for e in range(16):
            Tn.wait("V", P.now("V"))
            Tn.matmul(pgb[:, :], oh16[:, e, :], g16b[0:16, tc * 512:(tc + 1) * 512],
                      start=True, stop=True)
            V.wait("Tq", P.now("Tq"))
            V.tensor_copy(gb_all[:, e, :], pgb[:, :])
        for e in range(16):
            par = e % 2
            emit_barrier(P, dma_sems=("ldw0", "ldw1"))
            if wr_done[par] is not None:
                G.wait(*wr_done[par])
            for kc in range(DC):
                G.dma_start(wrb[par][:, kc, :], wrt[e, kc * 128:(kc + 1) * 128, :],
                            inc=f"ldw{par}", k=16)
            if gt_done[0] is not None:
                V.wait(*gt_done[0])
            for kc in range(DC):
                V.tensor_tensor(Gt[par][:, kc, :], x3ln16[:, kc, tc * 512:(tc + 1) * 512],
                                gb_all[:, e, :], op=OP.mult)
            Tn.wait(f"ldw{par}", P.now(f"ldw{par}"))
            Tn.wait("V", P.now("V"))
            for oc in range(DC):
                for kc in range(DC):
                    Tn.matmul(pacc[oc][:, :], wrb[par][:, kc, oc * 128:(oc + 1) * 128],
                              Gt[par][:, kc, :],
                              start=(e == 0 and kc == 0), stop=False)
            wr_done[par] = ("Tq", P.now("Tq"))
            gt_done[0] = ("Tq", P.now("Tq"))
        for oc in range(DC):
            Tn.matmul(pacc[oc][:, :], br16[0:16, oc * 128:(oc + 1) * 128],
                      g16b[0:16, tc * 512:(tc + 1) * 512], start=False, stop=False)
        for oc in range(DC):
            Tn.matmul(pacc[oc][:, :], bs16[0:2, oc * 128:(oc + 1) * 128],
                      halfones[0:2, :], start=False, stop=False)
        for oc in range(DC):
            for kc in range(DC):
                Tn.matmul(pacc[oc][:, :], wsum16[:, kc, oc * 128:(oc + 1) * 128],
                          x3ln16[:, kc, tc * 512:(tc + 1) * 512],
                          start=False, stop=(kc == DC - 1))
        if st_done is not None:
            V.wait(*st_done)
        V.wait("Tq", P.now("Tq"))
        for oc in range(DC):
            V.tensor_tensor(tmpf2[:, :], pacc[oc][:, :],
                            x3ln16[:, oc, tc * 512:(tc + 1) * 512], op=OP.add)
            V.tensor_tensor(outb[:, oc, :], tmpf2[:, :],
                            x3[:, oc, tc * 512:(tc + 1) * 512], op=OP.add)
        SY.wait("V", P.now("V"))
        SY.dma_start(outT[:, tc * 512:(tc + 1) * 512].rearrange("(c p) w -> p c w", p=128),
                     outb[:, :, :], inc="st", k=16)
        st_done = ("st", P.now("st"))
        Tn.wait("V", P.now("V"))
    SY.wait("st", P.now("st"))



def build_l1b():
    P = Prog()
    A = {}
    A["x2T"] = P.din("x2T", [D, TL])
    A["ctxT"] = P.din("ctxT", [D, TL])
    A["centT"] = P.din("centT", [D, H])
    A["eb16"] = P.din("eb16", [H, 1])
    A["wrt"] = P.din("wrt", [H, D, D])
    A["br16m"] = P.din("br16m", [H, D])
    A["wst"] = P.din("wst", [2, D, D])
    A["bsT2"] = P.din("bsT2", [2, D])
    A["oh16x"] = P.din("oh16x", [16, 16, 128])
    A["eye128"] = P.din("eye128", [128, 128])
    A["outT"] = P.dout("outT", [D, TL])
    emit_l1b(P, A)
    return P

def prep_l1b_inputs(ins, x2T_all, ctxT_all):
    oh16x = np.zeros((16, 16, 128), np.float32)
    for k in range(16):
        oh16x[k, k, :] = 1.0
    eye128 = np.eye(128, dtype=np.float32)
    centT = np.ascontiguousarray(ins["L1_ffn_centroids"].T)
    eb16 = np.ascontiguousarray(ins["L1_ffn_expert_biases"].reshape(16, 1))
    wrt = np.ascontiguousarray(ins["L1_ffn_Wr"].transpose(0, 2, 1))
    br16m = ins["L1_ffn_br"]
    wst = np.ascontiguousarray(ins["L1_ffn_Ws"].transpose(0, 2, 1))
    bsT2 = np.ascontiguousarray(ins["L1_ffn_bs"])
    maps = []
    for c in range(NCORE):
        m = dict(
            centT=centT, eb16=eb16,
            wrt=wrt, br16m=br16m, wst=wst, bsT2=bsT2, oh16x=oh16x, eye128=eye128,
        )
        if x2T_all[c] is not None:
            m["x2T"] = x2T_all[c]; m["ctxT"] = ctxT_all[c]
        maps.append(m)
    return maps



PAIRS = [[0, 1], [2, 3], [4, 5], [6, 7]]


def build_fused():
    P = Prog()
    nc = P.nc
    A = {}
    A["xT"] = P.din("xT", [D, T])
    A["w_uv"] = P.din("w_uv", [D, 1024])
    A["b_uv"] = P.din("b_uv", [128, 8])
    A["conv_w3"] = P.din("conv_w3", [128, 4, 3])
    A["conv_b4"] = P.din("conv_b4", [128, 4])
    A["logit_a4"] = P.din("logit_a4", [128, 4])
    A["ssm_b4"] = P.din("ssm_b4", [128, 4])
    A["ssm_c4"] = P.din("ssm_c4", [128, 4])
    A["ssm_d4"] = P.din("ssm_d4", [128, 4])
    A["xT_sl"] = P.din("xT_sl", [D, TL])
    A["opw"] = P.din("opw", [D, D])
    A["opb8"] = P.din("opb8", [128, 8])
    A["lnw8"] = P.din("lnw8", [128, 8])
    A["lnb8"] = P.din("lnb8", [128, 8])
    A["gate_wT"] = P.din("gate_wT", [D, 8])
    A["gate_b8"] = P.din("gate_b8", [8, 1])
    A["w1t"] = P.din("w1t", [8, D, 4096])
    A["b1T"] = P.din("b1T", [8, 128, 32])
    A["w2t"] = P.din("w2t", [8, 4096, D])
    A["b2T"] = P.din("b2T", [8, 128, 8])
    A["oh8"] = P.din("oh8", [8, 8, 128])
    A["eye128"] = P.din("eye128", [128, 128])
    A["wqT"] = P.din("wqT", [D, D])
    A["wkT"] = P.din("wkT", [D, D])
    A["wvT"] = P.din("wvT", [D, D])
    A["bq8"] = P.din("bq8", [128, 8])
    A["bk8"] = P.din("bk8", [128, 8])
    A["bv8"] = P.din("bv8", [128, 8])
    A["wprojd"] = P.din("wprojd", [16, 128, H])
    A["log_bw8"] = P.din("log_bw8", [1, 8])
    A["oh8x"] = P.din("oh8x", [8, 8, 128])
    A["eye16"] = P.din("eye16", [16, 16])
    A["centT"] = P.din("centT", [D, H])
    A["eb16"] = P.din("eb16", [H, 1])
    A["wrt"] = P.din("wrt", [H, D, D])
    A["br16m"] = P.din("br16m", [H, D])
    A["wst"] = P.din("wst", [2, D, D])
    A["bsT2"] = P.din("bsT2", [2, D])
    A["oh16x"] = P.din("oh16x", [16, 16, 128])
    meta = P.din("meta", [1, 8], I32)
    A["outT"] = P.dout("outT", [D, TL])

    g_mine = nc.dram_tensor("g_mine", [512, T], BF16).ap()
    g_pair = nc.dram_tensor("g_pair", [1024, T], BF16).ap()
    g_sl = nc.dram_tensor("g_sl", [D, TL], BF16).ap()
    x1T_i = nc.dram_tensor("x1T_i", [D, TL], F32).ap()
    x1lnT_i = nc.dram_tensor("x1lnT_i", [D, TL], BF16).ap()
    x2T_i = nc.dram_tensor("x2T_i", [D, TL], F32).ap()
    v_mine = nc.dram_tensor("v_mine", [D, TL], BF16).ap()
    v_pair = nc.dram_tensor("v_pair", [2 * D, TL], BF16).ap()
    v_sl = nc.dram_tensor("v_sl", [512, T], BF16).ap()
    z_mine = nc.dram_tensor("z_mine", [2 * H, TL], F32).ap()
    z_pair = nc.dram_tensor("z_pair", [4 * H, TL], F32).ap()
    zq_i = nc.dram_tensor("zq_i", [8, T], F32).ap()
    zk_i = nc.dram_tensor("zk_i", [8, T], F32).ap()
    ctx_mine = nc.dram_tensor("ctx_mine", [512, T], F32).ap()
    ctx_pair = nc.dram_tensor("ctx_pair", [1024, T], F32).ap()
    ctx_sl = nc.dram_tensor("ctx_sl", [D, TL], F32).ap()

    A["g_out"] = g_mine
    A["g16T"] = g_sl
    A["x1T"] = x1T_i
    A["x1lnT"] = x1lnT_i
    A["x2T"] = x2T_i
    A["v16out"] = v_mine
    A["zq_out"] = z_mine[0:H, :]
    A["zk_out"] = z_mine[H:2 * H, :]
    A["v_slice"] = v_sl
    A["zq_loc"] = zq_i
    A["zk_loc"] = zk_i
    A["ctx_out"] = ctx_mine
    A["ctxT"] = ctx_sl

    G, SY = P.G, P.SY
    smeta = P.sb("smeta_g", [1, 8], I32)
    SY.dma_start(smeta[:, :], meta, inc="ldm", k=16)

    rg = P.es.enter_context(nc.gpsimd.register("rgoff"))
    rg2 = P.es.enter_context(nc.gpsimd.register("rgoff2"))

    def ag(kind_in, kind_out):
        G.collective_compute("AllGather", mybir.AluOpType.bypass,
                             replica_groups=PAIRS, ins=[kind_in.opt()], outs=[kind_out.opt()])
        G.fence()

    with P.phase("_a"):
        emit_l0a(P, A)
    ag(g_mine, g_pair)
    G.wait("ldm", P.now("ldm"))
    G.reg_load(rg, smeta[0:1, 0:1])
    G.dma_start(bass.AP(g_sl.tensor, 0, [[TL, D], [1, TL]]),
                bass.AP(g_pair.tensor, rg, [[T, D], [1, TL]]), inc="xch", k=16)
    emit_barrier(P, all_sems=True)

    with P.phase("_b1"):
        emit_l0b1(P, A)

    with P.phase("_b2"):
        emit_l0b2(P, A)

    with P.phase("_c"):
        emit_l1a(P, A)
    ag(v_mine, v_pair)
    ag(z_mine, z_pair)
    G.reg_load(rg, smeta[0:1, 1:2])
    G.dma_start(bass.AP(v_sl.tensor, 0, [[T, 512], [1, TL]]),
                bass.AP(v_pair.tensor, rg, [[TL, 512], [1, TL]]), inc="xch", k=16)
    G.reg_load(rg2, smeta[0:1, 2:3])
    G.dma_start(bass.AP(v_sl.tensor, TL, [[T, 512], [1, TL]]),
                bass.AP(v_pair.tensor, rg2, [[TL, 512], [1, TL]]), inc="xch", k=16)
    G.reg_load(rg, smeta[0:1, 3:4])
    G.dma_start(bass.AP(zq_i.tensor, 0, [[T, 8], [256, 16], [1, 128]]),
                bass.AP(z_pair.tensor, rg, [[TL, 8], [128, 16], [1, 128]]), inc="xch", k=16)
    G.reg_load(rg2, smeta[0:1, 4:5])
    G.dma_start(bass.AP(zq_i.tensor, 128, [[T, 8], [256, 16], [1, 128]]),
                bass.AP(z_pair.tensor, rg2, [[TL, 8], [128, 16], [1, 128]]), inc="xch", k=16)
    G.reg_load(rg, smeta[0:1, 5:6])
    G.dma_start(bass.AP(zk_i.tensor, 0, [[T, 8], [256, 16], [1, 128]]),
                bass.AP(z_pair.tensor, rg, [[TL, 8], [128, 16], [1, 128]]), inc="xch", k=16)
    G.reg_load(rg2, smeta[0:1, 6:7])
    G.dma_start(bass.AP(zk_i.tensor, 128, [[T, 8], [256, 16], [1, 128]]),
                bass.AP(z_pair.tensor, rg2, [[TL, 8], [128, 16], [1, 128]]), inc="xch", k=16)
    emit_barrier(P, all_sems=True)

    with P.phase("_d"):
        emit_bump(P, A)
    ag(ctx_mine, ctx_pair)
    G.reg_load(rg, smeta[0:1, 0:1])
    G.dma_start(bass.AP(ctx_sl.tensor, 0, [[TL, D], [1, TL]]),
                bass.AP(ctx_pair.tensor, rg, [[T, D], [1, TL]]), inc="xch", k=16)
    emit_barrier(P, all_sems=True)

    with P.phase("_e"):
        emit_l1b(P, A)
    return P


def prep_fused_inputs(ins, xT_host):
    m_l0a = prep_l0a_inputs(ins, xT_host)
    m_l0b1 = prep_l0b1_inputs(ins, xT_host, [None] * B)
    m_l0b2 = prep_l0b2_inputs(ins, [None] * NCORE, [None] * NCORE)
    m_l1a = prep_l1a_inputs(ins, [None] * NCORE)
    m_bump = prep_bump_inputs_fused(ins)
    m_l1b = prep_l1b_inputs(ins, [None] * NCORE, [None] * NCORE)
    maps = []
    for c in range(NCORE):
        b, half = c // 2, c % 2
        m = {}
        for src in (m_l0a[c], m_l0b1[c], m_l0b2[c], m_l1a[c], m_bump[c], m_l1b[c]):
            m.update(src)
        for k in ("g16T", "x1T", "x1lnT", "x2T", "v_slice", "zq_loc", "zk_loc", "ctxT"):
            m.pop(k, None)
        m["meta"] = np.array([[half * TL,
                               half * 512 * TL,
                               D * TL + half * 512 * TL,
                               half * 8 * TL,
                               (2 * H + half * 8) * TL,
                               (H + half * 8) * TL,
                               (3 * H + half * 8) * TL,
                               0]], np.int32)
        maps.append(m)
    return maps


def prep_bump_inputs_fused(ins):
    oh8x = np.zeros((8, 8, 128), np.float32)
    for k in range(8):
        oh8x[k, k, :] = 1.0
    eye16 = np.eye(16, dtype=np.float32)
    maps = []
    for c in range(NCORE):
        b, half = c // 2, c % 2
        hs = slice(half * 8, half * 8 + 8)
        maps.append(dict(
            log_bw8=np.ascontiguousarray(ins["L1_attn_log_bandwidth"][hs].reshape(1, 8)),
            oh8x=oh8x, eye16=eye16,
        ))
    return maps



def kernel(**inputs):
    ins = {k: np.asarray(v) for k, v in inputs.items()}
    x = np.asarray(ins["L0_attn_x"], np.float32)
    xT_host = np.ascontiguousarray(x.transpose(0, 2, 1))
    P = build_fused()
    nc = P.finalize()
    maps = prep_fused_inputs(ins, xT_host)
    res = run_bass_kernel_spmd(nc, maps, core_ids=list(range(NCORE))).results
    out = np.empty((B, T, D), np.float32)
    for c in range(NCORE):
        b, half = c // 2, c % 2
        t0 = half * TL
        out[b, t0:t0 + TL, :] = np.asarray(res[c]["outT"]).T
    return out


# version 03
# speedup vs baseline: 1.1759x; 1.1759x over previous
import numpy as np
import contextlib
import concourse.bass as bass
import concourse.mybir as mybir
from concourse.alu_op_type import AluOpType as OP
from concourse.bass_utils import run_bass_kernel_spmd

F32 = mybir.dt.float32
BF16 = mybir.dt.bfloat16
U8 = mybir.dt.uint8
I32 = mybir.dt.int32
AF = mybir.ActivationFunctionType
AX = mybir.AxisListType

D = 1024
DC = 8
H = 16
DH = 64
B = 4
T = 4096
TL = 2048
NCORE = 8


PROG_SEM = {"vector": "V", "scalar": "S", "tensor": "Tq", "gpsimd": "G", "sync": "SYq"}
DMA_OPS = {"dma_start", "dma_start_transpose", "indirect_dma_start"}
REG_OPS = {"reg_load", "reg_mov", "reg_add", "reg_sub", "reg_mul", "reg_save", "reg_alu"}


class Eng:

    def __init__(self, prog, name):
        self._p = prog
        self._n = name
        self._ps = PROG_SEM[name]

    def wait(self, sem, val):
        if val <= 0:
            return
        h = self._p.sem(sem)
        self._p.steps[self._n].append(lambda q: q.wait_ge(h, val))

    def wait_now(self, sem):
        self.wait(sem, self._p.now(sem))

    def fence(self):
        self.wait(self._ps, self._p.now(self._ps))

    def __getattr__(self, attr):
        if attr.startswith("_"):
            raise AttributeError(attr)
        p = self._p

        def call(*args, inc=None, k=1, **kw):
            if attr in DMA_OPS:
                assert inc is not None, f"DMA op {attr} needs inc="
                semh = p.sem(inc)
                p.inc(inc, 16)

                def f(q):
                    getattr(q, attr)(*args, **kw).then_inc(semh, 16)
            elif attr in REG_OPS:
                def f(q):
                    getattr(q, attr)(*args, **kw)
            else:
                assert inc is None
                semh = p.sem(self._ps)
                p.inc(self._ps, 1)

                def f(q):
                    getattr(q, attr)(*args, **kw).then_inc(semh, 1)
            p.steps[self._n].append(f)
        return call


class Prog:
    def __init__(self):
        self.nc = bass.Bass("TRN2", target_bir_lowering=False)
        self.es = contextlib.ExitStack()
        self.steps = {e: [] for e in ("sync", "vector", "scalar", "tensor", "gpsimd")}
        self.cnt = {}
        self.sems = {}
        self._cms = {}
        self.T = Eng(self, "tensor")
        self.V = Eng(self, "vector")
        self.S = Eng(self, "scalar")
        self.G = Eng(self, "gpsimd")
        self.SY = Eng(self, "sync")

    def sem(self, name):
        if name not in self.sems:
            self.sems[name] = self.es.enter_context(self.nc.semaphore(name))
            self.cnt[name] = 0
        return self.sems[name]

    def inc(self, name, kk=1):
        self.sem(name)
        self.cnt[name] += kk
        return self.cnt[name]

    def now(self, name):
        self.sem(name)
        return self.cnt.get(name, 0)

    tag = ""
    phase_mode = False

    def sb(self, name, shape, dt=F32, freeable=False):
        name = name + self.tag
        cm = self.nc.sbuf_tensor(name, shape, dt)
        if freeable or self.phase_mode:
            h = cm.__enter__()
            self._cms[name] = cm
            return h
        return self.es.enter_context(cm)

    def free(self, name):
        name2 = name + self.tag
        if name2 not in self._cms:
            name2 = name
        self._cms.pop(name2).__exit__(None, None, None)

    def free_ps(self, name):
        self.free(name)

    def ps(self, name, shape, dt=F32, freeable=False):
        name = name + self.tag
        cm = self.nc.psum_tensor(name, shape, dt)
        if freeable or self.phase_mode:
            h = cm.__enter__()
            self._cms[name] = cm
            return h
        return self.es.enter_context(cm)

    import contextlib as _ctx

    @_ctx.contextmanager
    def phase(self, tag):
        before = list(self._cms)
        old_tag, old_mode = self.tag, self.phase_mode
        self.tag, self.phase_mode = tag, True
        try:
            yield
        finally:
            self.tag, self.phase_mode = old_tag, old_mode
            emit_barrier(self, all_sems=True)
            for name in reversed([n for n in self._cms if n not in before]):
                self._cms.pop(name).__exit__(None, None, None)

    def din(self, name, shape, dt=F32):
        return self.nc.dram_tensor(name, shape, dt, kind="ExternalInput").ap()

    def dout(self, name, shape, dt=F32):
        return self.nc.dram_tensor(name, shape, dt, kind="ExternalOutput").ap()

    def finalize(self):
        for name in reversed(list(self._cms)):
            self.free(name)
        with self.es:
            with self.nc.Block() as blk:
                for eng, fns in self.steps.items():
                    if fns:
                        getattr(blk, eng)(lambda q, fns=fns: [f(q) for f in fns])
        return self.nc


def ceil_div(a, b):
    return (a + b - 1) // b


def emit_barrier(P, dma_sems=(), all_sems=False):
    pairs = [(P.V, "V"), (P.S, "S"), (P.T, "Tq"), (P.G, "G"), (P.SY, "SYq")]
    counts = {s: P.now(s) for _, s in pairs}
    if all_sems:
        for s in list(P.cnt):
            counts[s] = P.now(s)
    for ds in dma_sems:
        counts[ds] = P.now(ds)
    for em, s in pairs:
        for s2, c in counts.items():
            if s2 != s:
                em.wait(s2, c)


class MMPool:

    def __init__(self, P, n=4, name="mmp", shape=(128, 512)):
        self.P = P
        self.tiles = [P.ps(f"{name}{i}", list(shape), F32) for i in range(n)]
        self.n = n
        self.group = 0
        self.marks = []

    def begin(self):
        i = self.group
        t = self.tiles[i % self.n]
        if i >= self.n:
            sem, cntv = self.marks[i - self.n]
            self.P.T.wait(sem, cntv)
        self.group += 1
        return t

    def mark(self, sem):
        self.marks.append((sem, self.P.now(sem)))



LN_SCRATCH_NAMES = ["sq16", "ones16", "ones1f", "c1", "c0", "epsv", "c1b", "c0b", "tmp16"]


def free_ln_scratch(P, tag=""):
    P.free_ps("lns_pbc1" + tag); P.free_ps("lns_pbc0" + tag)
    P.free_ps("lns_pred1" + tag); P.free_ps("lns_pred0" + tag)
    names = ["lns_sq0", "lns_ones16", "lns_ones1f", "lns_c1", "lns_c0", "lns_eps",
             "lns_c1b", "lns_c0b", "lns_tmp16"]
    for n in reversed(names):
        P.free(n + tag)


def make_ln_scratch(P, nW_max, tag="", freeable=False):
    return {
        "nW_max": nW_max,
        "sq16": P.sb("lns_sq0" + tag, [128, 512], BF16, freeable=freeable),
        "ones16": P.sb("lns_ones16" + tag, [128, 1], BF16, freeable=freeable),
        "ones1f": P.sb("lns_ones1f" + tag, [1, 128], F32, freeable=freeable),
        "c1": P.sb("lns_c1" + tag, [1, nW_max], F32, freeable=freeable),
        "c0": P.sb("lns_c0" + tag, [1, nW_max], F32, freeable=freeable),
        "epsv": P.sb("lns_eps" + tag, [1, 1], F32, freeable=freeable),
        "c1b": P.sb("lns_c1b" + tag, [128, 512], BF16, freeable=freeable),
        "c0b": P.sb("lns_c0b" + tag, [128, 512], BF16, freeable=freeable),
        "tmp16": P.sb("lns_tmp16" + tag, [128, 512], BF16, freeable=freeable),
        "p_red": [P.ps("lns_pred0" + tag, [1, 512], F32, freeable=freeable),
                  P.ps("lns_pred1" + tag, [1, 512], F32, freeable=freeable)],
        "p_bc": [P.ps("lns_pbc0" + tag, [128, 512], F32, freeable=freeable),
                 P.ps("lns_pbc1" + tag, [128, 512], F32, freeable=freeable)],
    }


def emit_ln_init(P, sc):
    P.V.memset(sc["ones16"][:, :], 1.0)
    P.V.memset(sc["epsv"][:, :], 1e-5)
    P.V.memset(sc["ones1f"][:, :], 1.0)


def emit_ln_reduce_chunk(P, sc, x16ch, tc, w, wait=None):
    V, S, Tn = P.V, P.S, P.T
    if wait is not None:
        S.wait(*wait)
        Tn.wait(*wait)
    Tn.wait("V", P.now("V"))
    for kc in range(DC):
        S.wait("Tq", P.now("Tq"))
        S.activation(sc["sq16"][:, 0:w], x16ch[:, kc, 0:w], AF.Square, bias=0.0, scale=1.0)
        Tn.wait("S", P.now("S"))
        Tn.matmul(sc["p_red"][0][0:1, 0:w], sc["ones16"][:, :], x16ch[:, kc, 0:w],
                  start=(kc == 0), stop=(kc == DC - 1))
        Tn.matmul(sc["p_red"][1][0:1, 0:w], sc["ones16"][:, :], sc["sq16"][:, 0:w],
                  start=(kc == 0), stop=(kc == DC - 1))
    V.wait("Tq", P.now("Tq"))
    V.tensor_copy(sc["c1"][0:1, tc * 512:tc * 512 + w], sc["p_red"][0][0:1, 0:w])
    V.tensor_copy(sc["c0"][0:1, tc * 512:tc * 512 + w], sc["p_red"][1][0:1, 0:w])


def emit_ln_stats(P, sc, nW):
    V, S = P.V, P.S
    c1, c0 = sc["c1"], sc["c0"]
    V.tensor_scalar(c1[0:1, 0:nW], c1[0:1, 0:nW], 1.0 / D, None, op0=OP.mult)
    V.tensor_scalar(c0[0:1, 0:nW], c0[0:1, 0:nW], 1.0 / D, None, op0=OP.mult)
    for tc in range(ceil_div(nW, 512)):
        w = min(512, nW - tc * 512)
        sl = (slice(0, 1), slice(tc * 512, tc * 512 + w))
        V.tensor_tensor(sc["p_red"][0][0:1, 0:w], c1[sl], c1[sl], op=OP.mult)
        V.tensor_tensor(c0[sl], c0[sl], sc["p_red"][0][0:1, 0:w], op=OP.subtract)
    S.wait_now("V")
    S.activation(c0[0:1, 0:nW], c0[0:1, 0:nW], AF.Sqrt, bias=sc["epsv"][0:1, 0:1], scale=1.0)
    V.wait_now("S")
    V.reciprocal(c0[0:1, 0:nW], c0[0:1, 0:nW])
    V.tensor_tensor(c1[0:1, 0:nW], c1[0:1, 0:nW], c0[0:1, 0:nW], op=OP.mult)
    V.tensor_scalar(c1[0:1, 0:nW], c1[0:1, 0:nW], -1.0, None, op0=OP.mult)


def emit_ln_bcast_chunk(P, sc, tc, w):
    V, Tn = P.V, P.T
    Tn.wait("V", P.now("V"))
    Tn.matmul(sc["p_bc"][0][:, 0:w], sc["ones1f"][0:1, :], sc["c0"][0:1, tc * 512:tc * 512 + w],
              start=True, stop=True)
    Tn.matmul(sc["p_bc"][1][:, 0:w], sc["ones1f"][0:1, :], sc["c1"][0:1, tc * 512:tc * 512 + w],
              start=True, stop=True)
    V.wait("Tq", P.now("Tq"))
    V.tensor_copy(sc["c1b"][:, 0:w], sc["p_bc"][0][:, 0:w])
    V.tensor_copy(sc["c0b"][:, 0:w], sc["p_bc"][1][:, 0:w])


def emit_ln_apply_chunk(P, sc, x16ch, out16ch, w):
    V = P.V
    for kc in range(DC):
        V.tensor_tensor(sc["tmp16"][:, 0:w], x16ch[:, kc, 0:w], sc["c1b"][:, 0:w], op=OP.mult)
        V.tensor_tensor(out16ch[:, kc, 0:w], sc["tmp16"][:, 0:w], sc["c0b"][:, 0:w], op=OP.add)



def emit_l0a(P, A):
    xT = A["xT"]
    w_uv = A["w_uv"]
    b_uv = A["b_uv"]
    conv_w3 = A["conv_w3"]
    conv_b4 = A["conv_b4"]
    logit_a4 = A["logit_a4"]
    ssm_b4 = A["ssm_b4"]
    ssm_c4 = A["ssm_c4"]
    ssm_d4 = A["ssm_d4"]
    g_out = A["g_out"]

    V, S, Tn, G, SY = P.V, P.S, P.T, P.G, P.SY

    xln16 = P.sb("xln16", [128, DC, T], BF16)
    wuv16 = P.sb("wuv16", [128, DC, 1024], BF16)
    vb_uv = P.sb("vb_uv", [128, 8])
    vconv_w = P.sb("vconv_w", [128, 4, 3])
    vconv_b = P.sb("vconv_b", [128, 4])
    vlogit = P.sb("vlogit", [128, 4])
    vsb = P.sb("vsb", [128, 4])
    vsc_ = P.sb("vsc", [128, 4])
    vsd = P.sb("vsd", [128, 4])
    va = P.sb("va", [128, 4])
    vlna = P.sb("vlna", [128, 4])
    va128 = P.sb("va128", [128, 4])
    s128 = P.sb("s128", [128, 1])
    ones128 = P.sb("ones128", [128, 128])
    iotaf = P.sb("iotaf", [128, 128])
    iotai = P.sb("iotai", [128, 128], I32)
    apw = P.sb("apw", [128, 128])
    Acg = P.sb("Acg", [128, 128])
    Acg128 = P.sb("Acg128", [128, 32])
    hend = P.sb("hend", [128, 32])
    scanC = P.sb("scanC", [128, 32])
    sc = make_ln_scratch(P, T, freeable=True)
    xbuf = [P.sb("xbuf0", [128, DC, 512], F32, freeable=True),
            P.sb("xbuf1", [128, DC, 512], F32, freeable=True)]
    x16ch = [P.sb("x16ch0", [128, DC, 512], BF16, freeable=True),
             P.sb("x16ch1", [128, DC, 512], BF16, freeable=True)]

    for kc in range(DC):
        G.dma_start(wuv16[:, kc, :], w_uv[kc * 128:(kc + 1) * 128, :], inc="ld", k=16)
    for dst, src_ in [(vb_uv, b_uv), (vconv_w, conv_w3), (vconv_b, conv_b4), (vlogit, logit_a4),
                      (vsb, ssm_b4), (vsc_, ssm_c4), (vsd, ssm_d4)]:
        SY.dma_start(dst[:, :] if dst is not vconv_w else dst[:, :, :], src_, inc="ld2", k=16)
    nld2 = P.now("ld2")

    emit_ln_init(P, sc)

    cast_done = []

    def stream_pass(apply):
        for tc in range(8):
            par = tc % 2
            if len(cast_done) >= 2:
                SY.wait("V", cast_done[-2])
                G.wait("V", cast_done[-2])
            for kc in range(DC):
                q = SY if kc % 2 == 0 else G
                q.dma_start(xbuf[par][:, kc, :],
                            xT[kc * 128:(kc + 1) * 128, tc * 512:(tc + 1) * 512],
                            inc=f"ldx{par}", k=16)
            V.wait(f"ldx{par}", P.now(f"ldx{par}"))
            for kc in range(DC):
                V.tensor_copy(x16ch[par][:, kc, :], xbuf[par][:, kc, :])
            cast_done.append(P.now("V"))
            if apply:
                emit_ln_bcast_chunk(P, sc, tc, 512)
                emit_ln_apply_chunk(P, sc, x16ch[par], xln16[:, :, tc * 512:(tc + 1) * 512], 512)
            else:
                emit_ln_reduce_chunk(P, sc, x16ch[par], tc, 512, wait=("V", P.now("V")))

    stream_pass(False)
    emit_ln_stats(P, sc, T)
    stream_pass(True)
    emit_barrier(P)
    P.free("x16ch1"); P.free("x16ch0"); P.free("xbuf1"); P.free("xbuf0")
    free_ln_scratch(P)

    u16 = P.sb("u16", [128, T], BF16)
    vA = P.sb("vA", [128, T + 2])
    vB = P.sb("vB", [128, T])
    vC = P.sb("vC", [128, T])
    Hc = P.sb("Hc", [128, T])
    g16 = P.sb("g16", [128, T], BF16)
    ys16 = P.sb("ys16", [128, T], BF16)
    pool = MMPool(P, n=4)

    V.memset(ones128[:, :], 1.0)
    V.memset(s128[:, :], 128.0)
    G.wait("ld2", nld2)
    G.iota(iotai[:, :], [[1, 128]], base=1, channel_multiplier=0)
    G.tensor_copy(iotaf[:, :], iotai[:, :])
    S.wait("ld2", nld2)
    S.wait_now("V")
    S.activation(va[:, :], vlogit[:, :], AF.Sigmoid, bias=0.0, scale=1.0)
    S.activation(vlna[:, :], va[:, :], AF.Ln, bias=0.0, scale=1.0)
    S.activation(va128[:, :], vlna[:, :], AF.Exp, bias=0.0, scale=s128[:, 0:1])
    emit_barrier(P)

    Tn.wait("ld", 16 * DC)
    v_after_gate = 0
    for cg in range(4):
        first_evac = True
        for which, oc in (("u", cg), ("v", cg + 4)):
            for tc in range(8):
                p = pool.begin()
                for kc in range(DC):
                    Tn.matmul(p[:, :], wuv16[:, kc, oc * 128:(oc + 1) * 128],
                              xln16[:, kc, tc * 512:(tc + 1) * 512],
                              start=(kc == 0), stop=(kc == DC - 1))
                S.wait("Tq", P.now("Tq"))
                if first_evac and cg > 0:
                    S.wait("V", v_after_gate)
                first_evac = False
                if which == "u":
                    S.activation(u16[:, tc * 512:(tc + 1) * 512], p[:, :], AF.Identity,
                                 bias=vb_uv[:, oc:oc + 1], scale=1.0)
                else:
                    S.activation(vA[:, 1 + tc * 512:1 + (tc + 1) * 512], p[:, :], AF.Silu,
                                 bias=vb_uv[:, oc:oc + 1], scale=1.0)
                pool.mark("S")
        V.memset(vA[:, 0:1], 0.0)
        V.memset(vA[:, T + 1:T + 2], 0.0)
        V.wait("S", P.now("S"))
        V.tensor_scalar(vB[:, :], vA[:, 0:T], vconv_w[:, cg, 0:1], vconv_b[:, cg:cg + 1],
                        op0=OP.mult, op1=OP.add)
        V.scalar_tensor_tensor(vC[:, :], vA[:, 1:T + 1], vconv_w[:, cg, 1:2], vB[:, :],
                               op0=OP.mult, op1=OP.add)
        V.scalar_tensor_tensor(vB[:, :], vA[:, 2:T + 2], vconv_w[:, cg, 2:3], vC[:, :],
                               op0=OP.mult, op1=OP.add)
        V.tensor_scalar(vC[:, :], vB[:, :], vsb[:, cg:cg + 1], None, op0=OP.mult)
        V.tensor_scalar(Acg[:, :], ones128[:, :], va[:, cg:cg + 1], None, op0=OP.mult)
        for tch in range(32):
            V.tensor_tensor_scan(Hc[:, tch * 128:(tch + 1) * 128], Acg[:, :],
                                 vC[:, tch * 128:(tch + 1) * 128], 0.0,
                                 op0=OP.mult, op1=OP.add)
            V.fence()
        V.tensor_copy(hend[:, :], Hc[:, 127:T:128])
        V.tensor_scalar(Acg128[:, :], ones128[:, 0:32], va128[:, cg:cg + 1], None, op0=OP.mult)
        V.fence()
        V.tensor_tensor_scan(scanC[:, :], Acg128[:, :], hend[:, :], 0.0,
                             op0=OP.mult, op1=OP.add)
        V.fence()
        V.tensor_scalar(apw[:, :], iotaf[:, :], vlna[:, cg:cg + 1], None, op0=OP.mult)
        S.wait("V", P.now("V"))
        S.activation(apw[:, :], apw[:, :], AF.Exp, bias=0.0, scale=1.0)
        V.wait("S", P.now("S"))
        for tch in range(1, 32):
            V.scalar_tensor_tensor(Hc[:, tch * 128:(tch + 1) * 128], apw[:, :],
                                   scanC[:, tch - 1:tch], Hc[:, tch * 128:(tch + 1) * 128],
                                   op0=OP.mult, op1=OP.add)
        V.tensor_scalar(vA[:, 0:T], vB[:, :], vsd[:, cg:cg + 1], None, op0=OP.mult)
        V.scalar_tensor_tensor(vC[:, :], Hc[:, :], vsc_[:, cg:cg + 1], vA[:, 0:T],
                               op0=OP.mult, op1=OP.add)
        S.wait("V", P.now("V"))
        S.activation(ys16[:, :], vC[:, :], AF.Silu, bias=0.0, scale=1.0)
        V.wait("S", P.now("S"))
        V.wait("st", 16 * cg)
        V.tensor_tensor(g16[:, :], u16[:, :], ys16[:, :], op=OP.mult)
        v_after_gate = P.now("V")
        SY.wait("V", P.now("V"))
        SY.dma_start(g_out[cg * 128:(cg + 1) * 128, :], g16[:, :], inc="st", k=16)
    SY.wait("st", P.now("st"))



def build_l0a():
    P = Prog()
    A = {}
    A["xT"] = P.din("xT", [D, T])
    A["w_uv"] = P.din("w_uv", [D, 1024])
    A["b_uv"] = P.din("b_uv", [128, 8])
    A["conv_w3"] = P.din("conv_w3", [128, 4, 3])
    A["conv_b4"] = P.din("conv_b4", [128, 4])
    A["logit_a4"] = P.din("logit_a4", [128, 4])
    A["ssm_b4"] = P.din("ssm_b4", [128, 4])
    A["ssm_c4"] = P.din("ssm_c4", [128, 4])
    A["ssm_d4"] = P.din("ssm_d4", [128, 4])
    A["g_out"] = P.dout("g_out", [512, T], BF16)
    emit_l0a(P, A)
    return P

def prep_l0a_inputs(ins, xT_host):
    W = ins["L0_attn_in_proj_w"]
    bias = ins["L0_attn_in_proj_b"]
    cw = ins["L0_attn_conv_w"]

    def vecT(v, ncol):
        return np.ascontiguousarray(v.reshape(ncol, 128).T)

    maps = []
    for c in range(NCORE):
        b, j = c // 2, c % 2
        cols_u = slice(j * 512, (j + 1) * 512)
        cols_v = slice(1024 + j * 512, 1024 + (j + 1) * 512)
        w_uv = np.ascontiguousarray(np.concatenate([W[:, cols_u], W[:, cols_v]], axis=1))
        b_uv = vecT(np.concatenate([bias[cols_u], bias[cols_v]]), 8)
        ch = slice(j * 512, (j + 1) * 512)
        conv_w3 = np.ascontiguousarray(cw[ch, 0, :].reshape(4, 128, 3).transpose(1, 0, 2))
        maps.append(dict(
            xT=np.ascontiguousarray(xT_host[b]),
            w_uv=w_uv, b_uv=b_uv, conv_w3=conv_w3,
            conv_b4=vecT(ins["L0_attn_conv_b"][ch], 4),
            logit_a4=vecT(ins["L0_attn_logit_a"][ch], 4),
            ssm_b4=vecT(ins["L0_attn_ssm_b"][ch], 4),
            ssm_c4=vecT(ins["L0_attn_ssm_c"][ch], 4),
            ssm_d4=vecT(ins["L0_attn_ssm_d"][ch], 4),
        ))
    return maps



def emit_l0b1(P, A):
    xT_sl = A["xT_sl"]
    g16T = A["g16T"]
    opw = A["opw"]
    opb8 = A["opb8"]
    lnw8 = A["lnw8"]
    lnb8 = A["lnb8"]
    x1T = A["x1T"]
    x1lnT = A["x1lnT"]

    V, S, Tn, G, SY = P.V, P.S, P.T, P.G, P.SY
    NT = TL // 512

    h16 = P.sb("h16", [128, DC, TL], BF16)
    xln16 = P.sb("xln16", [128, DC, TL], BF16)
    sc = make_ln_scratch(P, TL)
    w2sb = P.sb("w2sb", [128, DC, D], BF16)
    vopb = P.sb("vopb", [128, 8])
    vlnw = P.sb("vlnw", [128, 8])
    vlnb = P.sb("vlnb", [128, 8])
    x16ch = [P.sb("x16ch0", [128, DC, 512], BF16), P.sb("x16ch1", [128, DC, 512], BF16)]
    tmpfD = P.sb("tmpfD", [128, 512], F32)
    pool = MMPool(P, n=4)
    gbuf = [P.sb("gbuf0", [128, DC, 512], BF16, freeable=True),
            P.sb("gbuf1", [128, DC, 512], BF16, freeable=True)]

    for kc in range(DC):
        G.dma_start(w2sb[:, kc, :], opw[kc * 128:(kc + 1) * 128, :], inc="ld", k=16)
    for dst, src_ in [(vopb, opb8), (vlnw, lnw8), (vlnb, lnb8)]:
        SY.dma_start(dst[:, :], src_, inc="ld2", k=16)
    emit_ln_init(P, sc)

    use_done = {"x16ch0": None, "x16ch1": None}

    def stream16(dram, tc, consumers=None):
        par = tc % 2
        name = f"x16ch{par}"
        if use_done[name] is not None:
            G.wait(*use_done[name])
        G.dma_start(x16ch[par][:, :, :],
                    dram[:, tc * 512:(tc + 1) * 512].rearrange("(c p) w -> p c w", p=128),
                    inc=f"ld16{par}", k=16)
        return par, (f"ld16{par}", P.now(f"ld16{par}"))

    for tc in range(NT):
        par, wt = stream16(xT_sl, tc)
        emit_ln_reduce_chunk(P, sc, x16ch[par], tc, 512, wait=wt)
        use_done[f"x16ch{par}"] = ("Tq", P.now("Tq"))
    emit_ln_stats(P, sc, TL)
    for tc in range(NT):
        par, wt = stream16(xT_sl, tc)
        emit_ln_bcast_chunk(P, sc, tc, 512)
        V.wait(*wt)
        emit_ln_apply_chunk(P, sc, x16ch[par], xln16[:, :, tc * 512:(tc + 1) * 512], 512)
        use_done[f"x16ch{par}"] = ("V", P.now("V"))
    emit_barrier(P, dma_sems=("ld", "ld2"))

    hw16 = sc["tmp16"]
    gdone = [0, 0]
    for tc in range(NT):
        par = tc % 2
        if gdone[par]:
            SY.wait("Tq", gdone[par])
        SY.dma_start(gbuf[par][:, :, :],
                     g16T[:, tc * 512:(tc + 1) * 512].rearrange("(c p) w -> p c w", p=128),
                     inc=f"ldg{par}", k=16)
        Tn.wait(f"ldg{par}", P.now(f"ldg{par}"))
        for oc in range(DC):
            p = pool.begin()
            for kc in range(DC):
                Tn.matmul(p[:, :], w2sb[:, kc, oc * 128:(oc + 1) * 128],
                          gbuf[par][:, kc, :],
                          start=(kc == 0), stop=(kc == DC - 1))
            S.wait("Tq", P.now("Tq"))
            S.wait("V", P.now("V"))
            S.activation(hw16[:, :], p[:, :], AF.Identity, bias=vopb[:, oc:oc + 1], scale=1.0)
            V.wait("S", P.now("S"))
            V.tensor_tensor(h16[:, oc, tc * 512:(tc + 1) * 512], hw16[:, :],
                            xln16[:, oc, tc * 512:(tc + 1) * 512], op=OP.add)
            pool.mark("V")
        gdone[par] = P.now("Tq")
    emit_barrier(P)
    P.free("gbuf1"); P.free("gbuf0")
    xbuf = [P.sb("xbuf0", [128, DC, 512], F32, freeable=True),
            P.sb("xbuf1", [128, DC, 512], F32, freeable=True)]
    x1c = [P.sb("x1c0", [128, DC, 512], F32, freeable=True),
           P.sb("x1c1", [128, DC, 512], F32, freeable=True)]

    for tc in range(NT):
        emit_ln_reduce_chunk(P, sc, h16[:, :, tc * 512:(tc + 1) * 512], tc, 512,
                             wait=("V", P.now("V")))
    emit_ln_stats(P, sc, TL)
    xd_done = [0, 0]
    st_done = [0, 0]
    for tc in range(NT):
        par = tc % 2
        if xd_done[par]:
            SY.wait("V", xd_done[par])
        SY.dma_start(xbuf[par][:, :, :],
                     xT_sl[:, tc * 512:(tc + 1) * 512].rearrange("(c p) w -> p c w", p=128),
                     inc=f"ldD{par}", k=16)
        emit_ln_bcast_chunk(P, sc, tc, 512)
        V.wait(f"ldD{par}", P.now(f"ldD{par}"))
        if st_done[par]:
            V.wait(f"stx{par}", st_done[par])
        for kc in range(DC):
            V.tensor_tensor(sc["tmp16"][:, :], h16[:, kc, tc * 512:(tc + 1) * 512],
                            sc["c1b"][:, :], op=OP.mult)
            V.tensor_tensor(sc["tmp16"][:, :], sc["tmp16"][:, :], sc["c0b"][:, :], op=OP.add)
            S.wait("V", P.now("V"))
            S.activation(tmpfD[:, :], sc["tmp16"][:, :], AF.Identity,
                         bias=vlnb[:, kc:kc + 1], scale=vlnw[:, kc:kc + 1])
            V.wait("S", P.now("S"))
            V.tensor_tensor(x1c[par][:, kc, :], tmpfD[:, :], xbuf[par][:, kc, :], op=OP.add)
        xd_done[par] = P.now("V")
        SY.wait("V", P.now("V"))
        SY.dma_start(x1T[:, tc * 512:(tc + 1) * 512].rearrange("(c p) w -> p c w", p=128),
                     x1c[par][:, :, :], inc=f"stx{par}", k=16)
        st_done[par] = P.now(f"stx{par}")
    emit_barrier(P, dma_sems=("stx0", "stx1"))
    P.free("x1c1"); P.free("x1c0"); P.free("xbuf1"); P.free("xbuf0")
    xlnc = [P.sb("xlnc0", [128, DC, 512], BF16, freeable=True),
            P.sb("xlnc1", [128, DC, 512], BF16, freeable=True)]

    use_done["x16ch0"] = ("V", P.now("V")); use_done["x16ch1"] = ("V", P.now("V"))
    for tc in range(NT):
        par, wt = stream16(x1T, tc)
        emit_ln_reduce_chunk(P, sc, x16ch[par], tc, 512, wait=wt)
        use_done[f"x16ch{par}"] = ("Tq", P.now("Tq"))
    emit_ln_stats(P, sc, TL)
    stl_done = [0, 0]
    for tc in range(NT):
        par, wt = stream16(x1T, tc)
        emit_ln_bcast_chunk(P, sc, tc, 512)
        V.wait(*wt)
        if stl_done[par]:
            V.wait(f"stl{par}", stl_done[par])
        emit_ln_apply_chunk(P, sc, x16ch[par], xlnc[par], 512)
        use_done[f"x16ch{par}"] = ("V", P.now("V"))
        SY.wait("V", P.now("V"))
        SY.dma_start(x1lnT[:, tc * 512:(tc + 1) * 512].rearrange("(c p) w -> p c w", p=128),
                     xlnc[par][:, :, :], inc=f"stl{par}", k=16)
        stl_done[par] = P.now(f"stl{par}")
    SY.wait("stl0", P.now("stl0"))
    SY.wait("stl1", P.now("stl1"))



def build_l0b1():
    P = Prog()
    A = {}
    A["xT_sl"] = P.din("xT_sl", [D, TL])
    A["g16T"] = P.din("g16T", [D, TL], BF16)
    A["opw"] = P.din("opw", [D, D])
    A["opb8"] = P.din("opb8", [128, 8])
    A["lnw8"] = P.din("lnw8", [128, 8])
    A["lnb8"] = P.din("lnb8", [128, 8])
    A["x1T"] = P.dout("x1T", [D, TL])
    A["x1lnT"] = P.dout("x1lnT", [D, TL], BF16)
    emit_l0b1(P, A)
    return P

def prep_l0b1_inputs(ins, xT_host, g_all):
    def vecT(v, ncol):
        return np.ascontiguousarray(v.reshape(ncol, 128).T)
    maps = []
    for c in range(NCORE):
        b, half = c // 2, c % 2
        t0 = half * TL
        m = dict(
            xT_sl=np.ascontiguousarray(xT_host[b][:, t0:t0 + TL]),
            opw=ins["L0_attn_out_proj_w"],
            opb8=vecT(ins["L0_attn_out_proj_b"], 8),
            lnw8=vecT(ins["L0_attn_ln_w"], 8),
            lnb8=vecT(ins["L0_attn_ln_b"], 8),
        )
        if g_all[b] is not None:
            m["g16T"] = np.ascontiguousarray(g_all[b][:, t0:t0 + TL])
        maps.append(m)
    return maps




def emit_softmax_top2(P, lgT, nch, E, iotaE, scT, mkT, dispT, mselT, stat, spacer, need_msel=True):
    V, S = P.V, P.S
    m1, sums, rr, m1s, m2s, es = (stat["m1"], stat["sums"], stat["rr"],
                                  stat["m1s"], stat["m2s"], stat["es"])

    def space(n=2):
        for _ in range(n):
            V.tensor_copy(spacer[:, :], spacer[:, :])
        V.fence()

    for c in range(nch):
        V.tensor_reduce(m1[:, c:c + 1], lgT[:, c, :], axis=AX.X, op=OP.max, negate=True)
    space()
    S.wait("V", P.now("V"))
    for c in range(nch):
        S.activation(scT[:, c, :], lgT[:, c, :], AF.Exp, bias=m1[:, c:c + 1], scale=1.0)
    V.wait("S", P.now("S"))
    for c in range(nch):
        V.tensor_reduce(sums[:, c:c + 1], scT[:, c, :], axis=AX.X, op=OP.add)
    for c in range(nch):
        V.reciprocal(rr[:, c:c + 1], sums[:, c:c + 1])
    space()
    S.wait("V", P.now("V"))
    for c in range(nch):
        S.activation(scT[:, c, :], scT[:, c, :], AF.Identity, bias=0.0, scale=rr[:, c:c + 1])
    V.wait("S", P.now("S"))
    for c in range(nch):
        V.tensor_reduce(m1s[:, c:c + 1], scT[:, c, :], axis=AX.X, op=OP.max)
    space()
    for c in range(nch):
        V.tensor_scalar(mkT[:, c, :], scT[:, c, :], m1s[:, c:c + 1], None, op0=OP.is_lt)
    for c in range(nch):
        V.tensor_tensor(mkT[:, c, :], scT[:, c, :], mkT[:, c, :], op=OP.mult)
    for c in range(nch):
        V.tensor_reduce(m2s[:, c:c + 1], mkT[:, c, :], axis=AX.X, op=OP.max)
    space()
    for c in range(nch):
        V.tensor_scalar(mkT[:, c, :], scT[:, c, :], m2s[:, c:c + 1], None, op0=OP.is_ge)
    for c in range(nch):
        V.tensor_tensor(dispT[:, c, :], scT[:, c, :], mkT[:, c, :], op=OP.mult)
    if need_msel:
        for c in range(nch):
            V.tensor_scalar(mkT[:, c, :], dispT[:, c, :], 0.0, None, op0=OP.is_gt)
        for c in range(nch):
            V.tensor_tensor(mkT[:, c, :], iotaE[:, 0:E], mkT[:, c, :], op=OP.mult)
        for c in range(nch):
            V.tensor_reduce(es[:, c:c + 1], mkT[:, c, :], axis=AX.X, op=OP.max)
        space()
        for c in range(nch):
            V.tensor_scalar(mselT[:, c, :], iotaE[:, 0:E], es[:, c:c + 1], None, op0=OP.is_equal)



def emit_l0b2(P, A):
    TLh = TL // 2
    x1T = A["x1T"]
    x1lnT = A["x1lnT"]
    gate_wT = A["gate_wT"]
    gate_b8 = A["gate_b8"]
    w1t = A["w1t"]
    b1T = A["b1T"]
    w2t = A["w2t"]
    b2T = A["b2T"]
    oh8 = A["oh8"]
    eye128 = A["eye128"]
    x2T = A["x2T"]

    V, S, Tn, G, SY = P.V, P.S, P.T, P.G, P.SY
    NT = TL // 512

    acc16 = P.sb("acc16", [128, DC, TL], BF16)
    xlnc = [P.sb("xlnc0", [128, DC, 512], BF16), P.sb("xlnc1", [128, DC, 512], BF16)]
    wdispb = P.sb("wdispb", [128, TLh], BF16)
    mselbs = P.sb("mselbs", [128, TLh])
    disp8 = P.sb("disp8", [8, TL], BF16)
    msel8 = P.sb("msel8", [8, TL], BF16)
    gw16 = P.sb("gw16", [128, DC, 8], BF16)
    eye = P.sb("eye", [128, 128])
    oh16 = P.sb("oh16", [8, 8, 128], BF16)
    vgb = P.sb("vgb", [8, 1])
    b1sb = P.sb("b1sb", [128, 32])
    b2sb = P.sb("b2sb", [128, 8])
    tmpy = P.sb("tmpy", [128, 512])
    tmpy2 = P.sb("tmpy2", [128, 512])
    iota8 = P.sb("iota8", [128, 8])
    iota8i = P.sb("iota8i", [128, 8], I32)
    sc8 = P.sb("sc8", [128, 8])
    m8 = P.sb("m8", [128, 8])
    r1 = P.sb("r1", [128, 1]); r2 = P.sb("r2", [128, 1]); r3 = P.sb("r3", [128, 1])
    pool = MMPool(P, n=4)

    G.dma_start(gw16[:, :, :], gate_wT.rearrange("(c p) e -> p c e", p=128), inc="ld", k=16)
    G.dma_start(oh16[:, :, :], oh8, inc="ld", k=16)
    for dst, src_ in [(vgb, gate_b8), (eye, eye128)]:
        SY.dma_start(dst[:, :], src_, inc="ld2", k=16)
    for kc in range(DC):
        V.memset(acc16[:, kc, :], 0.0)
    G.iota(iota8i[:, :], [[1, 8]], base=0, channel_multiplier=0)
    G.tensor_copy(iota8[:, :], iota8i[:, :])

    lgT = P.sb("lgT", [128, 16, 8], freeable=True)
    dispT = P.sb("dispT", [128, 16, 8], freeable=True)
    scT_dbg = P.sb("scT_dbg", [128, 16, 8], freeable=True)
    m1_dbg_big = P.sb("m1_dbg_big", [128, 16, 8], freeable=True)
    mselT = P.sb("mselT", [128, 16, 8], freeable=True)
    lg = P.sb("lg", [8, TL], F32, freeable=True)
    pg = P.ps("pg", [8, 512], F32, freeable=True)
    ptT = P.ps("ptT", [128, 128], F32, freeable=True)
    free_pg_later = True
    Tn.wait("ld", P.now("ld"))
    S.wait("ld2", P.now("ld2"))
    xl_done = [None, None]
    for tc in range(NT):
        par = tc % 2
        if xl_done[par] is not None:
            SY.wait(*xl_done[par])
        SY.dma_start(xlnc[par][:, :, :],
                     x1lnT[:, tc * 512:(tc + 1) * 512].rearrange("(c p) w -> p c w", p=128),
                     inc=f"lnx{par}", k=16)
        Tn.wait(f"lnx{par}", P.now(f"lnx{par}"))
        Tn.wait("S", P.now("S"))
        for kc in range(DC):
            Tn.matmul(pg[:, :], gw16[:, kc, :], xlnc[par][:, kc, :],
                      start=(kc == 0), stop=(kc == DC - 1))
        xl_done[par] = ("Tq", P.now("Tq"))
        S.wait("Tq", P.now("Tq"))
        S.activation(lg[0:8, tc * 512:(tc + 1) * 512], pg[0:8, :], AF.Identity,
                     bias=vgb[0:8, 0:1], scale=1.0)
    for half in range(2):
        Tn.wait("S", P.now("S"))
        Tn.wait("V", P.now("V"))
        for cx in range(8):
            c = half * 8 + cx
            Tn.transpose(ptT[:, cx * 8:(cx + 1) * 8], lg[0:8, c * 128:(c + 1) * 128],
                         eye[0:8, 0:8])
        V.wait("Tq", P.now("Tq"))
        for cx in range(8):
            V.tensor_copy(lgT[:, half * 8 + cx, :], ptT[:, cx * 8:(cx + 1) * 8])
    G.wait("G", P.now("G"))
    V.wait("G", P.now("G"))
    stat = {k: P.sb("gstat_" + k, [128, 16], freeable=True)
            for k in ("m1", "sums", "rr", "m1s", "m2s", "es")}
    V.memset(tmpy[:, :], 0.0)
    emit_softmax_top2(P, lgT, 16, 8, iota8, scT_dbg, m1_dbg_big, dispT, mselT, stat, tmpy)
    for k in reversed(list(stat)):
        P.free("gstat_" + k)
    emit_barrier(P)
    P.free("ptT"); P.free("pg")
    pb8 = P.ps("pb8", [8, 2048], F32, freeable=True)
    for what, dst in ((dispT, disp8), (mselT, msel8)):
        Tn.wait("V", P.now("V"))
        for c in range(16):
            Tn.transpose(pb8[0:8, c * 128:(c + 1) * 128], what[:, c, :], eye[:, :])
        V.wait("Tq", P.now("Tq"))
        V.tensor_copy(dst[0:8, :], pb8[0:8, :])
    emit_barrier(P)
    P.free("pb8")
    P.free("lg")
    P.free("mselT")
    P.free("m1_dbg_big"); P.free("scT_dbg")
    P.free("dispT"); P.free("lgT")
    ypool = MMPool(P, n=4, name="ypool")
    h4k = P.sb("h4k", [128, 32, TLh], BF16, freeable=True)
    wbuf = P.sb("wbuf", [128, 16384], BF16, freeable=True)
    xw16 = P.sb("xw16", [128, DC, TLh], BF16, freeable=True)

    w1done = [None, None]
    w2done = None
    bdone = None
    for e in range(8):
        if bdone is not None:
            SY.wait("S", bdone)
        SY.dma_start(b1sb[:, :], b1T[e, :, :], inc="ldb", k=16)
        SY.dma_start(b2sb[:, :], b2T[e, :, :], inc="ldb", k=16)
        for st in range(2):
            t0 = st * TLh
            for q in range(2):
                p = pool.begin()
                Tn.wait("V", P.now("V"))
                Tn.matmul(p[:, :], oh16[:, e, :], disp8[0:8, t0 + q * 512:t0 + (q + 1) * 512],
                          start=True, stop=True)
                V.wait("Tq", P.now("Tq"))
                V.tensor_copy(wdispb[:, q * 512:(q + 1) * 512], p[:, :])
                pool.mark("V")
                p = pool.begin()
                Tn.matmul(p[:, :], oh16[:, e, :], msel8[0:8, t0 + q * 512:t0 + (q + 1) * 512],
                          start=True, stop=True)
                V.wait("Tq", P.now("Tq"))
                V.tensor_copy(mselbs[:, q * 512:(q + 1) * 512], p[:, :])
                pool.mark("V")
            V.wait("Tq", P.now("Tq"))
            for q in range(2):
                par = q % 2
                if xl_done[par] is not None:
                    SY.wait(*xl_done[par])
                SY.dma_start(xlnc[par][:, :, :],
                             x1lnT[:, t0 + q * 512:t0 + (q + 1) * 512].rearrange("(c p) w -> p c w", p=128),
                             inc=f"lnx{par}", k=16)
                V.wait(f"lnx{par}", P.now(f"lnx{par}"))
                for kc in range(DC):
                    V.tensor_tensor(xw16[:, kc, q * 512:(q + 1) * 512],
                                    xlnc[par][:, kc, :], wdispb[:, q * 512:(q + 1) * 512],
                                    op=OP.mult)
                xl_done[par] = ("V", P.now("V"))
            G.wait("Tq", P.now("Tq"))
            for hq in range(4):
                par = hq % 2
                if w1done[par] is not None:
                    G.wait(*w1done[par])
                w1v = wbuf[:, par * 8192:(par + 1) * 8192].rearrange("p (k w) -> p k w", k=DC)
                for kc in range(DC):
                    G.dma_start(w1v[:, kc, :],
                                w1t[e, kc * 128:(kc + 1) * 128, hq * 1024:(hq + 1) * 1024],
                                inc=f"ldw{par}", k=16)
                Tn.wait(f"ldw{par}", P.now(f"ldw{par}"))
                Tn.wait("V", P.now("V"))
                for hoc in range(8):
                    for q in range(2):
                        p = pool.begin()
                        for kc in range(DC):
                            Tn.matmul(p[:, :], w1v[:, kc, hoc * 128:(hoc + 1) * 128],
                                      xw16[:, kc, q * 512:(q + 1) * 512],
                                      start=(kc == 0), stop=(kc == DC - 1))
                        S.wait("Tq", P.now("Tq"))
                        S.wait("ldb", P.now("ldb"))
                        S.activation(h4k[:, hq * 8 + hoc, q * 512:(q + 1) * 512], p[:, :],
                                     AF.Relu, bias=b1sb[:, hq * 8 + hoc:hq * 8 + hoc + 1],
                                     scale=1.0)
                        pool.mark("S")
                w1done[par] = ("Tq", P.now("Tq"))
            S_h_done = P.now("S")
            for oh_ in range(2):
                G.wait("Tq", P.now("Tq"))
                w2v = wbuf.rearrange("p (k w) -> p k w", k=32)
                for kc2 in range(32):
                    G.dma_start(w2v[:, kc2, :],
                                w2t[e, kc2 * 128:(kc2 + 1) * 128, oh_ * 512:(oh_ + 1) * 512],
                                inc="ldw2", k=16)
                Tn.wait("ldw2", P.now("ldw2"))
                Tn.wait("S", S_h_done)
                for tg in range(2):
                    ps4 = [ypool.begin() for _ in range(4)]
                    for o4 in range(4):
                        for kc2 in range(32):
                            Tn.matmul(ps4[o4][:, :], w2v[:, kc2, o4 * 128:(o4 + 1) * 128],
                                      h4k[:, kc2, tg * 512:(tg + 1) * 512],
                                      start=(kc2 == 0), stop=(kc2 == 31))
                    for o4 in range(4):
                        oc2 = oh_ * 4 + o4
                        S.wait("Tq", P.now("Tq"))
                        S.wait("V", P.now("V"))
                        S.activation(tmpy[:, :], ps4[o4][:, :], AF.Identity,
                                     bias=b2sb[:, oc2:oc2 + 1], scale=1.0)
                        V.wait("S", P.now("S"))
                        V.tensor_tensor(tmpy2[:, :], tmpy[:, :],
                                        mselbs[:, tg * 512:(tg + 1) * 512], op=OP.mult)
                        V.tensor_tensor(acc16[:, oc2, t0 + tg * 512:t0 + (tg + 1) * 512],
                                        tmpy2[:, :],
                                        acc16[:, oc2, t0 + tg * 512:t0 + (tg + 1) * 512],
                                        op=OP.add)
                        ypool.mark("V")
                w2done = ("Tq", P.now("Tq"))
        bdone = P.now("S")
    emit_barrier(P)
    P.free("xw16"); P.free("wbuf"); P.free("h4k")
    x1b = [P.sb("x1b0", [128, DC, 512], F32, freeable=True),
           P.sb("x1b1", [128, DC, 512], F32, freeable=True)]
    x2b = [P.sb("x2b0", [128, DC, 512], F32, freeable=True),
           P.sb("x2b1", [128, DC, 512], F32, freeable=True)]
    xs_done = [None, None]
    ss_done = [None, None]
    for tc in range(NT):
        par = tc % 2
        if xs_done[par] is not None:
            SY.wait(*xs_done[par])
        SY.dma_start(x1b[par][:, :, :],
                     x1T[:, tc * 512:(tc + 1) * 512].rearrange("(c p) w -> p c w", p=128),
                     inc=f"lx{par}", k=16)
        V.wait(f"lx{par}", P.now(f"lx{par}"))
        if ss_done[par] is not None:
            V.wait(*ss_done[par])
        for kc in range(DC):
            V.tensor_tensor(x2b[par][:, kc, :], x1b[par][:, kc, :],
                            acc16[:, kc, tc * 512:(tc + 1) * 512], op=OP.add)
        xs_done[par] = ("V", P.now("V"))
        SY.wait("V", P.now("V"))
        SY.dma_start(x2T[:, tc * 512:(tc + 1) * 512].rearrange("(c p) w -> p c w", p=128),
                     x2b[par][:, :, :], inc=f"st{par}", k=16)
        ss_done[par] = (f"st{par}", P.now(f"st{par}"))
    SY.wait("st0", P.now("st0"))
    SY.wait("st1", P.now("st1"))



def build_l0b2():
    P = Prog()
    A = {}
    A["x1T"] = P.din("x1T", [D, TL])
    A["x1lnT"] = P.din("x1lnT", [D, TL], BF16)
    A["gate_wT"] = P.din("gate_wT", [D, 8])
    A["gate_b8"] = P.din("gate_b8", [8, 1])
    A["w1t"] = P.din("w1t", [8, D, 4096])
    A["b1T"] = P.din("b1T", [8, 128, 32])
    A["w2t"] = P.din("w2t", [8, 4096, D])
    A["b2T"] = P.din("b2T", [8, 128, 8])
    A["oh8"] = P.din("oh8", [8, 8, 128])
    A["eye128"] = P.din("eye128", [128, 128])
    A["x2T"] = P.dout("x2T", [D, TL])
    emit_l0b2(P, A)
    return P

def prep_l0b2_inputs(ins, x1T_all, x1lnT_all):
    maps = []
    oh8 = np.zeros((8, 8, 128), np.float32)
    for e in range(8):
        oh8[e, e, :] = 1.0
    eye128 = np.eye(128, dtype=np.float32)
    gate_wT = np.ascontiguousarray(ins["L0_ffn_gate_w"].T)
    gate_b8 = np.ascontiguousarray(ins["L0_ffn_gate_b"].reshape(8, 1))
    w1t = np.ascontiguousarray(ins["L0_ffn_w1"].transpose(0, 2, 1))
    w2t = np.ascontiguousarray(ins["L0_ffn_w2"].transpose(0, 2, 1))
    b1T = np.ascontiguousarray(ins["L0_ffn_b1"].reshape(8, 32, 128).transpose(0, 2, 1))
    b2T = np.ascontiguousarray(ins["L0_ffn_b2"].reshape(8, 8, 128).transpose(0, 2, 1))
    for c in range(NCORE):
        m = dict(
            gate_wT=gate_wT, gate_b8=gate_b8, w1t=w1t, b1T=b1T, w2t=w2t, b2T=b2T,
            oh8=oh8, eye128=eye128,
        )
        if x1T_all[c] is not None:
            m["x1T"] = x1T_all[c]; m["x1lnT"] = x1lnT_all[c]
        maps.append(m)
    return maps



def emit_l1a(P, A):
    x2T = A["x2T"]
    wqT = A["wqT"]
    wkT = A["wkT"]
    wvT = A["wvT"]
    bq8 = A["bq8"]
    bk8 = A["bk8"]
    bv8 = A["bv8"]
    wprojd = A["wprojd"]
    v16out = A["v16out"]
    zq_out = A["zq_out"]
    zk_out = A["zk_out"]

    V, S, Tn, G, SY = P.V, P.S, P.T, P.G, P.SY
    NT = TL // 512

    xln16 = P.sb("xln16", [128, DC, TL], BF16)
    v16 = P.sb("v16", [128, DC, TL], BF16)
    wq16 = P.sb("wq16", [128, DC, D], BF16)
    wk16 = P.sb("wk16", [128, DC, D], BF16)
    wv16 = P.sb("wv16", [128, DC, D], BF16)
    wp16 = P.sb("wp16", [128, 16, H], BF16)
    vbq = P.sb("vbq", [128, 8]); vbk = P.sb("vbk", [128, 8]); vbv = P.sb("vbv", [128, 8])
    sc = make_ln_scratch(P, TL)
    x16ch = [P.sb("x16ch0", [128, DC, 512], BF16), P.sb("x16ch1", [128, DC, 512], BF16)]
    qc16 = [P.sb("qc16_0", [128, TL], BF16), P.sb("qc16_1", [128, TL], BF16)]
    zq_sb = P.sb("zq_sb", [16, H, 128])
    zk_sb = P.sb("zk_sb", [16, H, 128])
    pool = MMPool(P, n=3)
    pz0 = P.ps("pz0", [16, 128], F32)
    pz = [pz0, pz0]

    for name, w, dst in (("q", wqT, wq16), ("k", wkT, wk16), ("v", wvT, wv16)):
        for kc in range(DC):
            G.dma_start(dst[:, kc, :], w[kc * 128:(kc + 1) * 128, :], inc="ld", k=16)
    G.dma_start(wp16[:, :, :], wprojd.rearrange("j p h -> p j h"), inc="ld", k=16)
    for dst, src_ in ((vbq, bq8), (vbk, bk8), (vbv, bv8)):
        SY.dma_start(dst[:, :], src_, inc="ld2", k=16)

    emit_ln_init(P, sc)
    use_done = {"x16ch0": None, "x16ch1": None}

    def stream16(dram, tc):
        par = tc % 2
        name = f"x16ch{par}"
        if use_done[name] is not None:
            G.wait(*use_done[name])
        G.dma_start(x16ch[par][:, :, :],
                    dram[:, tc * 512:(tc + 1) * 512].rearrange("(c p) w -> p c w", p=128),
                    inc=f"ld16{par}", k=16)
        return par, (f"ld16{par}", P.now(f"ld16{par}"))

    for tc in range(NT):
        par, wt = stream16(x2T, tc)
        emit_ln_reduce_chunk(P, sc, x16ch[par], tc, 512, wait=wt)
        use_done[f"x16ch{par}"] = ("Tq", P.now("Tq"))
    emit_ln_stats(P, sc, TL)
    for tc in range(NT):
        par, wt = stream16(x2T, tc)
        emit_ln_bcast_chunk(P, sc, tc, 512)
        V.wait(*wt)
        emit_ln_apply_chunk(P, sc, x16ch[par], xln16[:, :, tc * 512:(tc + 1) * 512], 512)
        use_done[f"x16ch{par}"] = ("V", P.now("V"))
    emit_barrier(P, dma_sems=("ld", "ld2"))

    zmark = 0
    qdone = [None, None]
    for which, wsb, bias, zsb in (("v", wv16, vbv, None), ("q", wq16, vbq, zq_sb), ("k", wk16, vbk, zk_sb)):
        for oc in range(DC):
            par = oc % 2
            for tc in range(NT):
                p = pool.begin()
                for kc in range(DC):
                    Tn.matmul(p[:, :], wsb[:, kc, oc * 128:(oc + 1) * 128],
                              xln16[:, kc, tc * 512:(tc + 1) * 512],
                              start=(kc == 0), stop=(kc == DC - 1))
                S.wait("Tq", P.now("Tq"))
                if zsb is None:
                    S.activation(v16[:, oc, tc * 512:(tc + 1) * 512], p[:, :], AF.Identity,
                                 bias=bias[:, oc:oc + 1], scale=1.0)
                else:
                    if qdone[par] is not None:
                        S.wait(*qdone[par])
                    S.activation(qc16[par][:, tc * 512:(tc + 1) * 512], p[:, :], AF.Identity,
                                 bias=bias[:, oc:oc + 1], scale=1.0)
                pool.mark("S")
            if zsb is not None:
                Tn.wait("S", P.now("S"))
                for gg in range(2):
                    g = 2 * oc + gg
                    pzt = pz[gg]
                    Tn.wait("V", P.now("V"))
                    for j in range(16):
                        Tn.matmul(pzt[:, :], wp16[gg * 64:gg * 64 + 64, j, :],
                                  qc16[par][gg * 64:gg * 64 + 64, j::16],
                                  start=(j == 0), stop=(j == 15))
                    V.wait("Tq", P.now("Tq"))
                    V.tensor_copy(zsb[:, g, :], pzt[0:16, :])
                zmark = P.now("Tq")
                qdone[par] = ("V", P.now("V"))
        if which == "v":
            SY.wait("S", P.now("S"))
            SY.dma_start(v16out.rearrange("(c p) w -> p c w", p=128), v16[:, :, :],
                         inc="st", k=16)
            if A.get("after_v") is not None:
                A["after_v"]()
    SY.wait("S", P.now("S"))
    SY.wait("V", P.now("V"))
    SY.dma_start(zq_out.rearrange("h (g r) -> h g r", r=128), zq_sb[0:16, :, :], inc="st", k=16)
    SY.dma_start(zk_out.rearrange("h (g r) -> h g r", r=128), zk_sb[0:16, :, :], inc="st", k=16)
    SY.wait("st", P.now("st"))



def build_l1a():
    P = Prog()
    A = {}
    A["x2T"] = P.din("x2T", [D, TL])
    A["wqT"] = P.din("wqT", [D, D])
    A["wkT"] = P.din("wkT", [D, D])
    A["wvT"] = P.din("wvT", [D, D])
    A["bq8"] = P.din("bq8", [128, 8])
    A["bk8"] = P.din("bk8", [128, 8])
    A["bv8"] = P.din("bv8", [128, 8])
    A["wprojd"] = P.din("wprojd", [16, 128, H])
    A["v16out"] = P.dout("v16out", [D, TL], BF16)
    A["zq_out"] = P.dout("zq_out", [H, TL])
    A["zk_out"] = P.dout("zk_out", [H, TL])
    emit_l1a(P, A)
    return P

def prep_l1a_inputs(ins, x2T_all):
    def vecT(v, ncol):
        return np.ascontiguousarray(v.reshape(ncol, 128).T)
    wqT = np.ascontiguousarray(ins["L1_attn_Wq"].T)
    wkT = np.ascontiguousarray(ins["L1_attn_Wk"].T)
    wvT = np.ascontiguousarray(ins["L1_attn_Wv"].T)
    wpT = ins["L1_attn_Wproj"].T.reshape(16, 64, 16)
    wprojd = np.ascontiguousarray(np.concatenate([wpT, wpT], axis=1))
    maps = []
    for c in range(NCORE):
        m = dict(
            wqT=wqT, wkT=wkT, wvT=wvT,
            bq8=vecT(ins["L1_attn_bq"], 8), bk8=vecT(ins["L1_attn_bk"], 8),
            bv8=vecT(ins["L1_attn_bv"], 8), wprojd=wprojd,
        )
        if x2T_all[c] is not None:
            m["x2T"] = x2T_all[c]
        maps.append(m)
    return maps



def emit_bump(P, A):
    HL = 8
    v_slice = A["v_slice"]
    zq_loc = A["zq_loc"]
    zk_loc = A["zk_loc"]
    log_bw8 = A["log_bw8"]
    oh8x = A["oh8x"]
    eye16 = A["eye16"]
    ctx_out0 = A["ctx_out0"]
    ctx_out1 = A["ctx_out1"]

    V, S, Tn, G, SY = P.V, P.S, P.T, P.G, P.SY

    zq = P.sb("zq", [HL, T])
    zk = P.sb("zk", [HL, T])
    lb = P.sb("lb", [1, HL])
    ohf = P.sb("ohf", [8, 8, 128])
    eye = P.sb("eye", [16, 16])
    row1 = P.sb("row1", [1, HL]); row2 = P.sb("row2", [1, HL])
    invb = P.sb("invb", [1, HL])
    invb128 = P.sb("invb128", [128, HL])
    epsb = P.sb("epsb", [1, 1])
    ones1f = P.sb("ones1f", [1, 128])
    ones16b = P.sb("ones16b", [128, 1], BF16)
    nzkT = P.sb("nzkT", [128, 32, HL])
    V2 = P.sb("V2", [128, 32, 128], BF16)
    zqb16 = P.sb("zqb16", [128, T], BF16)
    A16 = [P.sb("A16_0", [128, T], BF16), P.sb("A16_1", [128, T], BF16)]
    B16 = [P.sb("B16_0", [128, T], BF16), P.sb("B16_1", [128, T], BF16)]
    vsum = P.sb("vsum", [64, 1])
    ctxsb = P.sb("ctxsb", [64, T])

    SY.dma_start(zq[:, :], zq_loc, inc="ld", k=16)
    SY.dma_start(zk[:, :], zk_loc, inc="ld", k=16)
    SY.dma_start(lb[:, :], log_bw8, inc="ld", k=16)
    SY.dma_start(ohf[:, :, :], oh8x, inc="ld", k=16)
    SY.dma_start(eye[:, :], eye16, inc="ld", k=16)
    nld = P.now("ld")

    V.memset(epsb[:, :], 1e-4)
    V.memset(ones1f[:, :], 1.0)
    V.memset(ones16b[:, :], 1.0)
    S.wait("ld", nld)
    S.wait("V", P.now("V"))
    S.activation(row1[0:1, :], lb[0:1, :], AF.Abs, bias=0.0, scale=1.0)
    S.activation(row1[0:1, :], row1[0:1, :], AF.Exp, bias=0.0, scale=-1.0)
    S.activation(row1[0:1, :], row1[0:1, :], AF.Ln, bias=1.0, scale=1.0)
    S.activation(row2[0:1, :], lb[0:1, :], AF.Relu, bias=0.0, scale=1.0)
    V.wait("S", P.now("S"))
    V.tensor_tensor(invb[0:1, :], row1[0:1, :], row2[0:1, :], op=OP.add)
    V.tensor_scalar(invb[0:1, :], invb[0:1, :], epsb[0:1, 0:1], None, op0=OP.add)
    V.reciprocal(invb[0:1, :], invb[0:1, :])
    V.fence()
    pbc0 = P.ps("pbc0", [128, 512], F32, freeable=True)
    Tn.wait("V", P.now("V"))
    Tn.matmul(pbc0[:, 0:HL], ones1f[0:1, :], invb[0:1, :], start=True, stop=True)
    V.wait("Tq", P.now("Tq"))
    V.tensor_copy(invb128[:, :], pbc0[:, 0:HL])
    V.fence()
    P.free("pbc0")

    pzt = P.ps("pzt", [128, 16], F32, freeable=True)
    Tn.wait("ld", nld)
    for jc in range(32):
        Tn.wait("V", P.now("V"))
        Tn.transpose(pzt[:, 0:HL], zk[0:HL, jc * 128:(jc + 1) * 128], eye[0:HL, 0:HL])
        V.wait("Tq", P.now("Tq"))
        V.tensor_scalar(nzkT[:, jc, :], pzt[:, 0:HL], -1.0, None, op0=OP.mult)
    emit_barrier(P)
    P.free("pzt")

    vdone = None
    for hh in range(8):
        hp, hq = hh // 2, hh % 2
        if hq == 0:
            if vdone is not None:
                SY.wait(*vdone)
            for jc in range(32):
                SY.dma_start(V2[:, jc, :], v_slice[hp * 128:(hp + 1) * 128, jc * 128:(jc + 1) * 128],
                             transpose=True, inc="ldv", k=16)
        pbc = P.ps(f"pbch{hh}", [128, 512], F32, freeable=True)
        Tn.wait("V", P.now("V"))
        Tn.wait("S", P.now("S"))
        for qn in range(8):
            Tn.matmul(pbc[:, :], ohf[:, hh, :], zq[0:HL, qn * 512:(qn + 1) * 512],
                      start=True, stop=True)
            V.wait("Tq", P.now("Tq"))
            V.tensor_copy(zqb16[:, qn * 512:(qn + 1) * 512], pbc[:, :])
            Tn.wait("V", P.now("V"))
        P.free(f"pbch{hh}")
        pv = P.ps(f"pv{hh}", [64, 1], F32, freeable=True)
        Tn.wait("ldv", P.now("ldv"))
        Tn.wait("V", P.now("V"))
        for jc in range(32):
            Tn.matmul(pv[:, :], V2[:, jc, hq * 64:hq * 64 + 64], ones16b[:, :],
                      start=(jc == 0), stop=(jc == 31))
        V.wait("Tq", P.now("Tq"))
        V.tensor_copy(vsum[:, :], pv[0:64, :])
        V.fence()
        P.free(f"pv{hh}")
        pctx = P.ps(f"pctx{hh}", [64, T], F32, freeable=True)
        bdone = [None, None]
        a16_rd = [None, None]
        zqb_ok = P.now("V")
        for jc in range(32):
            par = jc % 2
            S.wait("V", a16_rd[par] if a16_rd[par] is not None else zqb_ok)
            if bdone[par] is not None:
                S.wait(*bdone[par])
            S.activation(A16[par][:, :], zqb16[:, :], AF.Abs,
                         bias=nzkT[:, jc, hh:hh + 1], scale=1.0)
            V.wait("S", P.now("S"))
            if bdone[par] is not None:
                V.wait(*bdone[par])
            V.tensor_scalar(B16[par][:, :], A16[par][:, :], invb128[:, hh:hh + 1], 1.0,
                            op0=OP.mult, op1=OP.min)
            a16_rd[par] = P.now("V")
            Tn.wait("V", P.now("V"))
            for qn in range(8):
                Tn.matmul(pctx[:, qn * 512:(qn + 1) * 512], V2[:, jc, hq * 64:hq * 64 + 64],
                          B16[par][:, qn * 512:(qn + 1) * 512],
                          start=(jc == 0), stop=(jc == 31))
            bdone[par] = ("Tq", P.now("Tq"))
        V.wait("Tq", P.now("Tq"))
        for qn in range(8):
            V.tensor_scalar(ctxsb[:, qn * 512:(qn + 1) * 512], pctx[0:64, qn * 512:(qn + 1) * 512],
                            vsum[0:64, 0:1], -1.0 / T, op0=OP.subtract, op1=OP.mult)
        P.free(f"pctx{hh}")
        SY.wait("V", P.now("V"))
        if hh < 6:
            SY.dma_start(ctx_out0[hh * 64:(hh + 1) * 64, :], ctxsb[0:64, :], inc="st", k=16)
        else:
            SY.dma_start(ctx_out1[(hh - 6) * 64:(hh - 5) * 64, :], ctxsb[0:64, :], inc="st", k=16)
        if hh == 5 and A.get("ctx_midcb") is not None:
            A["ctx_midcb"]()
        if hh == 7 and A.get("ctx_endcb") is not None:
            A["ctx_endcb"]()
        if hq == 1:
            vdone = ("Tq", P.now("Tq"))
    SY.wait("st", P.now("st"))



def build_bump():
    P = Prog()
    A = {}
    A["v_slice"] = P.din("v_slice", [512, T], BF16)
    A["zq_loc"] = P.din("zq_loc", [HL, T])
    A["zk_loc"] = P.din("zk_loc", [HL, T])
    A["log_bw8"] = P.din("log_bw8", [1, HL])
    A["oh8x"] = P.din("oh8x", [8, 8, 128])
    A["eye16"] = P.din("eye16", [16, 16])
    ctx_out = P.dout("ctx_out", [512, T])
    A["ctx_out0"] = ctx_out[0:384, :]
    A["ctx_out1"] = ctx_out[384:512, :]
    emit_bump(P, A)
    return P

def prep_bump_inputs(ins, v_all, zq_all, zk_all):
    oh8x = np.zeros((8, 8, 128), np.float32)
    for k in range(8):
        oh8x[k, k, :] = 1.0
    eye16 = np.eye(16, dtype=np.float32)
    maps = []
    for c in range(NCORE):
        b, half = c // 2, c % 2
        hs = slice(half * 8, half * 8 + 8)
        maps.append(dict(
            v_slice=np.ascontiguousarray(v_all[b][half * 512:(half + 1) * 512, :]),
            zq_loc=np.ascontiguousarray(zq_all[b][hs, :]),
            zk_loc=np.ascontiguousarray(zk_all[b][hs, :]),
            log_bw8=np.ascontiguousarray(ins["L1_attn_log_bandwidth"][hs].reshape(1, 8)),
            oh8x=oh8x, eye16=eye16,
        ))
    return maps



def emit_l1b(P, A):
    x2T = A["x2T"]
    ctxT = A["ctxT"]
    centT = A["centT"]
    eb16 = A["eb16"]
    wrt = A["wrt"]
    br16m = A["br16m"]
    wst = A["wst"]
    bsT2 = A["bsT2"]
    oh16x = A["oh16x"]
    eye128 = A["eye128"]
    outT = A["outT"]
    x3T = A["x3T_i"]

    V, S, Tn, G, SY = P.V, P.S, P.T, P.G, P.SY
    NT = TL // 512

    x3ln16 = P.sb("x3ln16", [128, DC, TL], BF16)
    wsum16 = P.sb("wsum16", [128, DC, D], BF16)
    cent16 = P.sb("cent16", [128, DC, H], BF16)
    veb = P.sb("veb", [H, 1])
    bs16 = P.sb("bs16", [2, D], BF16)
    halfones = P.sb("halfones", [2, 512], BF16)
    br16 = P.sb("br16", [H, D], BF16)
    eye = P.sb("eye", [128, 128])
    oh16 = P.sb("oh16", [16, 16, 128], BF16)
    g16b = P.sb("g16b", [H, TL], BF16)
    tmpf2 = P.sb("tmpf2", [128, 512])
    spacer = P.sb("spacer", [128, 512])
    x3c16 = P.sb("x3c16", [128, DC, TL], BF16, freeable=True)
    sc = make_ln_scratch(P, TL, freeable=True)

    G.dma_start(cent16[:, :, :], centT.rearrange("(c p) h -> p c h", p=128), inc="ld", k=16)
    G.dma_start(oh16[:, :, :], oh16x, inc="ld", k=16)
    G.dma_start(br16[:, :], br16m, inc="ld", k=16)
    G.dma_start(bs16[:, :], bsT2, inc="ld", k=16)
    SY.dma_start(veb[:, :], eb16, inc="ld2", k=16)
    SY.dma_start(eye[:, :], eye128, inc="ld2", k=16)

    emit_ln_init(P, sc)
    ab = P.sb("ab0", [128, DC, 512], F32, freeable=True)
    cb = P.sb("cb0", [128, DC, 512], F32, freeable=True)
    x3f = P.sb("x3f0", [128, DC, 512], F32, freeable=True)
    ab_rd = None
    st3 = None
    for tc in range(NT):
        sl = slice(tc * 512, (tc + 1) * 512)
        if ab_rd is not None:
            SY.wait(*ab_rd)
            G.wait(*ab_rd)
        SY.dma_start(ab[:, :, :], x2T[:, sl].rearrange("(c p) w -> p c w", p=128),
                     inc="lda", k=16)
        G.dma_start(cb[:, :, :], ctxT[:, sl].rearrange("(c p) w -> p c w", p=128),
                    inc="ldc", k=16)
        V.wait("lda", P.now("lda"))
        V.wait("ldc", P.now("ldc"))
        if st3 is not None:
            V.wait(*st3)
        for kc in range(DC):
            V.tensor_tensor(x3f[:, kc, :], ab[:, kc, :], cb[:, kc, :], op=OP.add)
        ab_rd = ("V", P.now("V"))
        for kc in range(DC):
            V.tensor_copy(x3c16[:, kc, sl], x3f[:, kc, :])
        SY.wait("V", P.now("V"))
        SY.dma_start(x3T[:, sl].rearrange("(c p) w -> p c w", p=128), x3f[:, :, :],
                     inc="st3", k=16)
        st3 = ("st3", P.now("st3"))
        emit_ln_reduce_chunk(P, sc, x3c16[:, :, sl], tc, 512, wait=("V", P.now("V")))
    emit_ln_stats(P, sc, TL)
    for tc in range(NT):
        sl = slice(tc * 512, (tc + 1) * 512)
        emit_ln_bcast_chunk(P, sc, tc, 512)
        emit_ln_apply_chunk(P, sc, x3c16[:, :, sl], x3ln16[:, :, sl], 512)
    emit_barrier(P, dma_sems=("ld", "ld2", "st3"))
    P.free("x3f0"); P.free("cb0"); P.free("ab0")
    free_ln_scratch(P)
    P.free("x3c16")

    lgT = P.sb("lgT", [128, 16, H], freeable=True)
    scT = P.sb("scT", [128, 16, H], freeable=True)
    mkT = P.sb("mkT", [128, 16, H], freeable=True)
    dispT = P.sb("dispT", [128, 16, H], freeable=True)
    lgs = P.sb("lgs", [H, TL], F32, freeable=True)
    stat = {k: P.sb("gstat_" + k, [128, 16], freeable=True)
            for k in ("m1", "sums", "rr", "m1s", "m2s", "es")}
    pg = P.ps("pg", [16, 512], F32, freeable=True)
    ptT = P.ps("ptT", [128, 256], F32, freeable=True)
    V.memset(spacer[:, :], 0.0)
    Tn.wait("ld", P.now("ld"))
    S.wait("ld2", P.now("ld2"))
    for tc in range(NT):
        Tn.wait("S", P.now("S"))
        for kc in range(DC):
            Tn.matmul(pg[:, :], cent16[:, kc, :], x3ln16[:, kc, tc * 512:(tc + 1) * 512],
                      start=(kc == 0), stop=(kc == DC - 1))
        S.wait("Tq", P.now("Tq"))
        S.activation(lgs[0:16, tc * 512:(tc + 1) * 512], pg[0:16, :], AF.Identity,
                     bias=veb[0:16, 0:1], scale=1.0)
    for half in range(2):
        Tn.wait("S", P.now("S"))
        Tn.wait("V", P.now("V"))
        for cx in range(8):
            c = half * 8 + cx
            Tn.transpose(ptT[:, cx * 16:(cx + 1) * 16], lgs[0:16, c * 128:(c + 1) * 128],
                         eye[0:16, 0:16])
        V.wait("Tq", P.now("Tq"))
        for cx in range(8):
            V.tensor_copy(lgT[:, half * 8 + cx, :], ptT[:, cx * 16:(cx + 1) * 16])
    emit_softmax_top2(P, lgT, 16, H, None, scT, mkT, dispT, None, stat, spacer, need_msel=False)
    pb16 = P.ps("pb16", [16, 2048], F32, freeable=True)
    Tn.wait("V", P.now("V"))
    for c in range(16):
        Tn.transpose(pb16[0:16, c * 128:(c + 1) * 128], dispT[:, c, :], eye[:, :])
    V.wait("Tq", P.now("Tq"))
    V.tensor_copy(g16b[0:16, :], pb16[0:16, :])
    emit_barrier(P)
    P.free("pb16"); P.free("ptT"); P.free("pg")
    for k in reversed(list(stat)):
        P.free("gstat_" + k)
    P.free("lgs"); P.free("dispT"); P.free("mkT"); P.free("scT"); P.free("lgT")

    wsb = [P.sb("wsb0", [128, D], F32, freeable=True),
           P.sb("wsb1", [128, D], F32, freeable=True)]
    V.memset(halfones[:, :], 0.5)
    for kc in range(DC):
        SY.wait("V", P.now("V"))
        SY.dma_start(wsb[0][:, :], wst[0, kc * 128:(kc + 1) * 128, :], inc="ldS", k=16)
        SY.dma_start(wsb[1][:, :], wst[1, kc * 128:(kc + 1) * 128, :], inc="ldS", k=16)
        V.wait("ldS", P.now("ldS"))
        V.tensor_tensor(wsb[0][:, :], wsb[0][:, :], wsb[1][:, :], op=OP.add)
        V.tensor_scalar(wsum16[:, kc, :], wsb[0][:, :], 0.5, None, op0=OP.mult)
    emit_barrier(P)
    P.free("wsb1"); P.free("wsb0")

    acc16 = P.sb("acc16", [128, DC, TL], BF16)
    gbe = [P.sb("gbe0", [128, TL]), P.sb("gbe1", [128, TL])]
    Gt = [P.sb("Gt0", [128, DC, 512], BF16), P.sb("Gt1", [128, DC, 512], BF16)]
    wrb = [P.sb("wrb0", [128, DC, D], BF16), P.sb("wrb1", [128, DC, D], BF16)]
    outb = P.sb("outb", [128, DC, 512], F32)
    x3b = P.sb("x3b", [128, DC, 512], F32)
    pool6 = MMPool(P, n=6, name="pacc")
    for kc in range(DC):
        V.memset(acc16[:, kc, :], 0.0)

    NE = 16
    wr_done = [None, None]
    n_ldw = [0, 0]
    gt_rd = [None, None]
    gt_wr = [None, None]

    def load_wr(e2):
        par2 = e2 % 2
        if wr_done[par2] is not None:
            G.wait(*wr_done[par2])
        for kc in range(DC):
            G.dma_start(wrb[par2][:, kc, :], wrt[e2, kc * 128:(kc + 1) * 128, :],
                        inc=f"ldw{par2}", k=16)
        n_ldw[par2] = P.now(f"ldw{par2}")

    def bcast_g(e2):
        par2 = e2 % 2
        for tcq in range(NT):
            p = pool6.begin()
            Tn.matmul(p[:, :], oh16[:, e2, :], g16b[0:16, tcq * 512:(tcq + 1) * 512],
                      start=True, stop=True)
            V.wait("Tq", P.now("Tq"))
            V.tensor_copy(gbe[par2][:, tcq * 512:(tcq + 1) * 512], p[:, :])
            pool6.mark("V")

    def emit_gt(idx):
        e2, tc2 = divmod(idx, NT)
        pt = idx % 2
        if gt_rd[pt] is not None:
            V.wait(*gt_rd[pt])
        sl2 = slice(tc2 * 512, (tc2 + 1) * 512)
        for kc in range(DC):
            V.tensor_tensor(Gt[pt][:, kc, :], x3ln16[:, kc, sl2],
                            gbe[e2 % 2][:, sl2], op=OP.mult)
        gt_wr[pt] = P.now("V")

    load_wr(0)
    bcast_g(0)
    emit_gt(0)
    for e in range(NE):
        par = e % 2
        if e + 1 < NE:
            bcast_g(e + 1)
            load_wr(e + 1)
        Tn.wait(f"ldw{par}", n_ldw[par])
        for tc in range(NT):
            idx = e * NT + tc
            pt = idx % 2
            if idx + 1 < NE * NT:
                emit_gt(idx + 1)
            sl = slice(tc * 512, (tc + 1) * 512)
            Tn.wait("V", gt_wr[pt])
            for oc in range(DC):
                p = pool6.begin()
                for kc in range(DC):
                    Tn.matmul(p[:, :], wrb[par][:, kc, oc * 128:(oc + 1) * 128],
                              Gt[pt][:, kc, :], start=(kc == 0), stop=(kc == DC - 1))
                V.wait("Tq", P.now("Tq"))
                V.tensor_tensor(acc16[:, oc, sl], acc16[:, oc, sl], p[:, :], op=OP.add)
                pool6.mark("V")
            gt_rd[pt] = ("Tq", P.now("Tq"))
        wr_done[par] = ("Tq", P.now("Tq"))

    x3_rd = None
    st_done = None
    for tc in range(NT):
        sl = slice(tc * 512, (tc + 1) * 512)
        if x3_rd is not None:
            SY.wait(*x3_rd)
            G.wait(*x3_rd)
        for kc in range(DC):
            q = SY if kc % 2 == 0 else G
            q.dma_start(x3b[:, kc, :], x3T[kc * 128:(kc + 1) * 128, sl], inc="ldx3", k=16)
        nldx3 = P.now("ldx3")
        for oc in range(DC):
            p = pool6.begin()
            Tn.matmul(p[:, :], br16[0:16, oc * 128:(oc + 1) * 128], g16b[0:16, sl],
                      start=True, stop=False)
            Tn.matmul(p[:, :], bs16[0:2, oc * 128:(oc + 1) * 128],
                      halfones[0:2, :], start=False, stop=False)
            for kc in range(DC):
                Tn.matmul(p[:, :], wsum16[:, kc, oc * 128:(oc + 1) * 128],
                          x3ln16[:, kc, sl], start=False, stop=(kc == DC - 1))
            V.wait("Tq", P.now("Tq"))
            V.wait("ldx3", nldx3)
            if st_done is not None and oc == 0:
                V.wait(*st_done)
            V.tensor_tensor(tmpf2[:, :], p[:, :], acc16[:, oc, sl], op=OP.add)
            V.tensor_tensor(tmpf2[:, :], tmpf2[:, :], x3ln16[:, oc, sl], op=OP.add)
            V.tensor_tensor(outb[:, oc, :], tmpf2[:, :], x3b[:, oc, :], op=OP.add)
            pool6.mark("V")
        x3_rd = ("V", P.now("V"))
        SY.wait("V", P.now("V"))
        SY.dma_start(outT[:, sl].rearrange("(c p) w -> p c w", p=128),
                     outb[:, :, :], inc="st", k=16)
        st_done = ("st", P.now("st"))
    SY.wait("st", P.now("st"))



def build_l1b():
    P = Prog()
    A = {}
    A["x2T"] = P.din("x2T", [D, TL])
    A["ctxT"] = P.din("ctxT", [D, TL])
    A["centT"] = P.din("centT", [D, H])
    A["eb16"] = P.din("eb16", [H, 1])
    A["wrt"] = P.din("wrt", [H, D, D])
    A["br16m"] = P.din("br16m", [H, D])
    A["wst"] = P.din("wst", [2, D, D])
    A["bsT2"] = P.din("bsT2", [2, D])
    A["oh16x"] = P.din("oh16x", [16, 16, 128])
    A["eye128"] = P.din("eye128", [128, 128])
    A["outT"] = P.dout("outT", [D, TL])
    A["x3T_i"] = P.nc.dram_tensor("x3T_i", [D, TL], F32).ap()
    emit_l1b(P, A)
    return P

def prep_l1b_inputs(ins, x2T_all, ctxT_all):
    oh16x = np.zeros((16, 16, 128), np.float32)
    for k in range(16):
        oh16x[k, k, :] = 1.0
    eye128 = np.eye(128, dtype=np.float32)
    centT = np.ascontiguousarray(ins["L1_ffn_centroids"].T)
    eb16 = np.ascontiguousarray(ins["L1_ffn_expert_biases"].reshape(16, 1))
    wrt = np.ascontiguousarray(ins["L1_ffn_Wr"].transpose(0, 2, 1))
    br16m = ins["L1_ffn_br"]
    wst = np.ascontiguousarray(ins["L1_ffn_Ws"].transpose(0, 2, 1))
    bsT2 = np.ascontiguousarray(ins["L1_ffn_bs"])
    maps = []
    for c in range(NCORE):
        m = dict(
            centT=centT, eb16=eb16,
            wrt=wrt, br16m=br16m, wst=wst, bsT2=bsT2, oh16x=oh16x, eye128=eye128,
        )
        if x2T_all[c] is not None:
            m["x2T"] = x2T_all[c]; m["ctxT"] = ctxT_all[c]
        maps.append(m)
    return maps



PAIRS = [[0, 1], [2, 3], [4, 5], [6, 7]]


def build_fused():
    P = Prog()
    nc = P.nc
    A = {}
    A["xT"] = P.din("xT", [D, T])
    A["w_uv"] = P.din("w_uv", [D, 1024])
    A["b_uv"] = P.din("b_uv", [128, 8])
    A["conv_w3"] = P.din("conv_w3", [128, 4, 3])
    A["conv_b4"] = P.din("conv_b4", [128, 4])
    A["logit_a4"] = P.din("logit_a4", [128, 4])
    A["ssm_b4"] = P.din("ssm_b4", [128, 4])
    A["ssm_c4"] = P.din("ssm_c4", [128, 4])
    A["ssm_d4"] = P.din("ssm_d4", [128, 4])
    A["xT_sl"] = P.din("xT_sl", [D, TL])
    A["opw"] = P.din("opw", [D, D])
    A["opb8"] = P.din("opb8", [128, 8])
    A["lnw8"] = P.din("lnw8", [128, 8])
    A["lnb8"] = P.din("lnb8", [128, 8])
    A["gate_wT"] = P.din("gate_wT", [D, 8])
    A["gate_b8"] = P.din("gate_b8", [8, 1])
    A["w1t"] = P.din("w1t", [8, D, 4096])
    A["b1T"] = P.din("b1T", [8, 128, 32])
    A["w2t"] = P.din("w2t", [8, 4096, D])
    A["b2T"] = P.din("b2T", [8, 128, 8])
    A["oh8"] = P.din("oh8", [8, 8, 128])
    A["eye128"] = P.din("eye128", [128, 128])
    A["wqT"] = P.din("wqT", [D, D])
    A["wkT"] = P.din("wkT", [D, D])
    A["wvT"] = P.din("wvT", [D, D])
    A["bq8"] = P.din("bq8", [128, 8])
    A["bk8"] = P.din("bk8", [128, 8])
    A["bv8"] = P.din("bv8", [128, 8])
    A["wprojd"] = P.din("wprojd", [16, 128, H])
    A["log_bw8"] = P.din("log_bw8", [1, 8])
    A["oh8x"] = P.din("oh8x", [8, 8, 128])
    A["eye16"] = P.din("eye16", [16, 16])
    A["centT"] = P.din("centT", [D, H])
    A["eb16"] = P.din("eb16", [H, 1])
    A["wrt"] = P.din("wrt", [H, D, D])
    A["br16m"] = P.din("br16m", [H, D])
    A["wst"] = P.din("wst", [2, D, D])
    A["bsT2"] = P.din("bsT2", [2, D])
    A["oh16x"] = P.din("oh16x", [16, 16, 128])
    meta = P.din("meta", [1, 12], I32)
    A["outT"] = P.dout("outT", [D, TL])

    g_mine = nc.dram_tensor("g_mine", [512, T], BF16).ap()
    g_pair = nc.dram_tensor("g_pair", [1024, T], BF16).ap()
    g_sl = nc.dram_tensor("g_sl", [D, TL], BF16).ap()
    x1T_i = nc.dram_tensor("x1T_i", [D, TL], F32).ap()
    x1lnT_i = nc.dram_tensor("x1lnT_i", [D, TL], BF16).ap()
    x2T_i = nc.dram_tensor("x2T_i", [D, TL], F32).ap()
    v_mine = nc.dram_tensor("v_mine", [D, TL], BF16).ap()
    v_pair = nc.dram_tensor("v_pair", [2 * D, TL], BF16).ap()
    v_sl = nc.dram_tensor("v_sl", [512, T], BF16).ap()
    z_mine = nc.dram_tensor("z_mine", [2 * H, TL], F32).ap()
    z_pair = nc.dram_tensor("z_pair", [4 * H, TL], F32).ap()
    zq_i = nc.dram_tensor("zq_i", [8, T], F32).ap()
    zk_i = nc.dram_tensor("zk_i", [8, T], F32).ap()
    ctx_mineA = nc.dram_tensor("ctx_mineA", [384, T], F32).ap()
    ctx_mineB = nc.dram_tensor("ctx_mineB", [128, T], F32).ap()
    ctx_pairA = nc.dram_tensor("ctx_pairA", [768, T], F32).ap()
    ctx_pairB = nc.dram_tensor("ctx_pairB", [256, T], F32).ap()
    ctx_sl = nc.dram_tensor("ctx_sl", [D, TL], F32).ap()
    x3T_i = nc.dram_tensor("x3T_i", [D, TL], F32).ap()

    A["g_out"] = g_mine
    A["g16T"] = g_sl
    A["x1T"] = x1T_i
    A["x1lnT"] = x1lnT_i
    A["x2T"] = x2T_i
    A["v16out"] = v_mine
    A["zq_out"] = z_mine[0:H, :]
    A["zk_out"] = z_mine[H:2 * H, :]
    A["v_slice"] = v_sl
    A["zq_loc"] = zq_i
    A["zk_loc"] = zk_i
    A["ctx_out0"] = ctx_mineA
    A["ctx_out1"] = ctx_mineB
    A["ctxT"] = ctx_sl
    A["x3T_i"] = x3T_i

    G, SY = P.G, P.SY
    smeta = P.sb("smeta_g", [1, 12], I32)
    SY.dma_start(smeta[:, :], meta, inc="ldm", k=16)

    rg = P.es.enter_context(nc.gpsimd.register("rgoff"))
    rg2 = P.es.enter_context(nc.gpsimd.register("rgoff2"))

    def ag(kind_in, kind_out):
        G.collective_compute("AllGather", mybir.AluOpType.bypass,
                             replica_groups=PAIRS, ins=[kind_in.opt()], outs=[kind_out.opt()])
        G.fence()

    with P.phase("_a"):
        emit_l0a(P, A)
    ag(g_mine, g_pair)
    G.wait("ldm", P.now("ldm"))
    G.reg_load(rg, smeta[0:1, 0:1])
    G.dma_start(bass.AP(g_sl.tensor, 0, [[TL, D], [1, TL]]),
                bass.AP(g_pair.tensor, rg, [[T, D], [1, TL]]), inc="xch", k=16)
    emit_barrier(P, all_sems=True)

    with P.phase("_b1"):
        emit_l0b1(P, A)

    with P.phase("_b2"):
        emit_l0b2(P, A)

    def after_v():
        G.wait("st", P.now("st"))
        ag(v_mine, v_pair)
        G.wait("ldm", P.now("ldm"))
        G.reg_load(rg, smeta[0:1, 1:2])
        G.dma_start(bass.AP(v_sl.tensor, 0, [[T, 512], [1, TL]]),
                    bass.AP(v_pair.tensor, rg, [[TL, 512], [1, TL]]), inc="xch", k=16)
        G.reg_load(rg2, smeta[0:1, 2:3])
        G.dma_start(bass.AP(v_sl.tensor, TL, [[T, 512], [1, TL]]),
                    bass.AP(v_pair.tensor, rg2, [[TL, 512], [1, TL]]), inc="xch", k=16)
    A["after_v"] = after_v
    with P.phase("_c"):
        emit_l1a(P, A)
    ag(z_mine, z_pair)
    G.reg_load(rg, smeta[0:1, 3:4])
    G.dma_start(bass.AP(zq_i.tensor, 0, [[T, 8], [256, 16], [1, 128]]),
                bass.AP(z_pair.tensor, rg, [[TL, 8], [128, 16], [1, 128]]), inc="xch", k=16)
    G.reg_load(rg2, smeta[0:1, 4:5])
    G.dma_start(bass.AP(zq_i.tensor, 128, [[T, 8], [256, 16], [1, 128]]),
                bass.AP(z_pair.tensor, rg2, [[TL, 8], [128, 16], [1, 128]]), inc="xch", k=16)
    G.reg_load(rg, smeta[0:1, 5:6])
    G.dma_start(bass.AP(zk_i.tensor, 0, [[T, 8], [256, 16], [1, 128]]),
                bass.AP(z_pair.tensor, rg, [[TL, 8], [128, 16], [1, 128]]), inc="xch", k=16)
    G.reg_load(rg2, smeta[0:1, 6:7])
    G.dma_start(bass.AP(zk_i.tensor, 128, [[T, 8], [256, 16], [1, 128]]),
                bass.AP(z_pair.tensor, rg2, [[TL, 8], [128, 16], [1, 128]]), inc="xch", k=16)
    emit_barrier(P, all_sems=True)

    def ctx_midcb():
        G.wait("st", P.now("st"))
        ag(ctx_mineA, ctx_pairA)
        G.wait("ldm", P.now("ldm"))
        G.reg_load(rg, smeta[0:1, 0:1])
        G.dma_start(bass.AP(ctx_sl.tensor, 0, [[TL, 384], [1, TL]]),
                    bass.AP(ctx_pairA.tensor, rg, [[T, 384], [1, TL]]), inc="xch", k=16)
        G.reg_load(rg2, smeta[0:1, 7:8])
        G.dma_start(bass.AP(ctx_sl.tensor, 512 * TL, [[TL, 384], [1, TL]]),
                    bass.AP(ctx_pairA.tensor, rg2, [[T, 384], [1, TL]]), inc="xch", k=16)

    def ctx_endcb():
        G.wait("st", P.now("st"))
        ag(ctx_mineB, ctx_pairB)
        G.reg_load(rg, smeta[0:1, 0:1])
        G.dma_start(bass.AP(ctx_sl.tensor, 384 * TL, [[TL, 128], [1, TL]]),
                    bass.AP(ctx_pairB.tensor, rg, [[T, 128], [1, TL]]), inc="xch", k=16)
        G.reg_load(rg2, smeta[0:1, 8:9])
        G.dma_start(bass.AP(ctx_sl.tensor, 896 * TL, [[TL, 128], [1, TL]]),
                    bass.AP(ctx_pairB.tensor, rg2, [[T, 128], [1, TL]]), inc="xch", k=16)
    A["ctx_midcb"] = ctx_midcb
    A["ctx_endcb"] = ctx_endcb
    with P.phase("_d"):
        emit_bump(P, A)

    with P.phase("_e"):
        emit_l1b(P, A)
    return P


def prep_fused_inputs(ins, xT_host):
    m_l0a = prep_l0a_inputs(ins, xT_host)
    m_l0b1 = prep_l0b1_inputs(ins, xT_host, [None] * B)
    m_l0b2 = prep_l0b2_inputs(ins, [None] * NCORE, [None] * NCORE)
    m_l1a = prep_l1a_inputs(ins, [None] * NCORE)
    m_bump = prep_bump_inputs_fused(ins)
    m_l1b = prep_l1b_inputs(ins, [None] * NCORE, [None] * NCORE)
    maps = []
    for c in range(NCORE):
        b, half = c // 2, c % 2
        m = {}
        for src in (m_l0a[c], m_l0b1[c], m_l0b2[c], m_l1a[c], m_bump[c], m_l1b[c]):
            m.update(src)
        for k in ("g16T", "x1T", "x1lnT", "x2T", "v_slice", "zq_loc", "zk_loc", "ctxT"):
            m.pop(k, None)
        m["meta"] = np.array([[half * TL,
                               half * 512 * TL,
                               D * TL + half * 512 * TL,
                               half * 8 * TL,
                               (2 * H + half * 8) * TL,
                               (H + half * 8) * TL,
                               (3 * H + half * 8) * TL,
                               384 * T + half * TL,
                               128 * T + half * TL,
                               0, 0, 0]], np.int32)
        maps.append(m)
    return maps


def prep_bump_inputs_fused(ins):
    oh8x = np.zeros((8, 8, 128), np.float32)
    for k in range(8):
        oh8x[k, k, :] = 1.0
    eye16 = np.eye(16, dtype=np.float32)
    maps = []
    for c in range(NCORE):
        b, half = c // 2, c % 2
        hs = slice(half * 8, half * 8 + 8)
        maps.append(dict(
            log_bw8=np.ascontiguousarray(ins["L1_attn_log_bandwidth"][hs].reshape(1, 8)),
            oh8x=oh8x, eye16=eye16,
        ))
    return maps



def kernel(**inputs):
    ins = {k: np.asarray(v) for k, v in inputs.items()}
    x = np.asarray(ins["L0_attn_x"], np.float32)
    xT_host = np.ascontiguousarray(x.transpose(0, 2, 1))
    P = build_fused()
    nc = P.finalize()
    maps = prep_fused_inputs(ins, xT_host)
    res = run_bass_kernel_spmd(nc, maps, core_ids=list(range(NCORE))).results
    out = np.empty((B, T, D), np.float32)
    for c in range(NCORE):
        b, half = c // 2, c % 2
        t0 = half * TL
        out[b, t0:t0 + TL, :] = np.asarray(res[c]["outT"]).T
    return out


# version 05
# speedup vs baseline: 1.3080x; 1.1123x over previous
import numpy as np
import contextlib
import concourse.bass as bass
import concourse.mybir as mybir
from concourse.alu_op_type import AluOpType as OP
from concourse.bass_utils import run_bass_kernel_spmd

F32 = mybir.dt.float32
BF16 = mybir.dt.bfloat16
U8 = mybir.dt.uint8
I32 = mybir.dt.int32
AF = mybir.ActivationFunctionType
AX = mybir.AxisListType

D = 1024
DC = 8
H = 16
DH = 64
B = 4
T = 4096
TL = 2048
NCORE = 8


PROG_SEM = {"vector": "V", "scalar": "S", "tensor": "Tq", "gpsimd": "G", "sync": "SYq"}
DMA_OPS = {"dma_start", "dma_start_transpose", "indirect_dma_start"}
REG_OPS = {"reg_load", "reg_mov", "reg_add", "reg_sub", "reg_mul", "reg_save", "reg_alu"}


class Eng:

    def __init__(self, prog, name):
        self._p = prog
        self._n = name
        self._ps = PROG_SEM[name]

    def wait(self, sem, val):
        if val <= 0:
            return
        h = self._p.sem(sem)
        self._p.steps[self._n].append(lambda q: q.wait_ge(h, val))

    def wait_now(self, sem):
        self.wait(sem, self._p.now(sem))

    def fence(self):
        self.wait(self._ps, self._p.now(self._ps))

    def __getattr__(self, attr):
        if attr.startswith("_"):
            raise AttributeError(attr)
        p = self._p

        def call(*args, inc=None, k=1, **kw):
            if attr in DMA_OPS:
                assert inc is not None, f"DMA op {attr} needs inc="
                semh = p.sem(inc)
                p.inc(inc, 16)

                def f(q):
                    getattr(q, attr)(*args, **kw).then_inc(semh, 16)
            elif attr in REG_OPS:
                def f(q):
                    getattr(q, attr)(*args, **kw)
            else:
                assert inc is None
                semh = p.sem(self._ps)
                p.inc(self._ps, 1)

                def f(q):
                    getattr(q, attr)(*args, **kw).then_inc(semh, 1)
            p.steps[self._n].append(f)
        return call


class Prog:
    def __init__(self):
        self.nc = bass.Bass("TRN2", target_bir_lowering=False)
        self.es = contextlib.ExitStack()
        self.steps = {e: [] for e in ("sync", "vector", "scalar", "tensor", "gpsimd")}
        self.cnt = {}
        self.sems = {}
        self._cms = {}
        self.T = Eng(self, "tensor")
        self.V = Eng(self, "vector")
        self.S = Eng(self, "scalar")
        self.G = Eng(self, "gpsimd")
        self.SY = Eng(self, "sync")

    def sem(self, name):
        if name not in self.sems:
            self.sems[name] = self.es.enter_context(self.nc.semaphore(name))
            self.cnt[name] = 0
        return self.sems[name]

    def inc(self, name, kk=1):
        self.sem(name)
        self.cnt[name] += kk
        return self.cnt[name]

    def now(self, name):
        self.sem(name)
        return self.cnt.get(name, 0)

    tag = ""
    phase_mode = False

    def sb(self, name, shape, dt=F32, freeable=False):
        name = name + self.tag
        cm = self.nc.sbuf_tensor(name, shape, dt)
        if freeable or self.phase_mode:
            h = cm.__enter__()
            self._cms[name] = cm
            return h
        return self.es.enter_context(cm)

    def free(self, name):
        name2 = name + self.tag
        if name2 not in self._cms:
            name2 = name
        self._cms.pop(name2).__exit__(None, None, None)

    def free_ps(self, name):
        self.free(name)

    def ps(self, name, shape, dt=F32, freeable=False):
        name = name + self.tag
        cm = self.nc.psum_tensor(name, shape, dt)
        if freeable or self.phase_mode:
            h = cm.__enter__()
            self._cms[name] = cm
            return h
        return self.es.enter_context(cm)

    import contextlib as _ctx

    @_ctx.contextmanager
    def phase(self, tag):
        before = list(self._cms)
        old_tag, old_mode = self.tag, self.phase_mode
        self.tag, self.phase_mode = tag, True
        try:
            yield
        finally:
            self.tag, self.phase_mode = old_tag, old_mode
            emit_barrier(self, all_sems=True)
            for name in reversed([n for n in self._cms if n not in before]):
                self._cms.pop(name).__exit__(None, None, None)

    def din(self, name, shape, dt=F32):
        return self.nc.dram_tensor(name, shape, dt, kind="ExternalInput").ap()

    def dout(self, name, shape, dt=F32):
        return self.nc.dram_tensor(name, shape, dt, kind="ExternalOutput").ap()

    def finalize(self):
        for name in reversed(list(self._cms)):
            self.free(name)
        with self.es:
            with self.nc.Block() as blk:
                for eng, fns in self.steps.items():
                    if fns:
                        getattr(blk, eng)(lambda q, fns=fns: [f(q) for f in fns])
        return self.nc


def ceil_div(a, b):
    return (a + b - 1) // b


def emit_barrier(P, dma_sems=(), all_sems=False):
    pairs = [(P.V, "V"), (P.S, "S"), (P.T, "Tq"), (P.G, "G"), (P.SY, "SYq")]
    counts = {s: P.now(s) for _, s in pairs}
    if all_sems:
        for s in list(P.cnt):
            counts[s] = P.now(s)
    for ds in dma_sems:
        counts[ds] = P.now(ds)
    for em, s in pairs:
        for s2, c in counts.items():
            if s2 != s:
                em.wait(s2, c)


class MMPool:

    def __init__(self, P, n=4, name="mmp", shape=(128, 512)):
        self.P = P
        self.tiles = [P.ps(f"{name}{i}", list(shape), F32) for i in range(n)]
        self.n = n
        self.group = 0
        self.marks = []

    def begin(self):
        i = self.group
        t = self.tiles[i % self.n]
        if i >= self.n:
            sem, cntv = self.marks[i - self.n]
            self.P.T.wait(sem, cntv)
        self.group += 1
        return t

    def mark(self, sem):
        self.marks.append((sem, self.P.now(sem)))



LN_SCRATCH_NAMES = ["sq16", "ones16", "ones1f", "c1", "c0", "epsv", "c1b", "c0b", "tmp16"]


def free_ln_scratch(P, tag=""):
    P.free_ps("lns_pbc1" + tag); P.free_ps("lns_pbc0" + tag)
    P.free_ps("lns_pred1" + tag); P.free_ps("lns_pred0" + tag)
    names = ["lns_sq0", "lns_ones16", "lns_ones1f", "lns_c1", "lns_c0", "lns_eps",
             "lns_c1b", "lns_c0b", "lns_tmp16"]
    for n in reversed(names):
        P.free(n + tag)


def make_ln_scratch(P, nW_max, tag="", freeable=False):
    return {
        "nW_max": nW_max,
        "sq16": P.sb("lns_sq0" + tag, [128, 512], BF16, freeable=freeable),
        "ones16": P.sb("lns_ones16" + tag, [128, 1], BF16, freeable=freeable),
        "ones1f": P.sb("lns_ones1f" + tag, [1, 128], F32, freeable=freeable),
        "c1": P.sb("lns_c1" + tag, [1, nW_max], F32, freeable=freeable),
        "c0": P.sb("lns_c0" + tag, [1, nW_max], F32, freeable=freeable),
        "epsv": P.sb("lns_eps" + tag, [1, 1], F32, freeable=freeable),
        "c1b": P.sb("lns_c1b" + tag, [128, 512], BF16, freeable=freeable),
        "c0b": P.sb("lns_c0b" + tag, [128, 512], BF16, freeable=freeable),
        "tmp16": P.sb("lns_tmp16" + tag, [128, 512], BF16, freeable=freeable),
        "p_red": [P.ps("lns_pred0" + tag, [1, 512], F32, freeable=freeable),
                  P.ps("lns_pred1" + tag, [1, 512], F32, freeable=freeable)],
        "p_bc": [P.ps("lns_pbc0" + tag, [128, 512], F32, freeable=freeable),
                 P.ps("lns_pbc1" + tag, [128, 512], F32, freeable=freeable)],
    }


def emit_ln_init(P, sc):
    P.V.memset(sc["ones16"][:, :], 1.0)
    P.V.memset(sc["epsv"][:, :], 1e-5)
    P.V.memset(sc["ones1f"][:, :], 1.0)


def emit_ln_reduce_chunk(P, sc, x16ch, tc, w, wait=None):
    V, S, Tn = P.V, P.S, P.T
    if wait is not None:
        S.wait(*wait)
        Tn.wait(*wait)
    Tn.wait("V", P.now("V"))
    for kc in range(DC):
        S.wait("Tq", P.now("Tq"))
        S.activation(sc["sq16"][:, 0:w], x16ch[:, kc, 0:w], AF.Square, bias=0.0, scale=1.0)
        Tn.wait("S", P.now("S"))
        Tn.matmul(sc["p_red"][0][0:1, 0:w], sc["ones16"][:, :], x16ch[:, kc, 0:w],
                  start=(kc == 0), stop=(kc == DC - 1))
        Tn.matmul(sc["p_red"][1][0:1, 0:w], sc["ones16"][:, :], sc["sq16"][:, 0:w],
                  start=(kc == 0), stop=(kc == DC - 1))
    V.wait("Tq", P.now("Tq"))
    V.tensor_copy(sc["c1"][0:1, tc * 512:tc * 512 + w], sc["p_red"][0][0:1, 0:w])
    V.tensor_copy(sc["c0"][0:1, tc * 512:tc * 512 + w], sc["p_red"][1][0:1, 0:w])


def emit_ln_stats(P, sc, nW):
    V, S = P.V, P.S
    c1, c0 = sc["c1"], sc["c0"]
    V.tensor_scalar(c1[0:1, 0:nW], c1[0:1, 0:nW], 1.0 / D, None, op0=OP.mult)
    V.tensor_scalar(c0[0:1, 0:nW], c0[0:1, 0:nW], 1.0 / D, None, op0=OP.mult)
    for tc in range(ceil_div(nW, 512)):
        w = min(512, nW - tc * 512)
        sl = (slice(0, 1), slice(tc * 512, tc * 512 + w))
        V.tensor_tensor(sc["p_red"][0][0:1, 0:w], c1[sl], c1[sl], op=OP.mult)
        V.tensor_tensor(c0[sl], c0[sl], sc["p_red"][0][0:1, 0:w], op=OP.subtract)
    S.wait_now("V")
    S.activation(c0[0:1, 0:nW], c0[0:1, 0:nW], AF.Sqrt, bias=sc["epsv"][0:1, 0:1], scale=1.0)
    V.wait_now("S")
    V.reciprocal(c0[0:1, 0:nW], c0[0:1, 0:nW])
    V.tensor_tensor(c1[0:1, 0:nW], c1[0:1, 0:nW], c0[0:1, 0:nW], op=OP.mult)
    V.tensor_scalar(c1[0:1, 0:nW], c1[0:1, 0:nW], -1.0, None, op0=OP.mult)


def emit_ln_bcast_chunk(P, sc, tc, w):
    V, Tn = P.V, P.T
    Tn.wait("V", P.now("V"))
    Tn.matmul(sc["p_bc"][0][:, 0:w], sc["ones1f"][0:1, :], sc["c0"][0:1, tc * 512:tc * 512 + w],
              start=True, stop=True)
    Tn.matmul(sc["p_bc"][1][:, 0:w], sc["ones1f"][0:1, :], sc["c1"][0:1, tc * 512:tc * 512 + w],
              start=True, stop=True)
    V.wait("Tq", P.now("Tq"))
    V.tensor_copy(sc["c1b"][:, 0:w], sc["p_bc"][0][:, 0:w])
    V.tensor_copy(sc["c0b"][:, 0:w], sc["p_bc"][1][:, 0:w])


def emit_ln_apply_chunk(P, sc, x16ch, out16ch, w):
    V = P.V
    for kc in range(DC):
        V.tensor_tensor(sc["tmp16"][:, 0:w], x16ch[:, kc, 0:w], sc["c1b"][:, 0:w], op=OP.mult)
        V.tensor_tensor(out16ch[:, kc, 0:w], sc["tmp16"][:, 0:w], sc["c0b"][:, 0:w], op=OP.add)



def emit_l0a(P, A):
    xT = A["xT"]
    w_uv = A["w_uv"]
    b_uv = A["b_uv"]
    conv_w3 = A["conv_w3"]
    conv_b4 = A["conv_b4"]
    logit_a4 = A["logit_a4"]
    ssm_b4 = A["ssm_b4"]
    ssm_c4 = A["ssm_c4"]
    ssm_d4 = A["ssm_d4"]
    g_out = A["g_out"]

    V, S, Tn, G, SY = P.V, P.S, P.T, P.G, P.SY

    xln16 = P.sb("xln16", [128, DC, T], BF16)
    wuv16 = P.sb("wuv16", [128, DC, 1024], BF16)
    vb_uv = P.sb("vb_uv", [128, 8])
    vconv_w = P.sb("vconv_w", [128, 4, 3])
    vconv_b = P.sb("vconv_b", [128, 4])
    vlogit = P.sb("vlogit", [128, 4])
    vsb = P.sb("vsb", [128, 4])
    vsc_ = P.sb("vsc", [128, 4])
    vsd = P.sb("vsd", [128, 4])
    va = P.sb("va", [128, 4])
    vlna = P.sb("vlna", [128, 4])
    va128 = P.sb("va128", [128, 4])
    s128 = P.sb("s128", [128, 1])
    ones128 = P.sb("ones128", [128, 128])
    iotaf = P.sb("iotaf", [128, 128])
    iotai = P.sb("iotai", [128, 128], I32)
    apw = P.sb("apw", [128, 128])
    Acg = P.sb("Acg", [128, 128])
    Acg128 = P.sb("Acg128", [128, 32])
    hend = P.sb("hend", [128, 32])
    scanC = P.sb("scanC", [128, 32])
    sc = make_ln_scratch(P, T, freeable=True)
    xbuf = [P.sb("xbuf0", [128, DC, 512], F32, freeable=True),
            P.sb("xbuf1", [128, DC, 512], F32, freeable=True)]

    for kc in range(DC):
        G.dma_start(wuv16[:, kc, :], w_uv[kc * 128:(kc + 1) * 128, :], inc="ld", k=16)
    for dst, src_ in [(vb_uv, b_uv), (vconv_w, conv_w3), (vconv_b, conv_b4), (vlogit, logit_a4),
                      (vsb, ssm_b4), (vsc_, ssm_c4), (vsd, ssm_d4)]:
        SY.dma_start(dst[:, :] if dst is not vconv_w else dst[:, :, :], src_, inc="ld2", k=16)
    nld2 = P.now("ld2")

    emit_ln_init(P, sc)

    cast_done = []
    for tc in range(8):
        par = tc % 2
        sl = slice(tc * 512, (tc + 1) * 512)
        if len(cast_done) >= 2:
            SY.wait("V", cast_done[-2][0]); SY.wait("S", cast_done[-2][1])
            G.wait("V", cast_done[-2][0]); G.wait("S", cast_done[-2][1])
        for kc in range(DC):
            q = SY if kc % 2 == 0 else G
            q.dma_start(xbuf[par][:, kc, :], xT[kc * 128:(kc + 1) * 128, sl],
                        inc=f"ldx{par}", k=16)
        V.wait(f"ldx{par}", P.now(f"ldx{par}"))
        S.wait(f"ldx{par}", P.now(f"ldx{par}"))
        for kc in range(DC):
            if kc % 2 == 0:
                V.tensor_copy(xln16[:, kc, sl], xbuf[par][:, kc, :])
            else:
                S.activation(xln16[:, kc, sl], xbuf[par][:, kc, :], AF.Identity,
                             bias=0.0, scale=1.0)
        cast_done.append((P.now("V"), P.now("S")))
        Tn.wait("S", P.now("S"))
        emit_ln_reduce_chunk(P, sc, xln16[:, :, sl], tc, 512, wait=("V", P.now("V")))
    emit_ln_stats(P, sc, T)
    for tc in range(8):
        sl = slice(tc * 512, (tc + 1) * 512)
        emit_ln_bcast_chunk(P, sc, tc, 512)
        for kc in range(DC):
            V.tensor_tensor(sc["tmp16"][:, :], xln16[:, kc, sl], sc["c1b"][:, :], op=OP.mult)
            V.tensor_tensor(xln16[:, kc, sl], sc["tmp16"][:, :], sc["c0b"][:, :], op=OP.add)
    emit_barrier(P)
    P.free("xbuf1"); P.free("xbuf0")
    free_ln_scratch(P)

    u16 = P.sb("u16", [128, T], BF16)
    vA = P.sb("vA", [128, T + 2])
    vB = P.sb("vB", [128, T])
    vC = P.sb("vC", [128, T])
    Hc = P.sb("Hc", [128, T])
    g16 = P.sb("g16", [128, T], BF16)
    ys16 = P.sb("ys16", [128, T], BF16)
    pool = MMPool(P, n=4)

    V.memset(ones128[:, :], 1.0)
    V.memset(s128[:, :], 128.0)
    G.wait("ld2", nld2)
    G.iota(iotai[:, :], [[1, 128]], base=1, channel_multiplier=0)
    G.tensor_copy(iotaf[:, :], iotai[:, :])
    S.wait("ld2", nld2)
    S.wait_now("V")
    S.activation(va[:, :], vlogit[:, :], AF.Sigmoid, bias=0.0, scale=1.0)
    S.activation(vlna[:, :], va[:, :], AF.Ln, bias=0.0, scale=1.0)
    S.activation(va128[:, :], vlna[:, :], AF.Exp, bias=0.0, scale=s128[:, 0:1])
    emit_barrier(P)

    Tn.wait("ld", 16 * DC)
    v_after_gate = 0
    for cg in range(4):
        first_evac = True
        for which, oc in (("u", cg), ("v", cg + 4)):
            for tc in range(8):
                p = pool.begin()
                for kc in range(DC):
                    Tn.matmul(p[:, :], wuv16[:, kc, oc * 128:(oc + 1) * 128],
                              xln16[:, kc, tc * 512:(tc + 1) * 512],
                              start=(kc == 0), stop=(kc == DC - 1))
                S.wait("Tq", P.now("Tq"))
                if first_evac and cg > 0:
                    S.wait("V", v_after_gate)
                first_evac = False
                if which == "u":
                    S.activation(u16[:, tc * 512:(tc + 1) * 512], p[:, :], AF.Identity,
                                 bias=vb_uv[:, oc:oc + 1], scale=1.0)
                else:
                    S.activation(vA[:, 1 + tc * 512:1 + (tc + 1) * 512], p[:, :], AF.Silu,
                                 bias=vb_uv[:, oc:oc + 1], scale=1.0)
                pool.mark("S")
        V.memset(vA[:, 0:1], 0.0)
        V.memset(vA[:, T + 1:T + 2], 0.0)
        V.wait("S", P.now("S"))
        V.tensor_scalar(vB[:, :], vA[:, 0:T], vconv_w[:, cg, 0:1], vconv_b[:, cg:cg + 1],
                        op0=OP.mult, op1=OP.add)
        V.scalar_tensor_tensor(vC[:, :], vA[:, 1:T + 1], vconv_w[:, cg, 1:2], vB[:, :],
                               op0=OP.mult, op1=OP.add)
        V.scalar_tensor_tensor(vB[:, :], vA[:, 2:T + 2], vconv_w[:, cg, 2:3], vC[:, :],
                               op0=OP.mult, op1=OP.add)
        V.tensor_scalar(vC[:, :], vB[:, :], vsb[:, cg:cg + 1], None, op0=OP.mult)
        V.tensor_scalar(Acg[:, :], ones128[:, :], va[:, cg:cg + 1], None, op0=OP.mult)
        for tch in range(32):
            V.tensor_tensor_scan(Hc[:, tch * 128:(tch + 1) * 128], Acg[:, :],
                                 vC[:, tch * 128:(tch + 1) * 128], 0.0,
                                 op0=OP.mult, op1=OP.add)
            V.fence()
        V.tensor_copy(hend[:, :], Hc[:, 127:T:128])
        V.tensor_scalar(Acg128[:, :], ones128[:, 0:32], va128[:, cg:cg + 1], None, op0=OP.mult)
        V.fence()
        V.tensor_tensor_scan(scanC[:, :], Acg128[:, :], hend[:, :], 0.0,
                             op0=OP.mult, op1=OP.add)
        V.fence()
        V.tensor_scalar(apw[:, :], iotaf[:, :], vlna[:, cg:cg + 1], None, op0=OP.mult)
        S.wait("V", P.now("V"))
        S.activation(apw[:, :], apw[:, :], AF.Exp, bias=0.0, scale=1.0)
        V.wait("S", P.now("S"))
        for tch in range(1, 32):
            V.scalar_tensor_tensor(Hc[:, tch * 128:(tch + 1) * 128], apw[:, :],
                                   scanC[:, tch - 1:tch], Hc[:, tch * 128:(tch + 1) * 128],
                                   op0=OP.mult, op1=OP.add)
        V.tensor_scalar(vA[:, 0:T], vB[:, :], vsd[:, cg:cg + 1], None, op0=OP.mult)
        V.scalar_tensor_tensor(vC[:, :], Hc[:, :], vsc_[:, cg:cg + 1], vA[:, 0:T],
                               op0=OP.mult, op1=OP.add)
        S.wait("V", P.now("V"))
        S.activation(ys16[:, :], vC[:, :], AF.Silu, bias=0.0, scale=1.0)
        V.wait("S", P.now("S"))
        V.wait("st", 16 * cg)
        V.tensor_tensor(g16[:, :], u16[:, :], ys16[:, :], op=OP.mult)
        v_after_gate = P.now("V")
        SY.wait("V", P.now("V"))
        SY.dma_start(g_out[cg * 128:(cg + 1) * 128, :], g16[:, :], inc="st", k=16)
    SY.wait("st", P.now("st"))



def build_l0a():
    P = Prog()
    A = {}
    A["xT"] = P.din("xT", [D, T])
    A["w_uv"] = P.din("w_uv", [D, 1024])
    A["b_uv"] = P.din("b_uv", [128, 8])
    A["conv_w3"] = P.din("conv_w3", [128, 4, 3])
    A["conv_b4"] = P.din("conv_b4", [128, 4])
    A["logit_a4"] = P.din("logit_a4", [128, 4])
    A["ssm_b4"] = P.din("ssm_b4", [128, 4])
    A["ssm_c4"] = P.din("ssm_c4", [128, 4])
    A["ssm_d4"] = P.din("ssm_d4", [128, 4])
    A["g_out"] = P.dout("g_out", [512, T], BF16)
    emit_l0a(P, A)
    return P

def prep_l0a_inputs(ins, xT_host):
    W = ins["L0_attn_in_proj_w"]
    bias = ins["L0_attn_in_proj_b"]
    cw = ins["L0_attn_conv_w"]

    def vecT(v, ncol):
        return np.ascontiguousarray(v.reshape(ncol, 128).T)

    maps = []
    for c in range(NCORE):
        b, j = c // 2, c % 2
        cols_u = slice(j * 512, (j + 1) * 512)
        cols_v = slice(1024 + j * 512, 1024 + (j + 1) * 512)
        w_uv = np.ascontiguousarray(np.concatenate([W[:, cols_u], W[:, cols_v]], axis=1))
        b_uv = vecT(np.concatenate([bias[cols_u], bias[cols_v]]), 8)
        ch = slice(j * 512, (j + 1) * 512)
        conv_w3 = np.ascontiguousarray(cw[ch, 0, :].reshape(4, 128, 3).transpose(1, 0, 2))
        maps.append(dict(
            xT=np.ascontiguousarray(xT_host[b]),
            w_uv=w_uv, b_uv=b_uv, conv_w3=conv_w3,
            conv_b4=vecT(ins["L0_attn_conv_b"][ch], 4),
            logit_a4=vecT(ins["L0_attn_logit_a"][ch], 4),
            ssm_b4=vecT(ins["L0_attn_ssm_b"][ch], 4),
            ssm_c4=vecT(ins["L0_attn_ssm_c"][ch], 4),
            ssm_d4=vecT(ins["L0_attn_ssm_d"][ch], 4),
        ))
    return maps



def emit_l0b1(P, A):
    xT_sl = A["xT_sl"]
    g16T = A["g16T"]
    opw = A["opw"]
    opb8 = A["opb8"]
    lnw8 = A["lnw8"]
    lnb8 = A["lnb8"]
    x1T = A["x1T"]
    x1lnT = A["x1lnT"]

    V, S, Tn, G, SY = P.V, P.S, P.T, P.G, P.SY
    NT = TL // 512

    h16 = P.sb("h16", [128, DC, TL], BF16)
    xln16 = P.sb("xln16", [128, DC, TL], BF16)
    sc = make_ln_scratch(P, TL)
    w2sb = P.sb("w2sb", [128, DC, D], BF16)
    vopb = P.sb("vopb", [128, 8])
    vlnw = P.sb("vlnw", [128, 8])
    vlnb = P.sb("vlnb", [128, 8])
    tmpfD = P.sb("tmpfD", [128, 512], F32)
    pool = MMPool(P, n=4)
    gbuf = [P.sb("gbuf0", [128, DC, 512], BF16, freeable=True),
            P.sb("gbuf1", [128, DC, 512], BF16, freeable=True)]

    for kc in range(DC):
        G.dma_start(w2sb[:, kc, :], opw[kc * 128:(kc + 1) * 128, :], inc="ld", k=16)
    for dst, src_ in [(vopb, opb8), (vlnw, lnw8), (vlnb, lnb8)]:
        SY.dma_start(dst[:, :], src_, inc="ld2", k=16)
    emit_ln_init(P, sc)

    for tc in range(NT):
        sl = slice(tc * 512, (tc + 1) * 512)
        G.dma_start(xln16[:, :, sl],
                    xT_sl[:, sl].rearrange("(c p) w -> p c w", p=128), inc="ldln", k=16)
        emit_ln_reduce_chunk(P, sc, xln16[:, :, sl], tc, 512, wait=("ldln", P.now("ldln")))
    emit_ln_stats(P, sc, TL)
    for tc in range(NT):
        sl = slice(tc * 512, (tc + 1) * 512)
        emit_ln_bcast_chunk(P, sc, tc, 512)
        for kc in range(DC):
            V.tensor_tensor(sc["tmp16"][:, :], xln16[:, kc, sl], sc["c1b"][:, :], op=OP.mult)
            V.tensor_tensor(xln16[:, kc, sl], sc["tmp16"][:, :], sc["c0b"][:, :], op=OP.add)
    emit_barrier(P, dma_sems=("ld", "ld2"))

    hw16 = sc["tmp16"]
    gdone = [0, 0]
    for tc in range(NT):
        par = tc % 2
        if gdone[par]:
            SY.wait("Tq", gdone[par])
        SY.dma_start(gbuf[par][:, :, :],
                     g16T[:, tc * 512:(tc + 1) * 512].rearrange("(c p) w -> p c w", p=128),
                     inc=f"ldg{par}", k=16)
        Tn.wait(f"ldg{par}", P.now(f"ldg{par}"))
        for oc in range(DC):
            p = pool.begin()
            for kc in range(DC):
                Tn.matmul(p[:, :], w2sb[:, kc, oc * 128:(oc + 1) * 128],
                          gbuf[par][:, kc, :],
                          start=(kc == 0), stop=(kc == DC - 1))
            S.wait("Tq", P.now("Tq"))
            S.wait("V", P.now("V"))
            S.activation(hw16[:, :], p[:, :], AF.Identity, bias=vopb[:, oc:oc + 1], scale=1.0)
            V.wait("S", P.now("S"))
            V.tensor_tensor(h16[:, oc, tc * 512:(tc + 1) * 512], hw16[:, :],
                            xln16[:, oc, tc * 512:(tc + 1) * 512], op=OP.add)
            pool.mark("V")
        gdone[par] = P.now("Tq")
    emit_barrier(P)
    P.free("gbuf1"); P.free("gbuf0")
    xbuf = [P.sb("xbuf0", [128, DC, 512], F32, freeable=True),
            P.sb("xbuf1", [128, DC, 512], F32, freeable=True)]
    x1c = [P.sb("x1c0", [128, DC, 512], F32, freeable=True),
           P.sb("x1c1", [128, DC, 512], F32, freeable=True)]

    for tc in range(NT):
        emit_ln_reduce_chunk(P, sc, h16[:, :, tc * 512:(tc + 1) * 512], tc, 512,
                             wait=("V", P.now("V")))
    emit_ln_stats(P, sc, TL)
    xd_done = [0, 0]
    st_done = [0, 0]
    for tc in range(NT):
        par = tc % 2
        if xd_done[par]:
            SY.wait("V", xd_done[par])
        SY.dma_start(xbuf[par][:, :, :],
                     xT_sl[:, tc * 512:(tc + 1) * 512].rearrange("(c p) w -> p c w", p=128),
                     inc=f"ldD{par}", k=16)
        emit_ln_bcast_chunk(P, sc, tc, 512)
        V.wait(f"ldD{par}", P.now(f"ldD{par}"))
        if st_done[par]:
            V.wait(f"stx{par}", st_done[par])
        for kc in range(DC):
            V.tensor_tensor(sc["tmp16"][:, :], h16[:, kc, tc * 512:(tc + 1) * 512],
                            sc["c1b"][:, :], op=OP.mult)
            V.tensor_tensor(sc["tmp16"][:, :], sc["tmp16"][:, :], sc["c0b"][:, :], op=OP.add)
            S.wait("V", P.now("V"))
            S.activation(tmpfD[:, :], sc["tmp16"][:, :], AF.Identity,
                         bias=vlnb[:, kc:kc + 1], scale=vlnw[:, kc:kc + 1])
            V.wait("S", P.now("S"))
            V.tensor_tensor(x1c[par][:, kc, :], tmpfD[:, :], xbuf[par][:, kc, :], op=OP.add)
        xd_done[par] = P.now("V")
        SY.wait("V", P.now("V"))
        SY.dma_start(x1T[:, tc * 512:(tc + 1) * 512].rearrange("(c p) w -> p c w", p=128),
                     x1c[par][:, :, :], inc=f"stx{par}", k=16)
        st_done[par] = P.now(f"stx{par}")
    emit_barrier(P, dma_sems=("stx0", "stx1"))
    P.free("x1c1"); P.free("x1c0"); P.free("xbuf1"); P.free("xbuf0")
    x1c16 = P.sb("x1c16", [128, DC, TL], BF16, freeable=True)

    for tc in range(NT):
        sl = slice(tc * 512, (tc + 1) * 512)
        G.dma_start(x1c16[:, :, sl],
                    x1T[:, sl].rearrange("(c p) w -> p c w", p=128), inc="ldl3", k=16)
        emit_ln_reduce_chunk(P, sc, x1c16[:, :, sl], tc, 512, wait=("ldl3", P.now("ldl3")))
    emit_ln_stats(P, sc, TL)
    for tc in range(NT):
        sl = slice(tc * 512, (tc + 1) * 512)
        emit_ln_bcast_chunk(P, sc, tc, 512)
        for kc in range(DC):
            V.tensor_tensor(sc["tmp16"][:, :], x1c16[:, kc, sl], sc["c1b"][:, :], op=OP.mult)
            V.tensor_tensor(x1c16[:, kc, sl], sc["tmp16"][:, :], sc["c0b"][:, :], op=OP.add)
        SY.wait("V", P.now("V"))
        SY.dma_start(x1lnT[:, sl].rearrange("(c p) w -> p c w", p=128),
                     x1c16[:, :, sl], inc="stl", k=16)
    SY.wait("stl", P.now("stl"))



def build_l0b1():
    P = Prog()
    A = {}
    A["xT_sl"] = P.din("xT_sl", [D, TL])
    A["g16T"] = P.din("g16T", [D, TL], BF16)
    A["opw"] = P.din("opw", [D, D])
    A["opb8"] = P.din("opb8", [128, 8])
    A["lnw8"] = P.din("lnw8", [128, 8])
    A["lnb8"] = P.din("lnb8", [128, 8])
    A["x1T"] = P.dout("x1T", [D, TL])
    A["x1lnT"] = P.dout("x1lnT", [D, TL], BF16)
    emit_l0b1(P, A)
    return P

def prep_l0b1_inputs(ins, xT_host, g_all):
    def vecT(v, ncol):
        return np.ascontiguousarray(v.reshape(ncol, 128).T)
    maps = []
    for c in range(NCORE):
        b, half = c // 2, c % 2
        t0 = half * TL
        m = dict(
            xT_sl=np.ascontiguousarray(xT_host[b][:, t0:t0 + TL]),
            opw=ins["L0_attn_out_proj_w"],
            opb8=vecT(ins["L0_attn_out_proj_b"], 8),
            lnw8=vecT(ins["L0_attn_ln_w"], 8),
            lnb8=vecT(ins["L0_attn_ln_b"], 8),
        )
        if g_all[b] is not None:
            m["g16T"] = np.ascontiguousarray(g_all[b][:, t0:t0 + TL])
        maps.append(m)
    return maps




def emit_softmax_top2(P, lgT, nch, E, iotaE, scT, mkT, dispT, mselT, stat, spacer, need_msel=True):
    V, S = P.V, P.S
    m1, sums, rr, m1s, m2s, es = (stat["m1"], stat["sums"], stat["rr"],
                                  stat["m1s"], stat["m2s"], stat["es"])

    def space(n=2):
        for _ in range(n):
            V.tensor_copy(spacer[:, :], spacer[:, :])
        V.fence()

    for c in range(nch):
        V.tensor_reduce(m1[:, c:c + 1], lgT[:, c, :], axis=AX.X, op=OP.max, negate=True)
    space()
    S.wait("V", P.now("V"))
    for c in range(nch):
        S.activation(scT[:, c, :], lgT[:, c, :], AF.Exp, bias=m1[:, c:c + 1], scale=1.0)
    V.wait("S", P.now("S"))
    for c in range(nch):
        V.tensor_reduce(sums[:, c:c + 1], scT[:, c, :], axis=AX.X, op=OP.add)
    for c in range(nch):
        V.reciprocal(rr[:, c:c + 1], sums[:, c:c + 1])
    space()
    S.wait("V", P.now("V"))
    for c in range(nch):
        S.activation(scT[:, c, :], scT[:, c, :], AF.Identity, bias=0.0, scale=rr[:, c:c + 1])
    V.wait("S", P.now("S"))
    for c in range(nch):
        V.tensor_reduce(m1s[:, c:c + 1], scT[:, c, :], axis=AX.X, op=OP.max)
    space()
    for c in range(nch):
        V.tensor_scalar(mkT[:, c, :], scT[:, c, :], m1s[:, c:c + 1], None, op0=OP.is_lt)
    for c in range(nch):
        V.tensor_tensor(mkT[:, c, :], scT[:, c, :], mkT[:, c, :], op=OP.mult)
    for c in range(nch):
        V.tensor_reduce(m2s[:, c:c + 1], mkT[:, c, :], axis=AX.X, op=OP.max)
    space()
    for c in range(nch):
        V.tensor_scalar(mkT[:, c, :], scT[:, c, :], m2s[:, c:c + 1], None, op0=OP.is_ge)
    for c in range(nch):
        V.tensor_tensor(dispT[:, c, :], scT[:, c, :], mkT[:, c, :], op=OP.mult)
    if need_msel:
        for c in range(nch):
            V.tensor_scalar(mkT[:, c, :], dispT[:, c, :], 0.0, None, op0=OP.is_gt)
        for c in range(nch):
            V.tensor_tensor(mkT[:, c, :], iotaE[:, 0:E], mkT[:, c, :], op=OP.mult)
        for c in range(nch):
            V.tensor_reduce(es[:, c:c + 1], mkT[:, c, :], axis=AX.X, op=OP.max)
        space()
        for c in range(nch):
            V.tensor_scalar(mselT[:, c, :], iotaE[:, 0:E], es[:, c:c + 1], None, op0=OP.is_equal)



def emit_l0b2(P, A):
    TLh = TL // 2
    x1T = A["x1T"]
    x1lnT = A["x1lnT"]
    gate_wT = A["gate_wT"]
    gate_b8 = A["gate_b8"]
    w1t = A["w1t"]
    b1T = A["b1T"]
    w2t = A["w2t"]
    b2T = A["b2T"]
    oh8 = A["oh8"]
    eye128 = A["eye128"]
    x2T = A["x2T"]

    V, S, Tn, G, SY = P.V, P.S, P.T, P.G, P.SY
    NT = TL // 512

    acc16 = P.sb("acc16", [128, DC, TL], BF16)
    xlnc = [P.sb("xlnc0", [128, DC, 512], BF16), P.sb("xlnc1", [128, DC, 512], BF16)]
    wdispb = P.sb("wdispb", [128, TLh], BF16)
    mselbs = P.sb("mselbs", [128, TLh])
    disp8 = P.sb("disp8", [8, TL], BF16)
    msel8 = P.sb("msel8", [8, TL], BF16)
    gw16 = P.sb("gw16", [128, DC, 8], BF16)
    eye = P.sb("eye", [128, 128])
    oh16 = P.sb("oh16", [8, 8, 128], BF16)
    vgb = P.sb("vgb", [8, 1])
    b1sb = P.sb("b1sb", [128, 32])
    b2sb = P.sb("b2sb", [128, 8])
    tmpy = P.sb("tmpy", [128, 512])
    tmpy2 = P.sb("tmpy2", [128, 512])
    iota8 = P.sb("iota8", [128, 8])
    iota8i = P.sb("iota8i", [128, 8], I32)
    sc8 = P.sb("sc8", [128, 8])
    m8 = P.sb("m8", [128, 8])
    r1 = P.sb("r1", [128, 1]); r2 = P.sb("r2", [128, 1]); r3 = P.sb("r3", [128, 1])
    pool = MMPool(P, n=4)

    G.dma_start(gw16[:, :, :], gate_wT.rearrange("(c p) e -> p c e", p=128), inc="ld", k=16)
    G.dma_start(oh16[:, :, :], oh8, inc="ld", k=16)
    for dst, src_ in [(vgb, gate_b8), (eye, eye128)]:
        SY.dma_start(dst[:, :], src_, inc="ld2", k=16)
    for kc in range(DC):
        V.memset(acc16[:, kc, :], 0.0)
    G.iota(iota8i[:, :], [[1, 8]], base=0, channel_multiplier=0)
    G.tensor_copy(iota8[:, :], iota8i[:, :])

    lgT = P.sb("lgT", [128, 16, 8], freeable=True)
    dispT = P.sb("dispT", [128, 16, 8], freeable=True)
    scT_dbg = P.sb("scT_dbg", [128, 16, 8], freeable=True)
    m1_dbg_big = P.sb("m1_dbg_big", [128, 16, 8], freeable=True)
    mselT = P.sb("mselT", [128, 16, 8], freeable=True)
    lg = P.sb("lg", [8, TL], F32, freeable=True)
    pg = P.ps("pg", [8, 512], F32, freeable=True)
    ptT = P.ps("ptT", [128, 128], F32, freeable=True)
    free_pg_later = True
    Tn.wait("ld", P.now("ld"))
    S.wait("ld2", P.now("ld2"))
    xl_done = [None, None]
    for tc in range(NT):
        par = tc % 2
        if xl_done[par] is not None:
            SY.wait(*xl_done[par])
        SY.dma_start(xlnc[par][:, :, :],
                     x1lnT[:, tc * 512:(tc + 1) * 512].rearrange("(c p) w -> p c w", p=128),
                     inc=f"lnx{par}", k=16)
        Tn.wait(f"lnx{par}", P.now(f"lnx{par}"))
        Tn.wait("S", P.now("S"))
        for kc in range(DC):
            Tn.matmul(pg[:, :], gw16[:, kc, :], xlnc[par][:, kc, :],
                      start=(kc == 0), stop=(kc == DC - 1))
        xl_done[par] = ("Tq", P.now("Tq"))
        S.wait("Tq", P.now("Tq"))
        S.activation(lg[0:8, tc * 512:(tc + 1) * 512], pg[0:8, :], AF.Identity,
                     bias=vgb[0:8, 0:1], scale=1.0)
    for half in range(2):
        Tn.wait("S", P.now("S"))
        Tn.wait("V", P.now("V"))
        for cx in range(8):
            c = half * 8 + cx
            Tn.transpose(ptT[:, cx * 8:(cx + 1) * 8], lg[0:8, c * 128:(c + 1) * 128],
                         eye[0:8, 0:8])
        V.wait("Tq", P.now("Tq"))
        for cx in range(8):
            V.tensor_copy(lgT[:, half * 8 + cx, :], ptT[:, cx * 8:(cx + 1) * 8])
    G.wait("G", P.now("G"))
    V.wait("G", P.now("G"))
    stat = {k: P.sb("gstat_" + k, [128, 16], freeable=True)
            for k in ("m1", "sums", "rr", "m1s", "m2s", "es")}
    V.memset(tmpy[:, :], 0.0)
    emit_softmax_top2(P, lgT, 16, 8, iota8, scT_dbg, m1_dbg_big, dispT, mselT, stat, tmpy)
    for k in reversed(list(stat)):
        P.free("gstat_" + k)
    emit_barrier(P)
    P.free("ptT"); P.free("pg")
    pb8 = P.ps("pb8", [8, 2048], F32, freeable=True)
    for what, dst in ((dispT, disp8), (mselT, msel8)):
        Tn.wait("V", P.now("V"))
        for c in range(16):
            Tn.transpose(pb8[0:8, c * 128:(c + 1) * 128], what[:, c, :], eye[:, :])
        V.wait("Tq", P.now("Tq"))
        V.tensor_copy(dst[0:8, :], pb8[0:8, :])
    emit_barrier(P)
    P.free("pb8")
    P.free("lg")
    P.free("mselT")
    P.free("m1_dbg_big"); P.free("scT_dbg")
    P.free("dispT"); P.free("lgT")
    ypool = MMPool(P, n=4, name="ypool")
    h4k = P.sb("h4k", [128, 32, TLh], BF16, freeable=True)
    wbuf = P.sb("wbuf", [128, 16384], BF16, freeable=True)
    xw16 = P.sb("xw16", [128, DC, TLh], BF16, freeable=True)

    w1done = [None, None]
    w2_half_use = [None, None]
    bdone = None
    for e in range(8):
        if bdone is not None:
            SY.wait("S", bdone)
        SY.dma_start(b1sb[:, :], b1T[e, :, :], inc="ldb", k=16)
        SY.dma_start(b2sb[:, :], b2T[e, :, :], inc="ldb", k=16)
        for st in range(2):
            t0 = st * TLh
            for q in range(2):
                p = pool.begin()
                Tn.wait("V", P.now("V"))
                Tn.matmul(p[:, :], oh16[:, e, :], disp8[0:8, t0 + q * 512:t0 + (q + 1) * 512],
                          start=True, stop=True)
                V.wait("Tq", P.now("Tq"))
                V.tensor_copy(wdispb[:, q * 512:(q + 1) * 512], p[:, :])
                pool.mark("V")
                p = pool.begin()
                Tn.matmul(p[:, :], oh16[:, e, :], msel8[0:8, t0 + q * 512:t0 + (q + 1) * 512],
                          start=True, stop=True)
                V.wait("Tq", P.now("Tq"))
                V.tensor_copy(mselbs[:, q * 512:(q + 1) * 512], p[:, :])
                pool.mark("V")
            V.wait("Tq", P.now("Tq"))
            for q in range(2):
                par = q % 2
                if xl_done[par] is not None:
                    SY.wait(*xl_done[par])
                SY.dma_start(xlnc[par][:, :, :],
                             x1lnT[:, t0 + q * 512:t0 + (q + 1) * 512].rearrange("(c p) w -> p c w", p=128),
                             inc=f"lnx{par}", k=16)
                V.wait(f"lnx{par}", P.now(f"lnx{par}"))
                for kc in range(DC):
                    V.tensor_tensor(xw16[:, kc, q * 512:(q + 1) * 512],
                                    xlnc[par][:, kc, :], wdispb[:, q * 512:(q + 1) * 512],
                                    op=OP.mult)
                xl_done[par] = ("V", P.now("V"))
            for hq in range(4):
                par = hq % 2
                if w2_half_use[par] is not None:
                    G.wait(*w2_half_use[par])
                if w1done[par] is not None:
                    G.wait(*w1done[par])
                w1v = wbuf[:, par * 8192:(par + 1) * 8192].rearrange("p (k w) -> p k w", k=DC)
                for kc in range(DC):
                    G.dma_start(w1v[:, kc, :],
                                w1t[e, kc * 128:(kc + 1) * 128, hq * 1024:(hq + 1) * 1024],
                                inc=f"ldw{par}", k=16)
                Tn.wait(f"ldw{par}", P.now(f"ldw{par}"))
                Tn.wait("V", P.now("V"))
                for hoc in range(8):
                    for q in range(2):
                        p = pool.begin()
                        for kc in range(DC):
                            Tn.matmul(p[:, :], w1v[:, kc, hoc * 128:(hoc + 1) * 128],
                                      xw16[:, kc, q * 512:(q + 1) * 512],
                                      start=(kc == 0), stop=(kc == DC - 1))
                        S.wait("Tq", P.now("Tq"))
                        S.wait("ldb", P.now("ldb"))
                        S.activation(h4k[:, hq * 8 + hoc, q * 512:(q + 1) * 512], p[:, :],
                                     AF.Relu, bias=b1sb[:, hq * 8 + hoc:hq * 8 + hoc + 1],
                                     scale=1.0)
                        pool.mark("S")
                w1done[par] = ("Tq", P.now("Tq"))
            S_h_done = P.now("S")
            w2v = wbuf.rearrange("p (k w) -> p k w", k=32)
            for oh_ in range(2):
                nldw2 = [0, 0]
                for kh in range(2):
                    if oh_ == 0:
                        G.wait(*w1done[kh])
                    else:
                        G.wait(*w2_half_use[kh])
                    for kc2 in range(kh * 16, kh * 16 + 16):
                        G.dma_start(w2v[:, kc2, :],
                                    w2t[e, kc2 * 128:(kc2 + 1) * 128, oh_ * 512:(oh_ + 1) * 512],
                                    inc="ldw2", k=16)
                    nldw2[kh] = P.now("ldw2")
                Tn.wait("S", S_h_done)
                for tg in range(2):
                    ps4 = [ypool.begin() for _ in range(4)]
                    for kh in range(2):
                        Tn.wait("ldw2", nldw2[kh])
                        for o4 in range(4):
                            for kc2 in range(kh * 16, kh * 16 + 16):
                                Tn.matmul(ps4[o4][:, :], w2v[:, kc2, o4 * 128:(o4 + 1) * 128],
                                          h4k[:, kc2, tg * 512:(tg + 1) * 512],
                                          start=(kc2 == 0), stop=(kc2 == 31))
                        if tg == 1:
                            w2_half_use[kh] = ("Tq", P.now("Tq"))
                    for o4 in range(4):
                        oc2 = oh_ * 4 + o4
                        S.wait("Tq", P.now("Tq"))
                        S.wait("V", P.now("V"))
                        S.activation(tmpy[:, :], ps4[o4][:, :], AF.Identity,
                                     bias=b2sb[:, oc2:oc2 + 1], scale=1.0)
                        V.wait("S", P.now("S"))
                        V.tensor_tensor(tmpy2[:, :], tmpy[:, :],
                                        mselbs[:, tg * 512:(tg + 1) * 512], op=OP.mult)
                        V.tensor_tensor(acc16[:, oc2, t0 + tg * 512:t0 + (tg + 1) * 512],
                                        tmpy2[:, :],
                                        acc16[:, oc2, t0 + tg * 512:t0 + (tg + 1) * 512],
                                        op=OP.add)
                        ypool.mark("V")
        bdone = P.now("S")
    emit_barrier(P)
    P.free("xw16"); P.free("wbuf"); P.free("h4k")
    x1b = [P.sb("x1b0", [128, DC, 512], F32, freeable=True),
           P.sb("x1b1", [128, DC, 512], F32, freeable=True)]
    x2b = [P.sb("x2b0", [128, DC, 512], F32, freeable=True),
           P.sb("x2b1", [128, DC, 512], F32, freeable=True)]
    xs_done = [None, None]
    ss_done = [None, None]
    for tc in range(NT):
        par = tc % 2
        if xs_done[par] is not None:
            SY.wait(*xs_done[par])
        SY.dma_start(x1b[par][:, :, :],
                     x1T[:, tc * 512:(tc + 1) * 512].rearrange("(c p) w -> p c w", p=128),
                     inc=f"lx{par}", k=16)
        V.wait(f"lx{par}", P.now(f"lx{par}"))
        if ss_done[par] is not None:
            V.wait(*ss_done[par])
        for kc in range(DC):
            V.tensor_tensor(x2b[par][:, kc, :], x1b[par][:, kc, :],
                            acc16[:, kc, tc * 512:(tc + 1) * 512], op=OP.add)
        xs_done[par] = ("V", P.now("V"))
        SY.wait("V", P.now("V"))
        SY.dma_start(x2T[:, tc * 512:(tc + 1) * 512].rearrange("(c p) w -> p c w", p=128),
                     x2b[par][:, :, :], inc=f"st{par}", k=16)
        ss_done[par] = (f"st{par}", P.now(f"st{par}"))
    SY.wait("st0", P.now("st0"))
    SY.wait("st1", P.now("st1"))



def build_l0b2():
    P = Prog()
    A = {}
    A["x1T"] = P.din("x1T", [D, TL])
    A["x1lnT"] = P.din("x1lnT", [D, TL], BF16)
    A["gate_wT"] = P.din("gate_wT", [D, 8])
    A["gate_b8"] = P.din("gate_b8", [8, 1])
    A["w1t"] = P.din("w1t", [8, D, 4096])
    A["b1T"] = P.din("b1T", [8, 128, 32])
    A["w2t"] = P.din("w2t", [8, 4096, D])
    A["b2T"] = P.din("b2T", [8, 128, 8])
    A["oh8"] = P.din("oh8", [8, 8, 128])
    A["eye128"] = P.din("eye128", [128, 128])
    A["x2T"] = P.dout("x2T", [D, TL])
    emit_l0b2(P, A)
    return P

def prep_l0b2_inputs(ins, x1T_all, x1lnT_all):
    maps = []
    oh8 = np.zeros((8, 8, 128), np.float32)
    for e in range(8):
        oh8[e, e, :] = 1.0
    eye128 = np.eye(128, dtype=np.float32)
    gate_wT = np.ascontiguousarray(ins["L0_ffn_gate_w"].T)
    gate_b8 = np.ascontiguousarray(ins["L0_ffn_gate_b"].reshape(8, 1))
    w1t = np.ascontiguousarray(ins["L0_ffn_w1"].transpose(0, 2, 1))
    w2t = np.ascontiguousarray(ins["L0_ffn_w2"].transpose(0, 2, 1))
    b1T = np.ascontiguousarray(ins["L0_ffn_b1"].reshape(8, 32, 128).transpose(0, 2, 1))
    b2T = np.ascontiguousarray(ins["L0_ffn_b2"].reshape(8, 8, 128).transpose(0, 2, 1))
    for c in range(NCORE):
        m = dict(
            gate_wT=gate_wT, gate_b8=gate_b8, w1t=w1t, b1T=b1T, w2t=w2t, b2T=b2T,
            oh8=oh8, eye128=eye128,
        )
        if x1T_all[c] is not None:
            m["x1T"] = x1T_all[c]; m["x1lnT"] = x1lnT_all[c]
        maps.append(m)
    return maps



def emit_l1a(P, A):
    x2T = A["x2T"]
    wqT = A["wqT"]
    wkT = A["wkT"]
    wvT = A["wvT"]
    bq8 = A["bq8"]
    bk8 = A["bk8"]
    bv8 = A["bv8"]
    wprojd = A["wprojd"]
    v16out = A["v16out"]
    zq_out = A["zq_out"]
    zk_out = A["zk_out"]

    V, S, Tn, G, SY = P.V, P.S, P.T, P.G, P.SY
    NT = TL // 512

    xln16 = P.sb("xln16", [128, DC, TL], BF16)
    v16 = P.sb("v16", [128, DC, TL], BF16)
    wq16 = P.sb("wq16", [128, DC, D], BF16)
    wk16 = P.sb("wk16", [128, DC, D], BF16)
    wv16 = P.sb("wv16", [128, DC, D], BF16)
    wp16 = P.sb("wp16", [128, 16, H], BF16)
    vbq = P.sb("vbq", [128, 8]); vbk = P.sb("vbk", [128, 8]); vbv = P.sb("vbv", [128, 8])
    sc = make_ln_scratch(P, TL)
    qc16 = [P.sb("qc16_0", [128, TL], BF16), P.sb("qc16_1", [128, TL], BF16)]
    zq_sb = P.sb("zq_sb", [16, H, 128])
    zk_sb = P.sb("zk_sb", [16, H, 128])
    pool = MMPool(P, n=3)
    pz0 = P.ps("pz0", [16, 128], F32)
    pz = [pz0, pz0]

    for name, w, dst in (("q", wqT, wq16), ("k", wkT, wk16), ("v", wvT, wv16)):
        for kc in range(DC):
            G.dma_start(dst[:, kc, :], w[kc * 128:(kc + 1) * 128, :], inc="ld", k=16)
    G.dma_start(wp16[:, :, :], wprojd.rearrange("j p h -> p j h"), inc="ld", k=16)
    for dst, src_ in ((vbq, bq8), (vbk, bk8), (vbv, bv8)):
        SY.dma_start(dst[:, :], src_, inc="ld2", k=16)

    emit_ln_init(P, sc)

    for tc in range(NT):
        sl = slice(tc * 512, (tc + 1) * 512)
        G.dma_start(xln16[:, :, sl],
                    x2T[:, sl].rearrange("(c p) w -> p c w", p=128), inc="ldln", k=16)
        emit_ln_reduce_chunk(P, sc, xln16[:, :, sl], tc, 512, wait=("ldln", P.now("ldln")))
    emit_ln_stats(P, sc, TL)
    for tc in range(NT):
        sl = slice(tc * 512, (tc + 1) * 512)
        emit_ln_bcast_chunk(P, sc, tc, 512)
        for kc in range(DC):
            V.tensor_tensor(sc["tmp16"][:, :], xln16[:, kc, sl], sc["c1b"][:, :], op=OP.mult)
            V.tensor_tensor(xln16[:, kc, sl], sc["tmp16"][:, :], sc["c0b"][:, :], op=OP.add)
    emit_barrier(P, dma_sems=("ld", "ld2"))

    zmark = 0
    qdone = [None, None]
    for which, wsb, bias, zsb in (("v", wv16, vbv, None), ("q", wq16, vbq, zq_sb), ("k", wk16, vbk, zk_sb)):
        for oc in range(DC):
            par = oc % 2
            for tc in range(NT):
                p = pool.begin()
                for kc in range(DC):
                    Tn.matmul(p[:, :], wsb[:, kc, oc * 128:(oc + 1) * 128],
                              xln16[:, kc, tc * 512:(tc + 1) * 512],
                              start=(kc == 0), stop=(kc == DC - 1))
                S.wait("Tq", P.now("Tq"))
                if zsb is None:
                    S.activation(v16[:, oc, tc * 512:(tc + 1) * 512], p[:, :], AF.Identity,
                                 bias=bias[:, oc:oc + 1], scale=1.0)
                else:
                    if qdone[par] is not None:
                        S.wait(*qdone[par])
                    S.activation(qc16[par][:, tc * 512:(tc + 1) * 512], p[:, :], AF.Identity,
                                 bias=bias[:, oc:oc + 1], scale=1.0)
                pool.mark("S")
            if zsb is not None:
                Tn.wait("S", P.now("S"))
                for gg in range(2):
                    g = 2 * oc + gg
                    pzt = pz[gg]
                    Tn.wait("V", P.now("V"))
                    for j in range(16):
                        Tn.matmul(pzt[:, :], wp16[gg * 64:gg * 64 + 64, j, :],
                                  qc16[par][gg * 64:gg * 64 + 64, j::16],
                                  start=(j == 0), stop=(j == 15))
                    V.wait("Tq", P.now("Tq"))
                    V.tensor_copy(zsb[:, g, :], pzt[0:16, :])
                zmark = P.now("Tq")
                qdone[par] = ("V", P.now("V"))
        if which == "v":
            SY.wait("S", P.now("S"))
            SY.dma_start(v16out.rearrange("(c p) w -> p c w", p=128), v16[:, :, :],
                         inc="st", k=16)
            if A.get("after_v") is not None:
                A["after_v"]()
    SY.wait("S", P.now("S"))
    SY.wait("V", P.now("V"))
    SY.dma_start(zq_out.rearrange("h (g r) -> h g r", r=128), zq_sb[0:16, :, :], inc="st", k=16)
    SY.dma_start(zk_out.rearrange("h (g r) -> h g r", r=128), zk_sb[0:16, :, :], inc="st", k=16)
    SY.wait("st", P.now("st"))



def build_l1a():
    P = Prog()
    A = {}
    A["x2T"] = P.din("x2T", [D, TL])
    A["wqT"] = P.din("wqT", [D, D])
    A["wkT"] = P.din("wkT", [D, D])
    A["wvT"] = P.din("wvT", [D, D])
    A["bq8"] = P.din("bq8", [128, 8])
    A["bk8"] = P.din("bk8", [128, 8])
    A["bv8"] = P.din("bv8", [128, 8])
    A["wprojd"] = P.din("wprojd", [16, 128, H])
    A["v16out"] = P.dout("v16out", [D, TL], BF16)
    A["zq_out"] = P.dout("zq_out", [H, TL])
    A["zk_out"] = P.dout("zk_out", [H, TL])
    emit_l1a(P, A)
    return P

def prep_l1a_inputs(ins, x2T_all):
    def vecT(v, ncol):
        return np.ascontiguousarray(v.reshape(ncol, 128).T)
    wqT = np.ascontiguousarray(ins["L1_attn_Wq"].T)
    wkT = np.ascontiguousarray(ins["L1_attn_Wk"].T)
    wvT = np.ascontiguousarray(ins["L1_attn_Wv"].T)
    wpT = ins["L1_attn_Wproj"].T.reshape(16, 64, 16)
    wprojd = np.ascontiguousarray(np.concatenate([wpT, wpT], axis=1))
    maps = []
    for c in range(NCORE):
        m = dict(
            wqT=wqT, wkT=wkT, wvT=wvT,
            bq8=vecT(ins["L1_attn_bq"], 8), bk8=vecT(ins["L1_attn_bk"], 8),
            bv8=vecT(ins["L1_attn_bv"], 8), wprojd=wprojd,
        )
        if x2T_all[c] is not None:
            m["x2T"] = x2T_all[c]
        maps.append(m)
    return maps



def emit_bump(P, A):
    HL = 8
    v_slice = A["v_slice"]
    zq_loc = A["zq_loc"]
    zk_loc = A["zk_loc"]
    log_bw8 = A["log_bw8"]
    oh8x = A["oh8x"]
    eye16 = A["eye16"]
    ctx_out0 = A["ctx_out0"]
    ctx_out1 = A["ctx_out1"]

    V, S, Tn, G, SY = P.V, P.S, P.T, P.G, P.SY

    zq = P.sb("zq", [HL, T])
    zk = P.sb("zk", [HL, T])
    lb = P.sb("lb", [1, HL])
    ohf = P.sb("ohf", [8, 8, 128])
    eye = P.sb("eye", [16, 16])
    row1 = P.sb("row1", [1, HL]); row2 = P.sb("row2", [1, HL])
    invb = P.sb("invb", [1, HL])
    invb128 = P.sb("invb128", [128, HL])
    epsb = P.sb("epsb", [1, 1])
    ones1f = P.sb("ones1f", [1, 128])
    ones16b = P.sb("ones16b", [128, 1], BF16)
    nzkT = P.sb("nzkT", [128, 32, HL])
    V2 = [P.sb("V2_0", [128, 32, 128], BF16), P.sb("V2_1", [128, 32, 128], BF16)]
    zqb16 = P.sb("zqb16", [128, T], BF16)
    A16 = [P.sb("A16_0", [128, T], BF16), P.sb("A16_1", [128, T], BF16)]
    B16 = [P.sb("B16_0", [128, T], BF16), P.sb("B16_1", [128, T], BF16)]
    vsum = P.sb("vsum", [64, 1])
    ctxsb = P.sb("ctxsb", [64, T])

    SY.dma_start(zq[:, :], zq_loc, inc="ld", k=16)
    SY.dma_start(zk[:, :], zk_loc, inc="ld", k=16)
    SY.dma_start(lb[:, :], log_bw8, inc="ld", k=16)
    SY.dma_start(ohf[:, :, :], oh8x, inc="ld", k=16)
    SY.dma_start(eye[:, :], eye16, inc="ld", k=16)
    nld = P.now("ld")

    V.memset(epsb[:, :], 1e-4)
    V.memset(ones1f[:, :], 1.0)
    V.memset(ones16b[:, :], 1.0)
    S.wait("ld", nld)
    S.wait("V", P.now("V"))
    S.activation(row1[0:1, :], lb[0:1, :], AF.Abs, bias=0.0, scale=1.0)
    S.activation(row1[0:1, :], row1[0:1, :], AF.Exp, bias=0.0, scale=-1.0)
    S.activation(row1[0:1, :], row1[0:1, :], AF.Ln, bias=1.0, scale=1.0)
    S.activation(row2[0:1, :], lb[0:1, :], AF.Relu, bias=0.0, scale=1.0)
    V.wait("S", P.now("S"))
    V.tensor_tensor(invb[0:1, :], row1[0:1, :], row2[0:1, :], op=OP.add)
    V.tensor_scalar(invb[0:1, :], invb[0:1, :], epsb[0:1, 0:1], None, op0=OP.add)
    V.reciprocal(invb[0:1, :], invb[0:1, :])
    V.fence()
    pbc0 = P.ps("pbc0", [128, 512], F32, freeable=True)
    Tn.wait("V", P.now("V"))
    Tn.matmul(pbc0[:, 0:HL], ones1f[0:1, :], invb[0:1, :], start=True, stop=True)
    V.wait("Tq", P.now("Tq"))
    V.tensor_copy(invb128[:, :], pbc0[:, 0:HL])
    V.fence()
    P.free("pbc0")

    pzt = P.ps("pzt", [128, 16], F32, freeable=True)
    Tn.wait("ld", nld)
    for jc in range(32):
        Tn.wait("V", P.now("V"))
        Tn.transpose(pzt[:, 0:HL], zk[0:HL, jc * 128:(jc + 1) * 128], eye[0:HL, 0:HL])
        V.wait("Tq", P.now("Tq"))
        V.tensor_scalar(nzkT[:, jc, :], pzt[:, 0:HL], -1.0, None, op0=OP.mult)
    emit_barrier(P)
    P.free("pzt")

    v2rd = [None, None]
    v2ld = [None, None]

    def load_v2(hp2):
        par2 = hp2 % 2
        if v2rd[par2] is not None:
            SY.wait(*v2rd[par2])
        for jc in range(32):
            SY.dma_start(V2[par2][:, jc, :],
                         v_slice[hp2 * 128:(hp2 + 1) * 128, jc * 128:(jc + 1) * 128],
                         transpose=True, inc="ldv", k=16)
        v2ld[par2] = P.now("ldv")

    load_v2(0)
    for hh in range(8):
        hp, hq = hh // 2, hh % 2
        vp = hp % 2
        if hq == 1 and hp + 1 < 4:
            load_v2(hp + 1)
        pbc = P.ps(f"pbch{hh}", [128, 2048], F32, freeable=True)
        Tn.wait("V", P.now("V"))
        Tn.wait("S", P.now("S"))
        for g in range(2):
            for qi in range(4):
                qn = g * 4 + qi
                Tn.matmul(pbc[:, qi * 512:(qi + 1) * 512], ohf[:, hh, :],
                          zq[0:HL, qn * 512:(qn + 1) * 512], start=True, stop=True)
            V.wait("Tq", P.now("Tq"))
            V.tensor_copy(zqb16[:, g * 2048:(g + 1) * 2048], pbc[:, :])
            Tn.wait("V", P.now("V"))
        P.free(f"pbch{hh}")
        pv = P.ps(f"pv{hh}", [64, 1], F32, freeable=True)
        Tn.wait("ldv", v2ld[vp])
        Tn.wait("V", P.now("V"))
        for jc in range(32):
            Tn.matmul(pv[:, :], V2[vp][:, jc, hq * 64:hq * 64 + 64], ones16b[:, :],
                      start=(jc == 0), stop=(jc == 31))
        V.wait("Tq", P.now("Tq"))
        V.tensor_copy(vsum[:, :], pv[0:64, :])
        V.fence()
        P.free(f"pv{hh}")
        pctx = P.ps(f"pctx{hh}", [64, T], F32, freeable=True)
        bdone = [None, None]
        a16_rd = [None, None]
        zqb_ok = P.now("V")
        for jc in range(32):
            par = jc % 2
            S.wait("V", a16_rd[par] if a16_rd[par] is not None else zqb_ok)
            if bdone[par] is not None:
                S.wait(*bdone[par])
            S.activation(A16[par][:, :], zqb16[:, :], AF.Abs,
                         bias=nzkT[:, jc, hh:hh + 1], scale=1.0)
            V.wait("S", P.now("S"))
            if bdone[par] is not None:
                V.wait(*bdone[par])
            V.tensor_scalar(B16[par][:, :], A16[par][:, :], invb128[:, hh:hh + 1], 1.0,
                            op0=OP.mult, op1=OP.min)
            a16_rd[par] = P.now("V")
            Tn.wait("V", P.now("V"))
            for qn in range(8):
                Tn.matmul(pctx[:, qn * 512:(qn + 1) * 512], V2[vp][:, jc, hq * 64:hq * 64 + 64],
                          B16[par][:, qn * 512:(qn + 1) * 512],
                          start=(jc == 0), stop=(jc == 31))
            bdone[par] = ("Tq", P.now("Tq"))
        V.wait("Tq", P.now("Tq"))
        for qn in range(8):
            V.tensor_scalar(ctxsb[:, qn * 512:(qn + 1) * 512], pctx[0:64, qn * 512:(qn + 1) * 512],
                            vsum[0:64, 0:1], -1.0 / T, op0=OP.subtract, op1=OP.mult)
        P.free(f"pctx{hh}")
        SY.wait("V", P.now("V"))
        if hh < 6:
            SY.dma_start(ctx_out0[hh * 64:(hh + 1) * 64, :], ctxsb[0:64, :], inc="st", k=16)
        else:
            SY.dma_start(ctx_out1[(hh - 6) * 64:(hh - 5) * 64, :], ctxsb[0:64, :], inc="st", k=16)
        if hh == 5 and A.get("ctx_midcb") is not None:
            A["ctx_midcb"]()
        if hh == 7 and A.get("ctx_endcb") is not None:
            A["ctx_endcb"]()
        if hq == 1:
            v2rd[vp] = ("Tq", P.now("Tq"))
    SY.wait("st", P.now("st"))



def build_bump():
    P = Prog()
    A = {}
    A["v_slice"] = P.din("v_slice", [512, T], BF16)
    A["zq_loc"] = P.din("zq_loc", [HL, T])
    A["zk_loc"] = P.din("zk_loc", [HL, T])
    A["log_bw8"] = P.din("log_bw8", [1, HL])
    A["oh8x"] = P.din("oh8x", [8, 8, 128])
    A["eye16"] = P.din("eye16", [16, 16])
    ctx_out = P.dout("ctx_out", [512, T])
    A["ctx_out0"] = ctx_out[0:384, :]
    A["ctx_out1"] = ctx_out[384:512, :]
    emit_bump(P, A)
    return P

def prep_bump_inputs(ins, v_all, zq_all, zk_all):
    oh8x = np.zeros((8, 8, 128), np.float32)
    for k in range(8):
        oh8x[k, k, :] = 1.0
    eye16 = np.eye(16, dtype=np.float32)
    maps = []
    for c in range(NCORE):
        b, half = c // 2, c % 2
        hs = slice(half * 8, half * 8 + 8)
        maps.append(dict(
            v_slice=np.ascontiguousarray(v_all[b][half * 512:(half + 1) * 512, :]),
            zq_loc=np.ascontiguousarray(zq_all[b][hs, :]),
            zk_loc=np.ascontiguousarray(zk_all[b][hs, :]),
            log_bw8=np.ascontiguousarray(ins["L1_attn_log_bandwidth"][hs].reshape(1, 8)),
            oh8x=oh8x, eye16=eye16,
        ))
    return maps



def emit_l1b(P, A):
    x2T = A["x2T"]
    ctxT = A["ctxT"]
    centT = A["centT"]
    eb16 = A["eb16"]
    wrt = A["wrt"]
    br16m = A["br16m"]
    wst = A["wst"]
    bsT2 = A["bsT2"]
    oh16x = A["oh16x"]
    eye128 = A["eye128"]
    outT = A["outT"]
    x3T = A["x3T_i"]

    V, S, Tn, G, SY = P.V, P.S, P.T, P.G, P.SY
    NT = TL // 512

    x3ln16 = P.sb("x3ln16", [128, DC, TL], BF16)
    wsum16 = P.sb("wsum16", [128, DC, D], BF16)
    cent16 = P.sb("cent16", [128, DC, H], BF16)
    veb = P.sb("veb", [H, 1])
    bs16 = P.sb("bs16", [2, D], BF16)
    halfones = P.sb("halfones", [2, 512], BF16)
    br16 = P.sb("br16", [H, D], BF16)
    eye = P.sb("eye", [128, 128])
    oh16 = P.sb("oh16", [16, 16, 128], BF16)
    g16b = P.sb("g16b", [H, TL], BF16)
    tmpf2 = P.sb("tmpf2", [128, 512])
    spacer = P.sb("spacer", [128, 512])
    x3c16 = P.sb("x3c16", [128, DC, TL], BF16, freeable=True)
    sc = make_ln_scratch(P, TL, freeable=True)

    G.dma_start(cent16[:, :, :], centT.rearrange("(c p) h -> p c h", p=128), inc="ld", k=16)
    G.dma_start(oh16[:, :, :], oh16x, inc="ld", k=16)
    G.dma_start(br16[:, :], br16m, inc="ld", k=16)
    G.dma_start(bs16[:, :], bsT2, inc="ld", k=16)
    SY.dma_start(veb[:, :], eb16, inc="ld2", k=16)
    SY.dma_start(eye[:, :], eye128, inc="ld2", k=16)

    emit_ln_init(P, sc)
    ab = P.sb("ab0", [128, DC, 512], F32, freeable=True)
    cb = P.sb("cb0", [128, DC, 512], F32, freeable=True)
    x3f = P.sb("x3f0", [128, DC, 512], F32, freeable=True)
    ab_rd = None
    st3 = None
    for tc in range(NT):
        sl = slice(tc * 512, (tc + 1) * 512)
        if ab_rd is not None:
            SY.wait(*ab_rd)
            G.wait(*ab_rd)
        SY.dma_start(ab[:, :, :], x2T[:, sl].rearrange("(c p) w -> p c w", p=128),
                     inc="lda", k=16)
        G.dma_start(cb[:, :, :], ctxT[:, sl].rearrange("(c p) w -> p c w", p=128),
                    inc="ldc", k=16)
        V.wait("lda", P.now("lda"))
        V.wait("ldc", P.now("ldc"))
        if st3 is not None:
            V.wait(*st3)
        for kc in range(DC):
            V.tensor_tensor(x3f[:, kc, :], ab[:, kc, :], cb[:, kc, :], op=OP.add)
        ab_rd = ("V", P.now("V"))
        for kc in range(DC):
            V.tensor_copy(x3c16[:, kc, sl], x3f[:, kc, :])
        SY.wait("V", P.now("V"))
        SY.dma_start(x3T[:, sl].rearrange("(c p) w -> p c w", p=128), x3f[:, :, :],
                     inc="st3", k=16)
        st3 = ("st3", P.now("st3"))
        emit_ln_reduce_chunk(P, sc, x3c16[:, :, sl], tc, 512, wait=("V", P.now("V")))
    emit_ln_stats(P, sc, TL)
    for tc in range(NT):
        sl = slice(tc * 512, (tc + 1) * 512)
        emit_ln_bcast_chunk(P, sc, tc, 512)
        emit_ln_apply_chunk(P, sc, x3c16[:, :, sl], x3ln16[:, :, sl], 512)
    emit_barrier(P, dma_sems=("ld", "ld2", "st3"))
    P.free("x3f0"); P.free("cb0"); P.free("ab0")
    free_ln_scratch(P)
    P.free("x3c16")

    lgT = P.sb("lgT", [128, 16, H], freeable=True)
    scT = P.sb("scT", [128, 16, H], freeable=True)
    mkT = P.sb("mkT", [128, 16, H], freeable=True)
    dispT = P.sb("dispT", [128, 16, H], freeable=True)
    lgs = P.sb("lgs", [H, TL], F32, freeable=True)
    stat = {k: P.sb("gstat_" + k, [128, 16], freeable=True)
            for k in ("m1", "sums", "rr", "m1s", "m2s", "es")}
    pg = P.ps("pg", [16, 512], F32, freeable=True)
    ptT = P.ps("ptT", [128, 256], F32, freeable=True)
    V.memset(spacer[:, :], 0.0)
    Tn.wait("ld", P.now("ld"))
    S.wait("ld2", P.now("ld2"))
    for tc in range(NT):
        Tn.wait("S", P.now("S"))
        for kc in range(DC):
            Tn.matmul(pg[:, :], cent16[:, kc, :], x3ln16[:, kc, tc * 512:(tc + 1) * 512],
                      start=(kc == 0), stop=(kc == DC - 1))
        S.wait("Tq", P.now("Tq"))
        S.activation(lgs[0:16, tc * 512:(tc + 1) * 512], pg[0:16, :], AF.Identity,
                     bias=veb[0:16, 0:1], scale=1.0)
    for half in range(2):
        Tn.wait("S", P.now("S"))
        Tn.wait("V", P.now("V"))
        for cx in range(8):
            c = half * 8 + cx
            Tn.transpose(ptT[:, cx * 16:(cx + 1) * 16], lgs[0:16, c * 128:(c + 1) * 128],
                         eye[0:16, 0:16])
        V.wait("Tq", P.now("Tq"))
        for cx in range(8):
            V.tensor_copy(lgT[:, half * 8 + cx, :], ptT[:, cx * 16:(cx + 1) * 16])
    emit_softmax_top2(P, lgT, 16, H, None, scT, mkT, dispT, None, stat, spacer, need_msel=False)
    pb16 = P.ps("pb16", [16, 2048], F32, freeable=True)
    Tn.wait("V", P.now("V"))
    for c in range(16):
        Tn.transpose(pb16[0:16, c * 128:(c + 1) * 128], dispT[:, c, :], eye[:, :])
    V.wait("Tq", P.now("Tq"))
    V.tensor_copy(g16b[0:16, :], pb16[0:16, :])
    emit_barrier(P)
    P.free("pb16"); P.free("ptT"); P.free("pg")
    for k in reversed(list(stat)):
        P.free("gstat_" + k)
    P.free("lgs"); P.free("dispT"); P.free("mkT"); P.free("scT"); P.free("lgT")

    wsb = [P.sb("wsb0", [128, D], F32, freeable=True),
           P.sb("wsb1", [128, D], F32, freeable=True)]
    V.memset(halfones[:, :], 0.5)
    for kc in range(DC):
        SY.wait("V", P.now("V"))
        SY.dma_start(wsb[0][:, :], wst[0, kc * 128:(kc + 1) * 128, :], inc="ldS", k=16)
        SY.dma_start(wsb[1][:, :], wst[1, kc * 128:(kc + 1) * 128, :], inc="ldS", k=16)
        V.wait("ldS", P.now("ldS"))
        V.tensor_tensor(wsb[0][:, :], wsb[0][:, :], wsb[1][:, :], op=OP.add)
        V.tensor_scalar(wsum16[:, kc, :], wsb[0][:, :], 0.5, None, op0=OP.mult)
    emit_barrier(P)
    P.free("wsb1"); P.free("wsb0")

    acc16 = P.sb("acc16", [128, DC, TL], BF16)
    gbe = [P.sb("gbe0", [128, TL]), P.sb("gbe1", [128, TL])]
    Gt = [P.sb("Gt0", [128, DC, 512], BF16), P.sb("Gt1", [128, DC, 512], BF16)]
    wrb = [P.sb("wrb0", [128, DC, D], BF16), P.sb("wrb1", [128, DC, D], BF16)]
    outb = P.sb("outb", [128, DC, 512], F32)
    x3b = P.sb("x3b", [128, DC, 512], F32)
    pool6 = MMPool(P, n=6, name="pacc")
    for kc in range(DC):
        V.memset(acc16[:, kc, :], 0.0)

    NE = 16
    wr_done = [None, None]
    n_ldw = [0, 0]
    gt_rd = [None, None]
    gt_wr = [None, None]

    def load_wr(e2):
        par2 = e2 % 2
        if wr_done[par2] is not None:
            G.wait(*wr_done[par2])
        for kc in range(DC):
            G.dma_start(wrb[par2][:, kc, :], wrt[e2, kc * 128:(kc + 1) * 128, :],
                        inc=f"ldw{par2}", k=16)
        n_ldw[par2] = P.now(f"ldw{par2}")

    def bcast_g(e2):
        par2 = e2 % 2
        for tcq in range(NT):
            p = pool6.begin()
            Tn.matmul(p[:, :], oh16[:, e2, :], g16b[0:16, tcq * 512:(tcq + 1) * 512],
                      start=True, stop=True)
            V.wait("Tq", P.now("Tq"))
            V.tensor_copy(gbe[par2][:, tcq * 512:(tcq + 1) * 512], p[:, :])
            pool6.mark("V")

    def emit_gt(idx):
        e2, tc2 = divmod(idx, NT)
        pt = idx % 2
        if gt_rd[pt] is not None:
            V.wait(*gt_rd[pt])
        sl2 = slice(tc2 * 512, (tc2 + 1) * 512)
        for kc in range(DC):
            V.tensor_tensor(Gt[pt][:, kc, :], x3ln16[:, kc, sl2],
                            gbe[e2 % 2][:, sl2], op=OP.mult)
        gt_wr[pt] = P.now("V")

    load_wr(0)
    bcast_g(0)
    emit_gt(0)
    for e in range(NE):
        par = e % 2
        if e + 1 < NE:
            bcast_g(e + 1)
            load_wr(e + 1)
        Tn.wait(f"ldw{par}", n_ldw[par])
        for tc in range(NT):
            idx = e * NT + tc
            pt = idx % 2
            if idx + 1 < NE * NT:
                emit_gt(idx + 1)
            sl = slice(tc * 512, (tc + 1) * 512)
            Tn.wait("V", gt_wr[pt])
            for oc in range(DC):
                p = pool6.begin()
                for kc in range(DC):
                    Tn.matmul(p[:, :], wrb[par][:, kc, oc * 128:(oc + 1) * 128],
                              Gt[pt][:, kc, :], start=(kc == 0), stop=(kc == DC - 1))
                V.wait("Tq", P.now("Tq"))
                V.tensor_tensor(acc16[:, oc, sl], acc16[:, oc, sl], p[:, :], op=OP.add)
                pool6.mark("V")
            gt_rd[pt] = ("Tq", P.now("Tq"))
        wr_done[par] = ("Tq", P.now("Tq"))

    x3_rd = None
    st_done = None
    for tc in range(NT):
        sl = slice(tc * 512, (tc + 1) * 512)
        if x3_rd is not None:
            SY.wait(*x3_rd)
            G.wait(*x3_rd)
        for kc in range(DC):
            q = SY if kc % 2 == 0 else G
            q.dma_start(x3b[:, kc, :], x3T[kc * 128:(kc + 1) * 128, sl], inc="ldx3", k=16)
        nldx3 = P.now("ldx3")
        for oc in range(DC):
            p = pool6.begin()
            Tn.matmul(p[:, :], br16[0:16, oc * 128:(oc + 1) * 128], g16b[0:16, sl],
                      start=True, stop=False)
            Tn.matmul(p[:, :], bs16[0:2, oc * 128:(oc + 1) * 128],
                      halfones[0:2, :], start=False, stop=False)
            for kc in range(DC):
                Tn.matmul(p[:, :], wsum16[:, kc, oc * 128:(oc + 1) * 128],
                          x3ln16[:, kc, sl], start=False, stop=(kc == DC - 1))
            V.wait("Tq", P.now("Tq"))
            V.wait("ldx3", nldx3)
            if st_done is not None and oc == 0:
                V.wait(*st_done)
            V.tensor_tensor(tmpf2[:, :], p[:, :], acc16[:, oc, sl], op=OP.add)
            V.tensor_tensor(tmpf2[:, :], tmpf2[:, :], x3ln16[:, oc, sl], op=OP.add)
            V.tensor_tensor(outb[:, oc, :], tmpf2[:, :], x3b[:, oc, :], op=OP.add)
            pool6.mark("V")
        x3_rd = ("V", P.now("V"))
        SY.wait("V", P.now("V"))
        SY.dma_start(outT[:, sl].rearrange("(c p) w -> p c w", p=128),
                     outb[:, :, :], inc="st", k=16)
        st_done = ("st", P.now("st"))
    SY.wait("st", P.now("st"))



def build_l1b():
    P = Prog()
    A = {}
    A["x2T"] = P.din("x2T", [D, TL])
    A["ctxT"] = P.din("ctxT", [D, TL])
    A["centT"] = P.din("centT", [D, H])
    A["eb16"] = P.din("eb16", [H, 1])
    A["wrt"] = P.din("wrt", [H, D, D])
    A["br16m"] = P.din("br16m", [H, D])
    A["wst"] = P.din("wst", [2, D, D])
    A["bsT2"] = P.din("bsT2", [2, D])
    A["oh16x"] = P.din("oh16x", [16, 16, 128])
    A["eye128"] = P.din("eye128", [128, 128])
    A["outT"] = P.dout("outT", [D, TL])
    A["x3T_i"] = P.nc.dram_tensor("x3T_i", [D, TL], F32).ap()
    emit_l1b(P, A)
    return P

def prep_l1b_inputs(ins, x2T_all, ctxT_all):
    oh16x = np.zeros((16, 16, 128), np.float32)
    for k in range(16):
        oh16x[k, k, :] = 1.0
    eye128 = np.eye(128, dtype=np.float32)
    centT = np.ascontiguousarray(ins["L1_ffn_centroids"].T)
    eb16 = np.ascontiguousarray(ins["L1_ffn_expert_biases"].reshape(16, 1))
    wrt = np.ascontiguousarray(ins["L1_ffn_Wr"].transpose(0, 2, 1))
    br16m = ins["L1_ffn_br"]
    wst = np.ascontiguousarray(ins["L1_ffn_Ws"].transpose(0, 2, 1))
    bsT2 = np.ascontiguousarray(ins["L1_ffn_bs"])
    maps = []
    for c in range(NCORE):
        m = dict(
            centT=centT, eb16=eb16,
            wrt=wrt, br16m=br16m, wst=wst, bsT2=bsT2, oh16x=oh16x, eye128=eye128,
        )
        if x2T_all[c] is not None:
            m["x2T"] = x2T_all[c]; m["ctxT"] = ctxT_all[c]
        maps.append(m)
    return maps



PAIRS = [[0, 1], [2, 3], [4, 5], [6, 7]]


def build_fused():
    P = Prog()
    nc = P.nc
    A = {}
    A["xT"] = P.din("xT", [D, T])
    A["w_uv"] = P.din("w_uv", [D, 1024])
    A["b_uv"] = P.din("b_uv", [128, 8])
    A["conv_w3"] = P.din("conv_w3", [128, 4, 3])
    A["conv_b4"] = P.din("conv_b4", [128, 4])
    A["logit_a4"] = P.din("logit_a4", [128, 4])
    A["ssm_b4"] = P.din("ssm_b4", [128, 4])
    A["ssm_c4"] = P.din("ssm_c4", [128, 4])
    A["ssm_d4"] = P.din("ssm_d4", [128, 4])
    A["xT_sl"] = P.din("xT_sl", [D, TL])
    A["opw"] = P.din("opw", [D, D])
    A["opb8"] = P.din("opb8", [128, 8])
    A["lnw8"] = P.din("lnw8", [128, 8])
    A["lnb8"] = P.din("lnb8", [128, 8])
    A["gate_wT"] = P.din("gate_wT", [D, 8])
    A["gate_b8"] = P.din("gate_b8", [8, 1])
    A["w1t"] = P.din("w1t", [8, D, 4096])
    A["b1T"] = P.din("b1T", [8, 128, 32])
    A["w2t"] = P.din("w2t", [8, 4096, D])
    A["b2T"] = P.din("b2T", [8, 128, 8])
    A["oh8"] = P.din("oh8", [8, 8, 128])
    A["eye128"] = P.din("eye128", [128, 128])
    A["wqT"] = P.din("wqT", [D, D])
    A["wkT"] = P.din("wkT", [D, D])
    A["wvT"] = P.din("wvT", [D, D])
    A["bq8"] = P.din("bq8", [128, 8])
    A["bk8"] = P.din("bk8", [128, 8])
    A["bv8"] = P.din("bv8", [128, 8])
    A["wprojd"] = P.din("wprojd", [16, 128, H])
    A["log_bw8"] = P.din("log_bw8", [1, 8])
    A["oh8x"] = P.din("oh8x", [8, 8, 128])
    A["eye16"] = P.din("eye16", [16, 16])
    A["centT"] = P.din("centT", [D, H])
    A["eb16"] = P.din("eb16", [H, 1])
    A["wrt"] = P.din("wrt", [H, D, D])
    A["br16m"] = P.din("br16m", [H, D])
    A["wst"] = P.din("wst", [2, D, D])
    A["bsT2"] = P.din("bsT2", [2, D])
    A["oh16x"] = P.din("oh16x", [16, 16, 128])
    meta = P.din("meta", [1, 12], I32)
    A["outT"] = P.dout("outT", [D, TL])

    g_mine = nc.dram_tensor("g_mine", [512, T], BF16).ap()
    g_pair = nc.dram_tensor("g_pair", [1024, T], BF16).ap()
    g_sl = nc.dram_tensor("g_sl", [D, TL], BF16).ap()
    x1T_i = nc.dram_tensor("x1T_i", [D, TL], F32).ap()
    x1lnT_i = nc.dram_tensor("x1lnT_i", [D, TL], BF16).ap()
    x2T_i = nc.dram_tensor("x2T_i", [D, TL], F32).ap()
    v_mine = nc.dram_tensor("v_mine", [D, TL], BF16).ap()
    v_pair = nc.dram_tensor("v_pair", [2 * D, TL], BF16).ap()
    v_sl = nc.dram_tensor("v_sl", [512, T], BF16).ap()
    z_mine = nc.dram_tensor("z_mine", [2 * H, TL], F32).ap()
    z_pair = nc.dram_tensor("z_pair", [4 * H, TL], F32).ap()
    zq_i = nc.dram_tensor("zq_i", [8, T], F32).ap()
    zk_i = nc.dram_tensor("zk_i", [8, T], F32).ap()
    ctx_mineA = nc.dram_tensor("ctx_mineA", [384, T], F32).ap()
    ctx_mineB = nc.dram_tensor("ctx_mineB", [128, T], F32).ap()
    ctx_pairA = nc.dram_tensor("ctx_pairA", [768, T], F32).ap()
    ctx_pairB = nc.dram_tensor("ctx_pairB", [256, T], F32).ap()
    ctx_sl = nc.dram_tensor("ctx_sl", [D, TL], F32).ap()
    x3T_i = nc.dram_tensor("x3T_i", [D, TL], F32).ap()

    A["g_out"] = g_mine
    A["g16T"] = g_sl
    A["x1T"] = x1T_i
    A["x1lnT"] = x1lnT_i
    A["x2T"] = x2T_i
    A["v16out"] = v_mine
    A["zq_out"] = z_mine[0:H, :]
    A["zk_out"] = z_mine[H:2 * H, :]
    A["v_slice"] = v_sl
    A["zq_loc"] = zq_i
    A["zk_loc"] = zk_i
    A["ctx_out0"] = ctx_mineA
    A["ctx_out1"] = ctx_mineB
    A["ctxT"] = ctx_sl
    A["x3T_i"] = x3T_i

    G, SY = P.G, P.SY
    smeta = P.sb("smeta_g", [1, 12], I32)
    SY.dma_start(smeta[:, :], meta, inc="ldm", k=16)

    rg = P.es.enter_context(nc.gpsimd.register("rgoff"))
    rg2 = P.es.enter_context(nc.gpsimd.register("rgoff2"))

    def ag(kind_in, kind_out):
        G.collective_compute("AllGather", mybir.AluOpType.bypass,
                             replica_groups=PAIRS, ins=[kind_in.opt()], outs=[kind_out.opt()])
        G.fence()

    with P.phase("_a"):
        emit_l0a(P, A)
    ag(g_mine, g_pair)
    G.wait("ldm", P.now("ldm"))
    G.reg_load(rg, smeta[0:1, 0:1])
    G.dma_start(bass.AP(g_sl.tensor, 0, [[TL, D], [1, TL]]),
                bass.AP(g_pair.tensor, rg, [[T, D], [1, TL]]), inc="xch", k=16)
    emit_barrier(P, all_sems=True)

    with P.phase("_b1"):
        emit_l0b1(P, A)

    with P.phase("_b2"):
        emit_l0b2(P, A)

    def after_v():
        G.wait("st", P.now("st"))
        ag(v_mine, v_pair)
        G.wait("ldm", P.now("ldm"))
        G.reg_load(rg, smeta[0:1, 1:2])
        G.dma_start(bass.AP(v_sl.tensor, 0, [[T, 512], [1, TL]]),
                    bass.AP(v_pair.tensor, rg, [[TL, 512], [1, TL]]), inc="xch", k=16)
        G.reg_load(rg2, smeta[0:1, 2:3])
        G.dma_start(bass.AP(v_sl.tensor, TL, [[T, 512], [1, TL]]),
                    bass.AP(v_pair.tensor, rg2, [[TL, 512], [1, TL]]), inc="xch", k=16)
    A["after_v"] = after_v
    with P.phase("_c"):
        emit_l1a(P, A)
    ag(z_mine, z_pair)
    G.reg_load(rg, smeta[0:1, 3:4])
    G.dma_start(bass.AP(zq_i.tensor, 0, [[T, 8], [256, 16], [1, 128]]),
                bass.AP(z_pair.tensor, rg, [[TL, 8], [128, 16], [1, 128]]), inc="xch", k=16)
    G.reg_load(rg2, smeta[0:1, 4:5])
    G.dma_start(bass.AP(zq_i.tensor, 128, [[T, 8], [256, 16], [1, 128]]),
                bass.AP(z_pair.tensor, rg2, [[TL, 8], [128, 16], [1, 128]]), inc="xch", k=16)
    G.reg_load(rg, smeta[0:1, 5:6])
    G.dma_start(bass.AP(zk_i.tensor, 0, [[T, 8], [256, 16], [1, 128]]),
                bass.AP(z_pair.tensor, rg, [[TL, 8], [128, 16], [1, 128]]), inc="xch", k=16)
    G.reg_load(rg2, smeta[0:1, 6:7])
    G.dma_start(bass.AP(zk_i.tensor, 128, [[T, 8], [256, 16], [1, 128]]),
                bass.AP(z_pair.tensor, rg2, [[TL, 8], [128, 16], [1, 128]]), inc="xch", k=16)
    emit_barrier(P, all_sems=True)

    def ctx_midcb():
        G.wait("st", P.now("st"))
        ag(ctx_mineA, ctx_pairA)
        G.wait("ldm", P.now("ldm"))
        G.reg_load(rg, smeta[0:1, 0:1])
        G.dma_start(bass.AP(ctx_sl.tensor, 0, [[TL, 384], [1, TL]]),
                    bass.AP(ctx_pairA.tensor, rg, [[T, 384], [1, TL]]), inc="xch", k=16)
        G.reg_load(rg2, smeta[0:1, 7:8])
        G.dma_start(bass.AP(ctx_sl.tensor, 512 * TL, [[TL, 384], [1, TL]]),
                    bass.AP(ctx_pairA.tensor, rg2, [[T, 384], [1, TL]]), inc="xch", k=16)

    def ctx_endcb():
        G.wait("st", P.now("st"))
        ag(ctx_mineB, ctx_pairB)
        G.reg_load(rg, smeta[0:1, 0:1])
        G.dma_start(bass.AP(ctx_sl.tensor, 384 * TL, [[TL, 128], [1, TL]]),
                    bass.AP(ctx_pairB.tensor, rg, [[T, 128], [1, TL]]), inc="xch", k=16)
        G.reg_load(rg2, smeta[0:1, 8:9])
        G.dma_start(bass.AP(ctx_sl.tensor, 896 * TL, [[TL, 128], [1, TL]]),
                    bass.AP(ctx_pairB.tensor, rg2, [[T, 128], [1, TL]]), inc="xch", k=16)
    A["ctx_midcb"] = ctx_midcb
    A["ctx_endcb"] = ctx_endcb
    with P.phase("_d"):
        emit_bump(P, A)

    with P.phase("_e"):
        emit_l1b(P, A)
    return P


def prep_fused_inputs(ins, xT_host):
    m_l0a = prep_l0a_inputs(ins, xT_host)
    m_l0b1 = prep_l0b1_inputs(ins, xT_host, [None] * B)
    m_l0b2 = prep_l0b2_inputs(ins, [None] * NCORE, [None] * NCORE)
    m_l1a = prep_l1a_inputs(ins, [None] * NCORE)
    m_bump = prep_bump_inputs_fused(ins)
    m_l1b = prep_l1b_inputs(ins, [None] * NCORE, [None] * NCORE)
    maps = []
    for c in range(NCORE):
        b, half = c // 2, c % 2
        m = {}
        for src in (m_l0a[c], m_l0b1[c], m_l0b2[c], m_l1a[c], m_bump[c], m_l1b[c]):
            m.update(src)
        for k in ("g16T", "x1T", "x1lnT", "x2T", "v_slice", "zq_loc", "zk_loc", "ctxT"):
            m.pop(k, None)
        m["meta"] = np.array([[half * TL,
                               half * 512 * TL,
                               D * TL + half * 512 * TL,
                               half * 8 * TL,
                               (2 * H + half * 8) * TL,
                               (H + half * 8) * TL,
                               (3 * H + half * 8) * TL,
                               384 * T + half * TL,
                               128 * T + half * TL,
                               0, 0, 0]], np.int32)
        maps.append(m)
    return maps


def prep_bump_inputs_fused(ins):
    oh8x = np.zeros((8, 8, 128), np.float32)
    for k in range(8):
        oh8x[k, k, :] = 1.0
    eye16 = np.eye(16, dtype=np.float32)
    maps = []
    for c in range(NCORE):
        b, half = c // 2, c % 2
        hs = slice(half * 8, half * 8 + 8)
        maps.append(dict(
            log_bw8=np.ascontiguousarray(ins["L1_attn_log_bandwidth"][hs].reshape(1, 8)),
            oh8x=oh8x, eye16=eye16,
        ))
    return maps



def kernel(**inputs):
    ins = {k: np.asarray(v) for k, v in inputs.items()}
    x = np.asarray(ins["L0_attn_x"], np.float32)
    xT_host = np.ascontiguousarray(x.transpose(0, 2, 1))
    P = build_fused()
    nc = P.finalize()
    maps = prep_fused_inputs(ins, xT_host)
    res = run_bass_kernel_spmd(nc, maps, core_ids=list(range(NCORE))).results
    out = np.empty((B, T, D), np.float32)
    for c in range(NCORE):
        b, half = c // 2, c % 2
        t0 = half * TL
        out[b, t0:t0 + TL, :] = np.asarray(res[c]["outT"]).T
    return out
